# version 10
# baseline (speedup 1.0000x reference)
import numpy as np
import ml_dtypes
import concourse.bass as bass
import concourse.mybir as mybir
from concourse.bass_utils import run_bass_kernel_spmd

F32 = mybir.dt.float32
BF16 = mybir.dt.bfloat16
AF = mybir.ActivationFunctionType
OP = mybir.AluOpType
AX = mybir.AxisListType.X

ENGINES = ("sync", "tensor", "vector", "scalar", "gpsimd")
T_SEQ = 2048
NCC = 8
NB = 2
TT0 = 512
CH = 128
NCH0 = TT0 // CH
NT0 = T_SEQ // TT0
LN_EPS = 1e-5
GN_EPS = 6.4e-4


class Sem:
    __slots__ = ("name", "val", "h", "owner")

    def __init__(self, name, owner=None):
        self.name, self.val, self.h, self.owner = name, 0, None, owner


class Prog:
    def __init__(self, nc):
        self.nc = nc
        self.stream = {e: [] for e in ENGINES}
        self.sems = []
        self.last_wait = {e: {} for e in ENGINES}

    def sem(self, name, owner=None):
        s = Sem(name, owner)
        self.sems.append(s)
        return s

    def on(self, eng, fn):
        self.stream[eng].append(fn)

    def wait(self, eng, *tokens, hard=False):
        need = {}
        for t in tokens:
            if not t:
                continue
            sem, val = t
            if sem.owner == eng and not hard:
                continue
            if not globals().get("NO_DEDUP") and val <= self.last_wait[eng].get(sem, 0):
                continue
            need[sem] = max(need.get(sem, 0), val)
        for sem, val in need.items():
            self.last_wait[eng][sem] = val
            self.on(eng, lambda e, s=sem, v=val: e.wait_ge(s.h, v))

    def finalize(self):
        nc = self.nc
        for s in self.sems:
            s.h = nc.alloc_semaphore(s.name)
        with nc.Block() as block:
            for e in ENGINES:
                ops = self.stream[e]

                def runner(engine, ops=ops):
                    for fn in ops:
                        fn(engine)

                getattr(block, e)(runner)


class Buf:
    __slots__ = ("ap", "ready", "readers", "aliases")

    def __init__(self, ap=None):
        self.ap = ap
        self.ready = []
        self.readers = []
        self.aliases = []

    def war(self):
        t = tuple(self.readers) + tuple(self.ready)
        for a in self.aliases:
            t += tuple(a.readers) + tuple(a.ready)
        return t

    def wrote(self, tok):
        self.ready = [tok] if tok else []
        self.readers = []

    def add_write(self, tok):
        if tok:
            self.ready.append(tok)

    def raw(self):
        return tuple(self.ready)

    def read_by(self, tok):
        if tok:
            self.readers.append(tok)


def alias(*bufs):
    for b in bufs:
        b.aliases = [x for x in bufs if x is not b]


class Carver:
    def __init__(self, tensor, ncols, name):
        self.t, self.n, self.off, self.name = tensor, ncols, 0, name

    def take(self, cols, rows=128):
        assert self.off + cols <= self.n, (
            f"{self.name}: overflow {self.off}+{cols}>{self.n}")
        ap = self.t.ap()[0:rows, self.off:self.off + cols]
        self.off += cols
        return ap

    def buf(self, cols, rows=128):
        b = Buf(self.take(cols, rows))
        b.ready = list(getattr(self, "fence", ()))
        return b

    def mark(self):
        return self.off

    def reset(self, off):
        self.off = off


def _inc(P, sem, k):
    sem.val += k
    return (sem, sem.val)


SIM_SAFE = True
_SELFWAIT_ENGINES = ("vector", "scalar", "gpsimd")


def E(P, eng, sem, build, waits=(), k=1, hard=False):
    if SIM_SAFE and eng in _SELFWAIT_ENGINES and k == 1:
        waits = tuple(waits) + ((sem, sem.val),)
        hard = True
    P.wait(eng, *waits, hard=hard)
    tok = _inc(P, sem, k)
    P.on(eng, lambda e, b=build, s=sem, kk=k: b(e).then_inc(s.h, kk))
    return tok


def _pc(v):
    return np.ascontiguousarray(np.asarray(v, np.float32).reshape(8, 128).T)


def _stage_host(inp):
    g = {}
    g["c0_maa_x"] = _pc(inp["L0_attn_time_maa_x"])
    g["c0_maa_w"] = _pc(inp["L0_attn_time_maa_w"])
    g["c0_maa_k"] = _pc(inp["L0_attn_time_maa_k"])
    g["c0_maa_v"] = _pc(inp["L0_attn_time_maa_v"])
    g["c0_maa_r"] = _pc(inp["L0_attn_time_maa_r"])
    g["c0_maa_g"] = _pc(inp["L0_attn_time_maa_g"])
    g["c0_decay"] = _pc(inp["L0_attn_time_decay"])
    g["c0_ln_w"] = _pc(inp["L0_attn_ln_w"])
    g["c0_ln_b"] = _pc(inp["L0_attn_ln_b"])
    g["c0_u"] = _pc(np.asarray(inp["L0_attn_time_faaaa"], np.float32).reshape(1024))
    w1 = np.asarray(inp["L0_attn_time_maa_w1"], np.float32)
    g["w_maa1"] = np.ascontiguousarray(
        w1.reshape(8, 128, 160).transpose(1, 0, 2).reshape(128, 8 * 160))
    w2r = np.asarray(inp["L0_attn_time_maa_w2"], np.float32)
    w2p = np.zeros((5, 128, 1024), np.float32)
    for s in range(5):
        r0 = 32 * s if s < 4 else 0
        w2p[s, r0:r0 + 32] = w2r[s]
    g["w_maa2"] = np.ascontiguousarray(w2p.reshape(640, 1024))
    d1 = np.asarray(inp["L0_attn_time_decay_w1"], np.float32)
    g["w_dw1"] = np.ascontiguousarray(
        d1.reshape(8, 128, 64).transpose(1, 0, 2).reshape(128, 8 * 64))
    g["w_dw2"] = np.asarray(inp["L0_attn_time_decay_w2"], np.float32)
    for nm in ("Wr", "Wk", "Wv", "Wg", "Wo"):
        g["w_" + nm] = np.asarray(inp["L0_attn_" + nm], np.float32)
    gw = np.asarray(inp["L0_ffn_gate_w"], np.float32)
    g["w_gate0"] = np.ascontiguousarray(
        gw.reshape(8, 128, 8).transpose(1, 0, 2).reshape(128, 64))
    g["c0_gate_b"] = np.asarray(inp["L0_ffn_gate_b"], np.float32).reshape(8, 1)
    g["c0_shb1"] = np.ascontiguousarray(
        np.asarray(inp["L0_ffn_sh_b1"], np.float32).reshape(32, 128).T)
    g["c0_shb2"] = _pc(inp["L0_ffn_sh_b2"])
    eb1 = np.asarray(inp["L0_ffn_eb1"], np.float32)
    g["c0_eb1"] = np.ascontiguousarray(
        eb1.reshape(8, 32, 128).transpose(2, 0, 1).reshape(128, 256))
    eb2 = np.asarray(inp["L0_ffn_eb2"], np.float32)
    g["c0_eb2"] = np.ascontiguousarray(
        eb2.reshape(8, 8, 128).transpose(2, 0, 1).reshape(128, 64))
    g["w_shw1"] = np.asarray(inp["L0_ffn_sh_w1"], np.float32)
    g["w_shw2"] = np.asarray(inp["L0_ffn_sh_w2"], np.float32)
    g["w_ew1"] = np.asarray(inp["L0_ffn_ew1"], np.float32)
    g["w_ew2"] = np.asarray(inp["L0_ffn_ew2"], np.float32)
    g["w_1Wq"] = np.ascontiguousarray(np.asarray(inp["L1_attn_Wq"], np.float32).T)
    g["w_1Wk"] = np.ascontiguousarray(np.asarray(inp["L1_attn_Wk"], np.float32).T)
    g["w_1Wv"] = np.ascontiguousarray(np.asarray(inp["L1_attn_Wv"], np.float32).T)
    g["c1_bq"] = _pc(inp["L1_attn_bq"])
    g["c1_bk"] = _pc(inp["L1_attn_bk"])
    g["c1_bv_row"] = np.asarray(inp["L1_attn_bv"], np.float32).reshape(1, 1024)
    wg1 = np.asarray(inp["L1_ffn_w_gate"], np.float32)
    g["w_gate1"] = np.ascontiguousarray(
        wg1.T.reshape(8, 128, 16).transpose(1, 0, 2).reshape(128, 128))
    g["w_1w1e"] = np.ascontiguousarray(
        np.asarray(inp["L1_ffn_w1_e"], np.float32).transpose(0, 2, 1))
    g["w_1w3e"] = np.ascontiguousarray(
        np.asarray(inp["L1_ffn_w3_e"], np.float32).transpose(0, 2, 1))
    g["w_1w2e"] = np.ascontiguousarray(
        np.asarray(inp["L1_ffn_w2_e"], np.float32).transpose(0, 2, 1))
    g["w_1w1s"] = np.ascontiguousarray(np.asarray(inp["L1_ffn_w1_s"], np.float32).T)
    g["w_1w3s"] = np.ascontiguousarray(np.asarray(inp["L1_ffn_w3_s"], np.float32).T)
    g["w_1w2s"] = np.ascontiguousarray(np.asarray(inp["L1_ffn_w2_s"], np.float32).T)
    g["c_ident"] = np.eye(128, dtype=np.float32)
    lt = (np.arange(128)[:, None] < np.arange(128)[None, :]).astype(np.float32)
    g["c_ltm"] = np.ascontiguousarray(np.tile(lt, (1, 4)))
    g["c_caus"] = np.where(np.arange(128)[None, :] <= np.arange(128)[:, None],
                           0.0, -1e30).astype(np.float32)
    oh8 = np.zeros((8, 1024), np.float32)
    for e in range(8):
        oh8[e, e * 128:(e + 1) * 128] = 1.0
    g["c_oh8"] = oh8
    oh16 = np.zeros((16, 2048), np.float32)
    for e in range(16):
        oh16[e, e * 128:(e + 1) * 128] = 1.0
    g["c_oh16"] = oh16
    ind = np.zeros((128, 128), np.float32)
    for cc in range(8):
        for p in range(128):
            ind[p, cc * 16 + 2 * cc + p // 64] = 1.0
    g["c_ind16"] = ind
    indt = np.zeros((16, 1024), np.float32)
    for cc in range(8):
        for p in range(128):
            indt[2 * cc + p // 64, cc * 128 + p] = 1.0
    g["c_ind16t"] = indt
    g["c_iotaeps"] = np.tile((-1e-7 * np.arange(16, dtype=np.float32))[None, :],
                             (128, 1))
    g["c_identf"] = np.eye(128, dtype=np.float32)
    f32names = ("c0_maa_x", "c0_maa_w", "c0_maa_k", "c0_maa_v", "c0_maa_r",
                "c0_maa_g", "c0_decay", "c0_ln_w", "c0_ln_b", "c0_u",
                "c0_shb1", "c0_shb2", "c0_eb1", "c0_eb2", "c1_bq", "c1_bk",
                "c0_gate_b", "c1_bv_row", "c_iotaeps", "c_identf")
    for nm in ("c_ind16", "c_ind16t", "c_oh8", "c_oh16"):
        g[nm] = g[nm].astype(ml_dtypes.bfloat16)
    bfnames = ("c_ident", "c_ltm", "c_caus", "w_maa1", "w_dw1", "w_dw2",
               "w_gate0", "w_gate1")
    def cat(names):
        parts, offs, off = [], {}, 0
        for nm in names:
            a = np.asarray(g[nm], np.float32)
            pad = np.zeros((128, a.shape[1]), np.float32)
            pad[0:a.shape[0]] = a
            parts.append(pad)
            offs[nm] = (off, a.shape[1], a.shape[0])
            off += a.shape[1]
        return np.ascontiguousarray(np.concatenate(parts, 1)), offs
    g["c_init32"], g["_offs32"] = cat(f32names)
    g["c_initbf"], g["_offsbf"] = cat(bfnames)
    return g


INIT_OFFS = {}


def stage_inputs_per_core(inputs):
    shared = _stage_host(inputs)
    INIT_OFFS["f32"] = shared.pop("_offs32")
    INIT_OFFS["bf"] = shared.pop("_offsbf")
    x = np.asarray(inputs["L0_attn_x"], np.float32)
    percore = []
    for i in range(8):
        d = dict(shared)
        d["x"] = np.ascontiguousarray(x[2 * i:2 * i + 2].transpose(0, 2, 1))
        percore.append(d)
    return percore


class Ctx:
    pass


PROLOGUE_ORDER = (
    "w_maa2", "w_Wr", "w_Wk", "w_Wv", "w_Wg", "w_Wo",
    "w_shw1", "w_shw2", "w_ew1", "w_ew2",
    "w_1Wq", "w_1Wk", "w_1Wv",
    "w_1w1s", "w_1w3s", "w_1w2s", "w_1w1e", "w_1w3e", "w_1w2e",
)


def make_ctx(nc, stage_shapes, n_blocks=4):
    c = Ctx()
    c.nc = nc
    c.n_blocks = n_blocks
    P = c.P = Prog(nc)
    c.inp = {}
    for name, shp in stage_shapes.items():
        dt = BF16 if (hasattr(shp, "dtype") or name in
                      ("c_ind16", "c_ind16t", "c_oh8", "c_oh16")) else F32
        c.inp[name] = nc.dram_tensor(name, list(shp), dt,
                                     kind="ExternalInput").ap()
    c.out = nc.dram_tensor("y", [NB, 1024, T_SEQ], F32, kind="ExternalOutput").ap()
    c.scr = {n: nc.dram_tensor("s" + n, list(s), BF16, kind="Internal").ap()
             for n, s in stage_shapes.items() if n.startswith("w_")}
    c.KS = nc.dram_tensor("KS", [1024, T_SEQ], BF16, kind="Internal").ap()
    c.VS = nc.dram_tensor("VS", [T_SEQ, 1024], BF16, kind="Internal").ap()
    c.QS = nc.dram_tensor("QS", [1024, T_SEQ], BF16, kind="Internal").ap()

    c.s_pe = P.sem("s_pe", "tensor")
    c.s_dve = P.sem("s_dve", "vector")
    c.s_act = P.sem("s_act", "scalar")
    c.s_pool = P.sem("s_pool", "gpsimd")
    c.s_gd = P.sem("s_gd")
    c.s_ld = P.sem("s_ld")
    c.s_out = P.sem("s_out")
    c.s_ws = [P.sem(f"s_ws{i}") for i in range(6)]
    c.s_x = P.sem("s_x")

    class APView:
        def __init__(self, ap_):
            self._ap = ap_

        def ap(self):
            return self._ap

    c.APView = APView
    c.XRES = nc.alloc_sbuf_tensor("XRES", [128, NCC * T_SEQ], F32)
    c.S = nc.alloc_sbuf_tensor("Swkv", [128, 512], F32)
    c.ONESC = nc.alloc_sbuf_tensor("ONESC", [128, 1], F32)
    c.ONESCB = nc.alloc_sbuf_tensor("ONESCB", [128, 1], BF16)
    c.ONESR = nc.alloc_sbuf_tensor("ONESR", [1, 128], F32)
    c.ZERO = nc.alloc_sbuf_tensor("ZEROt", [128, 128], F32)
    c.EPSLN = nc.alloc_sbuf_tensor("EPSLN", [128, 1], F32)
    c.DUMMY = nc.alloc_sbuf_tensor("DUMMYT", [128, 1], F32)
    c.EPSGN = nc.alloc_sbuf_tensor("EPSGN", [128, 1], F32)
    o32, obf = INIT_OFFS["f32"], INIT_OFFS["bf"]
    c.INIT32 = nc.alloc_sbuf_tensor("INIT32", [128, stage_shapes["c_init32"][1]],
                                    F32)
    c.INITBF = nc.alloc_sbuf_tensor("INITBF", [128, stage_shapes["c_initbf"][1]],
                                    BF16)

    def v32(nm):
        off, w, rows = o32[nm]
        return APView(c.INIT32.ap()[0:rows, off:off + w])

    def vbf(nm):
        off, w, rows = obf[nm]
        return APView(c.INITBF.ap()[0:rows, off:off + w])

    c.cols = {nm: v32(nm) for nm in
              ("c0_maa_x", "c0_maa_w", "c0_maa_k", "c0_maa_v", "c0_maa_r",
               "c0_maa_g", "c0_decay", "c0_ln_w", "c0_ln_b", "c0_u",
               "c0_shb1", "c0_shb2", "c0_eb1", "c0_eb2", "c1_bq", "c1_bk")}
    c.GATEB0 = v32("c0_gate_b")
    c.BVROW = v32("c1_bv_row")
    c.IOTAEPS = v32("c_iotaeps")
    c.IDENTF = v32("c_identf")
    c.IDENT = vbf("c_ident")
    c.LTM = vbf("c_ltm")
    c.CAUS = vbf("c_caus")
    c.MAA1 = vbf("w_maa1")
    c.DW1 = vbf("w_dw1")
    c.DW2 = vbf("w_dw2")
    c.GATE0 = vbf("w_gate0")
    c.GATE1 = vbf("w_gate1")


    c.AF = Carver(nc.alloc_sbuf_tensor("AF32", [128, 8448], F32), 8448, "AF32")
    c.AB = Carver(nc.alloc_sbuf_tensor("ABF", [128, 39808], BF16), 39808, "ABF")
    c.ROWS = [Buf(c.AF.take(512, rows=1)) for _ in range(4)]
    c._row_i = 0

    c.PS = [nc.alloc_psum_tensor(f"pb{i}", [128, 512], F32) for i in range(8)]
    c.PB = [Buf(t.ap()) for t in c.PS]
    c.xres = [Buf(c.XRES.ap()[:, cc * T_SEQ:(cc + 1) * T_SEQ]) for cc in range(NCC)]
    c.sbuf = Buf(c.S.ap())

    _init_consts(c)
    return c


def row(c):
    b = c.ROWS[c._row_i % len(c.ROWS)]
    c._row_i += 1
    return b


def _init_consts(c):
    P = c.P
    for t, v in ((c.ONESC, 1.0), (c.ONESCB, 1.0), (c.ONESR, 1.0), (c.ZERO, 0.0),
                 (c.EPSLN, LN_EPS), (c.EPSGN, GN_EPS)):
        E(P, "gpsimd", c.s_pool, lambda e, tt=t, vv=v: e.memset(tt.ap()[:, :], vv))

    E(P, "sync", c.s_ld, lambda e, d=c.INIT32.ap()[:, :],
      s=c.inp["c_init32"]: e.dma_start(d, s), k=16)
    E(P, "gpsimd", c.s_gd, lambda e, d=c.INITBF.ap()[:, :],
      s=c.inp["c_initbf"]: e.dma_start(d, s), k=16)
    c.tok_init = ((c.s_ld, c.s_ld.val), (c.s_gd, c.s_gd.val),
                  (c.s_pool, c.s_pool.val))


def build_prologue(c):
    P = c.P
    c.prol = {name: None for name in PROLOGUE_ORDER}
    for name in PROLOGUE_ORDER[:6]:
        s_w = P.sem("s_prol_" + name)
        tok = E(P, "gpsimd", s_w,
                lambda e, s=c.inp[name].flatten().unsqueeze(0),
                d=c.scr[name].flatten().unsqueeze(0):
                e.dma_start(d, s), waits=(), k=16)
        c.prol[name] = tok
    c.deferred_prol = None


class WRing:

    def __init__(self, c, carver, cols, nbuf=2, slot0=0):
        self.c = c
        self.bufs = [carver.buf(cols) for _ in range(nbuf)]
        self.sems = [c.s_ws[slot0 + i] for i in range(nbuf)]
        self.gsems = [c.P.sem(f"s_wg{slot0 + i}") for i in range(nbuf)]
        self.i = 0

    def load(self, src_ap, rows=128, cols=None, extra=(), row0=0):
        c = self.c
        slot = self.i % len(self.bufs)
        b = self.bufs[slot]
        self.i += 1
        dap = b.ap if cols is None else b.ap[row0:row0 + rows, 0:cols]
        if len(src_ap.shape) == 3:
            dap = dap.rearrange("p (a b) -> p a b", a=src_ap.shape[1])
        eng = "gpsimd" if src_ap.tensor.dtype == F32 else "sync"
        sem = (self.gsems if eng == "gpsimd" else self.sems)[slot]
        tok = E(c.P, eng, sem, lambda e, d=dap, s=src_ap: e.dma_start(d, s),
                waits=b.war() + tuple(extra) + ((sem, sem.val),), k=16)
        b.wrote(tok)
        return b


def emit_preln(c, t0, width, out_tiles, sq_buf, tmp_buf):
    P = c.P
    for blk in range(width // 512):
        tb = t0 + blk * 512
        pbs, pbq = c.PB[2], c.PB[3]
        tok = tok2 = None
        for cc in range(NCC):
            xap = c.XRES.ap()[:, cc * T_SEQ + tb: cc * T_SEQ + tb + 512]
            tsq = E(P, "scalar", c.s_act,
                    lambda e, o=sq_buf.ap, i=xap: e.activation(o, i, AF.Square),
                    waits=c.xres[cc].raw() + sq_buf.war())
            sq_buf.wrote(tsq)
            tok = E(P, "tensor", c.s_pe,
                    lambda e, o=c.PS[2].ap()[0:1, :], l=c.ONESC.ap()[:, :],
                    r=xap, s=(cc == 0), p=(cc == NCC - 1):
                    e.matmul(o, l, r, start=s, stop=p),
                    waits=c.xres[cc].raw() + (pbs.war() if cc == 0 else ()))
            c.xres[cc].read_by(tok)
            tok2 = E(P, "tensor", c.s_pe,
                     lambda e, o=c.PS[3].ap()[0:1, :], l=c.ONESCB.ap()[:, :],
                     r=sq_buf.ap, s=(cc == 0), p=(cc == NCC - 1):
                     e.matmul(o, l, r, start=s, stop=p),
                     waits=sq_buf.raw() + (pbq.war() if cc == 0 else ()))
            sq_buf.read_by(tok2)
        pbs.wrote(tok)
        pbq.wrote(tok2)
        mr, msq, sd, rst = row(c), row(c), row(c), row(c)
        t1 = E(P, "vector", c.s_dve, lambda e, o=mr.ap, i=c.PS[2].ap()[0:1, :]:
               e.tensor_scalar(o, i, 1.0 / 1024, None, OP.mult),
               waits=pbs.raw() + mr.war())
        pbs.read_by(t1)
        mr.wrote(t1)
        t2 = E(P, "vector", c.s_dve, lambda e, o=msq.ap, i=c.PS[3].ap()[0:1, :]:
               e.tensor_scalar(o, i, 1.0 / 1024, None, OP.mult),
               waits=pbq.raw() + msq.war())
        pbq.read_by(t2)
        msq.wrote(t2)
        t3 = E(P, "vector", c.s_dve, lambda e, o=sd.ap, a=mr.ap, b=mr.ap:
               e.tensor_tensor(o, a, b, OP.mult), waits=sd.war())
        t4 = E(P, "vector", c.s_dve, lambda e, o=msq.ap, a=msq.ap, b=sd.ap:
               e.tensor_tensor(o, a, b, OP.subtract))
        t4 = E(P, "vector", c.s_dve, lambda e, o=msq.ap, i=msq.ap:
               e.tensor_scalar(o, i, 0.0, None, OP.max))
        msq.wrote(t4)
        t5 = E(P, "scalar", c.s_act, lambda e, o=sd.ap, i=msq.ap:
               e.activation(o, i, AF.Sqrt, bias=c.EPSLN.ap()[0:1, :]),
               waits=(t4,) + c.tok_init)
        sd.wrote(t5)
        msq.read_by(t5)
        t6 = E(P, "vector", c.s_dve, lambda e, o=rst.ap, i=sd.ap:
               e.reciprocal(o, i), waits=(t5,) + rst.war())
        rst.wrote(t6)
        sd.read_by(t6)
        bm = E(P, "tensor", c.s_pe,
               lambda e, o=c.PS[2].ap()[:, :], l=c.ONESR.ap()[:, :], r=mr.ap:
               e.matmul(o, l, r, start=True, stop=True),
               waits=(t1,) + pbs.war())
        pbs.wrote(bm)
        mr.read_by(bm)
        br = E(P, "tensor", c.s_pe,
               lambda e, o=c.PS[3].ap()[:, :], l=c.ONESR.ap()[:, :], r=rst.ap:
               e.matmul(o, l, r, start=True, stop=True),
               waits=(t6,) + pbq.war())
        pbq.wrote(br)
        rst.read_by(br)
        for cc in range(NCC):
            xap = c.XRES.ap()[:, cc * T_SEQ + tb: cc * T_SEQ + tb + 512]
            ta = E(P, "vector", c.s_dve,
                   lambda e, o=tmp_buf.ap, a=xap, b=c.PS[2].ap()[:, :]:
                   e.tensor_tensor(o, a, b, OP.subtract),
                   waits=pbs.raw() + c.xres[cc].raw() + tmp_buf.war())
            pbs.read_by(ta)
            c.xres[cc].read_by(ta)
            tmp_buf.wrote(ta)
            oap = out_tiles[cc].ap[:, blk * 512:blk * 512 + 512]
            tz = E(P, "vector", c.s_dve,
                   lambda e, o=oap, a=tmp_buf.ap, b=c.PS[3].ap()[:, :]:
                   e.tensor_tensor(o, a, b, OP.mult),
                   waits=pbq.raw() + out_tiles[cc].war())
            pbq.read_by(tz)
            tmp_buf.read_by(tz)
            if blk == 0:
                out_tiles[cc].wrote(tz)
            else:
                out_tiles[cc].add_write(tz)


def pib(PIBUF, mc, ch, off, w=1):
    base = (mc * NCH0 + ch) * 130
    return PIBUF.ap[:, base + off: base + off + w]


def pib_tab(PIBUF, ch, off):
    v = PIBUF.ap.rearrange("p (mc ch w) -> p mc ch w", mc=NCC, w=130)
    return v[:, :, ch:ch + 1, off:off + 1].rearrange("p mc ch w -> p (mc ch w)")


DIRECT = frozenset(PROLOGUE_ORDER[6:])


def wsrc(c, wname):
    return c.inp[wname] if wname in DIRECT else c.scr[wname]


def wsup(c, wname, half, mwidth=256):
    return (wsrc(c, wname).rearrange("(kc p) m -> p kc m", p=128)
            [:, :, half * mwidth:(half + 1) * mwidth])


def phase_l0attn(c):
    P = c.P
    f0, b0 = c.AF.mark(), c.AB.mark()

    STRV = [c.AB.buf(512) for _ in range(NCC)]
    STRG = [c.AB.buf(512) for _ in range(NCC)]
    RTA = c.AB.take(NCC * 512)
    RT = [Buf(RTA[:, mc * 512:(mc + 1) * 512]) for mc in range(NCC)]
    KTEA = c.AB.take(NCC * 512)
    KTE = [Buf(KTEA[:, mc * 512:(mc + 1) * 512]) for mc in range(NCC)]
    KTOA = c.AB.take(NCC * 512)
    KTO = [Buf(KTOA[:, mc * 512:(mc + 1) * 512]) for mc in range(NCC)]
    PIBUF = Buf(c.AF.take(NCC * NCH0 * 130))
    CARRY = [Buf(c.AF.take(8)) for _ in range(2)]
    EMID = Buf(c.AF.take(32))
    CD = Buf(c.AF.take(32))
    CD2 = Buf(c.AF.take(32))
    NMID = Buf(c.AF.take(32))
    RCD = Buf(c.AF.take(32))
    IND16 = c.AB.buf(128)
    IND16T = Buf(c.AB.take(1024, rows=16))
    for _b, _nm in ((IND16, "c_ind16"), (IND16T, "c_ind16t")):
        _t = E(P, "sync", c.s_ld, lambda e, d=_b.ap, s=c.inp[_nm]:
               e.dma_start(d, s),
               waits=_b.war() + tuple(c.AB.fence)
               + ((c.s_ld, c.s_ld.val),), k=16)
        _b.wrote(_t)

    fi, bi = c.AF.mark(), c.AB.mark()
    ZT = [c.AB.buf(512) for _ in range(NCC)]
    XXT = [c.AB.buf(512) for _ in range(NCC)]
    XXX = [c.AB.buf(512) for _ in range(2)]
    STRR = [c.AB.buf(512) for _ in range(NCC)]
    SQB = XXX[0]
    TMPB = XXX[1]
    MXALL = c.AB.buf(512)
    MXALL2 = c.AB.buf(512)
    TH = c.AB.buf(512, rows=64)
    TMP2 = [c.AB.buf(512) for _ in range(2)]
    EXPWR = [Buf(c.AF.take(512)) for _ in range(2)]
    EXPX = [Buf(c.AF.take(512)) for _ in range(2)]
    fi2, bi2 = c.AF.mark(), c.AB.mark()

    c.AF.reset(fi)
    c.AB.reset(bi)
    VTM = [c.AB.buf(1024) for _ in range(NCH0)]
    yb_base = c.AB.mark()
    YBALL = c.AB.take(NCC * 512)
    YB = [Buf(c.AB.t.ap()[:, yb_base + cc * 512: yb_base + (cc + 1) * 512])
          for cc in range(NCC)]
    abuf_base = c.AB.mark()
    ABUF = c.AB.buf(512)
    TMPA = c.AB.buf(512)
    D4 = c.AB.buf(512)
    TMPA2 = D4
    KHT = [c.AB.buf(128) for _ in range(2)]
    RKA = c.AB.buf(1024)
    KHA = RKA
    YSQ = TMPA
    SSC2 = c.AB.buf(1024)
    ENWS = Buf(c.AF.t.ap()[:, fi:fi + 1024])
    ENWB = Buf(c.AB.t.ap()[:, abuf_base + 512:abuf_base + 1536])
    GMB = c.AB.buf(512, rows=16)
    GRB = c.AB.buf(512, rows=16)
    YNG = [c.AB.buf(512) for _ in range(NCC)]
    GSIL = [c.AB.buf(512)]
    RLW = ABUF
    GN = [Buf(c.AF.take(512, rows=16)) for _ in range(3)]
    DGT = Buf(c.AF.take(16))
    fi3, bi3 = c.AF.mark(), c.AB.mark()

    inner = (ZT + XXT + XXX + STRR + [MXALL, MXALL2, TH]
             + TMP2 + EXPWR + EXPX)
    post = (VTM + YB + YNG + GSIL + [ABUF, TMPA, D4, SSC2, ENWS, ENWB,
                                     GMB, GRB, DGT, RKA] + KHT + GN)
    for b in inner:
        b.aliases = post
        b.ready = list(c.AB.fence)
    for b in post:
        b.aliases = inner
        b.ready = list(c.AB.fence)
    for b in (STRV + STRG + RT + KTE + KTO + CARRY
              + [EMID, CD, CD2, NMID, RCD, PIBUF]):
        b.ready = list(c.AB.fence)
    ENWB.aliases = ENWB.aliases + [TMPA, D4]
    TMPA.aliases = TMPA.aliases + [ENWB]
    D4.aliases = D4.aliases + [ENWB]
    c.AF.reset(max(fi2, fi3))
    c.AB.reset(max(bi2, bi3))

    tS = E(P, "gpsimd", c.s_pool, lambda e: e.memset(c.S.ap()[:, :], 0.0),
           waits=c.sbuf.war())
    c.sbuf.wrote(tS)
    for i in range(2):
        tC = E(P, "gpsimd", c.s_pool, lambda e, a=CARRY[i].ap: e.memset(a, 0.0),
               waits=CARRY[i].war())
        CARRY[i].wrote(tC)
    for mc in range(NCC):
        tz1 = E(P, "gpsimd", c.s_pool,
                lambda e, a=KTE[mc].ap: e.memset(a, 0.0),
                waits=KTE[mc].war())
        KTE[mc].add_write(tz1)
        tz2 = E(P, "gpsimd", c.s_pool,
                lambda e, a=KTO[mc].ap: e.memset(a, 0.0),
                waits=KTO[mc].war())
        KTO[mc].add_write(tz2)
    tz3 = E(P, "gpsimd", c.s_pool, lambda e, a=SSC2.ap: e.memset(a, 0.0),
            waits=SSC2.war())
    SSC2.add_write(tz3)
    tz4 = E(P, "gpsimd", c.s_pool, lambda e, a=MXALL2.ap:
            e.memset(a, 0.0), waits=MXALL2.war())
    MXALL2.add_write(tz4)

    maa_s_col = [c.cols["c0_maa_w"], c.cols["c0_maa_k"], c.cols["c0_maa_v"],
                 c.cols["c0_maa_r"], c.cols["c0_maa_g"]]
    trunc = globals().get("L0_TRUNC", "full")

    def stream_tile(s_idx, dest_tiles):
        mxb = MXALL if s_idx < 4 else MXALL2
        lb = c.WRB.load(c.scr["w_maa2"][s_idx * 128:(s_idx + 1) * 128, :],
                        rows=128, cols=1024,
                        extra=(c.prol["w_maa2"],))
        for mc in range(NCC):
            psx, pbx = (c.PS[4], c.PB[4]) if mc % 2 == 0 else (c.PS[5], c.PB[5])
            tmm = E(P, "tensor", c.s_pe,
                    lambda e, o=psx.ap()[:, :],
                    l=lb.ap[0:128, mc * 128:(mc + 1) * 128],
                    r=mxb.ap:
                    e.matmul(o, l, r, start=True, stop=True),
                    waits=lb.raw() + mxb.raw() + pbx.war())
            lb.read_by(tmm)
            mxb.read_by(tmm)
            pbx.wrote(tmm)
            t2b = TMP2[mc % 2]
            tm2 = E(P, "vector", c.s_dve,
                    lambda e, o=t2b.ap, a=psx.ap()[:, :],
                    s=maa_s_col[s_idx].ap()[:, mc:mc + 1], b=XXT[mc].ap:
                    e.scalar_tensor_tensor(o, a, s, b, OP.add, OP.mult),
                    waits=pbx.raw() + XXT[mc].raw() + t2b.war() + c.tok_init)
            t2b.wrote(tm2)
            pbx.read_by(tm2)
            XXT[mc].read_by(tm2)
            ts3 = E(P, "gpsimd", c.s_pool,
                    lambda e, o=dest_tiles[mc].ap, a=t2b.ap, b=ZT[mc].ap:
                    e.tensor_tensor(o, a, b, OP.add),
                    waits=t2b.raw() + ZT[mc].raw() + dest_tiles[mc].war())
            dest_tiles[mc].wrote(ts3)
            t2b.read_by(ts3)
            ZT[mc].read_by(ts3)

    def project_scaled(s_idx, wname, out_tiles, kind):
        stream_tile(s_idx, STRR)
        for half in range(4):
            wb = c.WRA.load(wsup(c, wname, half), cols=2048,
                            extra=(c.prol[wname],))
            for sub in range(2):
                mc = half * 2 + sub
                psx, pbx = (c.PS[6], c.PB[6]) if mc % 2 == 0 else (c.PS[7], c.PB[7])
                tokm = None
                for kc in range(NCC):
                    tokm = E(P, "tensor", c.s_pe,
                             lambda e, o=psx.ap()[:, :],
                             l=wb.ap[:, kc * 256 + sub * 128:
                                     kc * 256 + sub * 128 + 128],
                             r=STRR[kc].ap, s=(kc == 0), p=(kc == NCC - 1):
                             e.matmul(o, l, r, start=s, stop=p),
                             waits=wb.raw() + STRR[kc].raw()
                             + (pbx.war() if kc == 0 else ()))
                    wb.read_by(tokm)
                    STRR[kc].read_by(tokm)
                pbx.wrote(tokm)
                ex = EXPX[mc % 2]
                for ch in range(NCH0):
                    if kind == "r":
                        iap, sc, bap = pib(PIBUF, mc, ch, 0, 128), -1.0, \
                            pib(PIBUF, mc, ch, 64)
                    else:
                        iap, sc, bap = pib(PIBUF, mc, ch, 1, 128), 1.0, \
                            NMID.ap[:, ch * 8 + mc:ch * 8 + mc + 1]
                    tex = E(P, "scalar", c.s_act,
                            lambda e, o=ex.ap[:, ch * 128:(ch + 1) * 128],
                            i=iap, s=sc, b=bap:
                            e.activation(o, i, AF.Exp, scale=s, bias=b),
                            waits=PIBUF.raw() + NMID.raw()
                            + (ex.war() if ch == 0 else ()))
                    PIBUF.read_by(tex)
                    NMID.read_by(tex)
                    (ex.wrote if ch == 0 else ex.add_write)(tex)
                tcl = E(P, "vector", c.s_dve,
                        lambda e, o=ex.ap, i=ex.ap:
                        e.tensor_scalar(o, i, 1e17, None, OP.min),
                        waits=ex.raw())
                ex.add_write(tcl)
                if kind == "r":
                    tev = E(P, "vector", c.s_dve,
                            lambda e, o=out_tiles[mc].ap, a=psx.ap()[:, :],
                            b=ex.ap:
                            e.tensor_tensor(o, a, b, OP.mult),
                            waits=pbx.raw() + ex.raw() + out_tiles[mc].war())
                    out_tiles[mc].wrote(tev)
                    pbx.read_by(tev)
                    ex.read_by(tev)
                else:
                    be, bo = KTE[mc], KTO[mc]
                    t1 = E(P, "vector", c.s_dve,
                           lambda e, o=be.ap[0:64, :], a=psx.ap()[0:64, :],
                           b=ex.ap[0:64, :]:
                           e.tensor_tensor(o, a, b, OP.mult),
                           waits=pbx.raw() + ex.raw() + be.war())
                    be.add_write(t1)
                    pbx.read_by(t1)
                    ex.read_by(t1)
                    t2 = E(P, "vector", c.s_dve,
                           lambda e, o=bo.ap[64:128, :], a=psx.ap()[64:128, :],
                           b=ex.ap[64:128, :]:
                           e.tensor_tensor(o, a, b, OP.mult),
                           waits=bo.war())
                    bo.add_write(t2)
                    pbx.read_by(t2)
                    ex.read_by(t2)

    for tile in range(NT0):
        t0 = tile * TT0
        cin, cout = CARRY[tile % 2], CARRY[(tile + 1) % 2]
        if trunc == "none":
            continue
        emit_preln(c, t0, TT0, ZT, SQB, TMPB)
        if trunc == "preln":
            continue
        for cc in range(NCC):
            tcs = E(P, "vector", c.s_dve,
                    lambda e, o=cout.ap[:, cc:cc + 1], i=ZT[cc].ap[:, 511:512]:
                    e.tensor_copy(o, i), waits=ZT[cc].raw() + cout.war())
            (cout.wrote if cc == 0 else cout.add_write)(tcs)
            ZT[cc].read_by(tcs)
            tx = E(P, "vector", c.s_dve,
                   lambda e, o=XXT[cc].ap[:, 1:512], a=ZT[cc].ap[:, 0:511],
                   b=ZT[cc].ap[:, 1:512]:
                   e.tensor_tensor(o, a, b, OP.subtract),
                   waits=ZT[cc].raw() + XXT[cc].war() + cin.raw())
            tx0 = E(P, "vector", c.s_dve,
                    lambda e, o=XXT[cc].ap[:, 0:1], a=cin.ap[:, cc:cc + 1],
                    b=ZT[cc].ap[:, 0:1]:
                    e.tensor_tensor(o, a, b, OP.subtract))
            XXT[cc].wrote(tx)
            XXT[cc].add_write(tx0)
            ZT[cc].read_by(tx0)
            cin.read_by(tx0)
            xb = XXX[cc % 2]
            txx = E(P, "vector", c.s_dve,
                    lambda e, o=xb.ap, a=XXT[cc].ap,
                    s=c.cols["c0_maa_x"].ap()[:, cc:cc + 1], b=ZT[cc].ap:
                    e.scalar_tensor_tensor(o, a, s, b, OP.mult, OP.add),
                    waits=xb.war() + c.tok_init)
            xb.wrote(txx)
            XXT[cc].read_by(txx)
            ZT[cc].read_by(txx)
            pA, pB = c.PB[0], c.PB[1]
            tokA = E(P, "tensor", c.s_pe,
                     lambda e, o=c.PS[0].ap()[:, :],
                     l=c.MAA1.ap()[:, cc * 160:cc * 160 + 128],
                     r=xb.ap, s=(cc == 0), p=(cc == NCC - 1):
                     e.matmul(o, l, r, start=s, stop=p),
                     waits=xb.raw() + (pA.war() if cc == 0 else ())
                     + c.tok_init)
            xb.read_by(tokA)
            tokB = E(P, "tensor", c.s_pe,
                     lambda e, o=c.PS[1].ap()[0:32, :],
                     l=c.MAA1.ap()[:, cc * 160 + 128:cc * 160 + 160],
                     r=xb.ap, s=(cc == 0), p=(cc == NCC - 1):
                     e.matmul(o, l, r, start=s, stop=p),
                     waits=(pB.war() if cc == 0 else ()))
            xb.read_by(tokB)
        pA.wrote(tokA)
        pB.wrote(tokB)
        for dst, rows, src_ap, pb_ in (
                (MXALL, (0, 64), c.PS[0].ap()[0:64, :], pA),
                (MXALL, (64, 128), c.PS[0].ap()[64:128, :], pA),
                (MXALL2, (0, 32), c.PS[1].ap()[0:32, :], pB)):
            tm = E(P, "scalar", c.s_act,
                   lambda e, o=dst.ap[rows[0]:rows[1], :], i=src_ap:
                   e.activation(o, i, AF.Tanh),
                   waits=pb_.raw() + (dst.war() if (rows[0] == 0 or dst is MXALL2)
                                      else ()))
            (dst.wrote if (dst is MXALL and rows[0] == 0) else dst.add_write)(tm)
            pb_.read_by(tm)

        tz = E(P, "gpsimd", c.s_pool,
               lambda e, a=PIBUF.ap.rearrange("p (q w) -> p q w", w=130)
               [:, :, 0:1].rearrange("p q w -> p (q w)"):
               e.memset(a, 0.0), waits=PIBUF.war())
        PIBUF.wrote(tz)
        stream_tile(0, STRR)
        tth = None
        for cc in range(NCC):
            tth = E(P, "tensor", c.s_pe,
                    lambda e, o=c.PS[5].ap()[0:64, :],
                    l=c.DW1.ap()[:, cc * 64:(cc + 1) * 64], r=STRR[cc].ap,
                    s=(cc == 0), p=(cc == NCC - 1):
                    e.matmul(o, l, r, start=s, stop=p),
                    waits=STRR[cc].raw() + (c.PB[5].war() if cc == 0 else ())
                    + c.tok_init)
            STRR[cc].read_by(tth)
        c.PB[5].wrote(tth)
        tte = E(P, "scalar", c.s_act,
                lambda e, o=TH.ap, i=c.PS[5].ap()[0:64, :]:
                e.activation(o, i, AF.Tanh), waits=c.PB[5].raw() + TH.war())
        TH.wrote(tte)
        c.PB[5].read_by(tte)
        for mc in range(NCC):
            psx, pbx = (c.PS[4], c.PB[4]) if mc % 2 == 0 else (c.PS[5], c.PB[5])
            tw = E(P, "tensor", c.s_pe,
                   lambda e, o=psx.ap()[:, :],
                   l=c.DW2.ap()[:, mc * 128:(mc + 1) * 128], r=TH.ap:
                   e.matmul(o, l, r, start=True, stop=True),
                   waits=TH.raw() + pbx.war() + c.tok_init)
            TH.read_by(tw)
            pbx.wrote(tw)
            ew = EXPWR[mc % 2]
            te = E(P, "scalar", c.s_act,
                   lambda e, o=ew.ap, i=psx.ap()[:, :],
                   b=c.cols["c0_decay"].ap()[:, mc:mc + 1]:
                   e.activation(o, i, AF.Exp, bias=b),
                   waits=pbx.raw() + ew.war() + c.tok_init)
            ew.wrote(te)
            pbx.read_by(te)
            for ch in range(NCH0):
                base = (mc * NCH0 + ch) * 130
                tsc = E(P, "vector", c.s_dve,
                        lambda e, o=PIBUF.ap[:, base + 1:base + 129],
                        d0=ew.ap[:, ch * 128:(ch + 1) * 128],
                        d1=c.ZERO.ap()[:, 0:128]:
                        e.tensor_tensor_scan(o, d0, d1, 0.0, OP.add, OP.add),
                        waits=ew.raw() + PIBUF.raw() + c.tok_init)
                PIBUF.add_write(tsc)
                ew.read_by(tsc)
        for ch in range(NCH0):
            te1 = E(P, "scalar", c.s_act,
                    lambda e, o=EMID.ap[:, ch * 8:ch * 8 + 8],
                    i=pib_tab(PIBUF, ch, 64):
                    e.activation(o, i, AF.Exp, scale=-1.0),
                    waits=PIBUF.raw() + EMID.war())
            (EMID.wrote if ch == 0 else EMID.add_write)(te1)
            PIBUF.read_by(te1)
            te2 = E(P, "scalar", c.s_act,
                    lambda e, o=CD.ap[:, ch * 8:ch * 8 + 8],
                    i=pib_tab(PIBUF, ch, 128):
                    e.activation(o, i, AF.Exp, scale=-1.0),
                    waits=PIBUF.raw() + CD.war())
            (CD.wrote if ch == 0 else CD.add_write)(te2)
            PIBUF.read_by(te2)
            tnm = E(P, "vector", c.s_dve,
                    lambda e, o=NMID.ap[:, ch * 8:ch * 8 + 8],
                    i=pib_tab(PIBUF, ch, 64):
                    e.tensor_scalar(o, i, -1.0, None, OP.mult),
                    waits=PIBUF.raw() + NMID.war())
            (NMID.wrote if ch == 0 else NMID.add_write)(tnm)
            PIBUF.read_by(tnm)
        tr1 = E(P, "vector", c.s_dve, lambda e, o=RCD.ap, i=EMID.ap:
                e.reciprocal(o, i), waits=EMID.raw() + RCD.war())
        RCD.wrote(tr1)
        EMID.read_by(tr1)
        tr2 = E(P, "vector", c.s_dve, lambda e, o=CD2.ap, a=CD.ap, b=RCD.ap:
                e.tensor_tensor(o, a, b, OP.mult),
                waits=CD.raw() + CD2.war())
        CD2.wrote(tr2)
        RCD.read_by(tr2)
        CD.read_by(tr2)

        if trunc == "wpath":
            continue
        project_scaled(3, "w_Wr", RT, "r")
        project_scaled(1, "w_Wk", None, "k")
        stream_tile(2, STRV)
        stream_tile(4, STRG)

        for half in range(4):
            wb = c.WRA.load(wsup(c, "w_Wv", half), cols=2048,
                            extra=(c.prol["w_Wv"],))
            for tb in range(NCH0):
                psx, pbx = (c.PS[6], c.PB[6]) if tb % 2 == 0 else (c.PS[7], c.PB[7])
                tokm = None
                for kc in range(NCC):
                    tokm = E(P, "tensor", c.s_pe,
                             lambda e, o=psx.ap()[:, 0:256],
                             l=STRV[kc].ap[:, tb * 128:(tb + 1) * 128],
                             r=wb.ap[:, kc * 256:(kc + 1) * 256],
                             s=(kc == 0), p=(kc == NCC - 1):
                             e.matmul(o, l, r, start=s, stop=p),
                             waits=wb.raw() + STRV[kc].raw()
                             + (pbx.war() if kc == 0 else ()))
                    wb.read_by(tokm)
                    STRV[kc].read_by(tokm)
                pbx.wrote(tokm)
                tv = E(P, "scalar", c.s_act,
                       lambda e, o=VTM[tb].ap[:, half * 256:(half + 1) * 256],
                       i=psx.ap()[:, 0:256]: e.activation(o, i, AF.Copy),
                       waits=pbx.raw() + (VTM[tb].war() if half == 0 else ()))
                (VTM[tb].wrote if half == 0 else VTM[tb].add_write)(tv)
                pbx.read_by(tv)

        if trunc == "streams":
            continue
        if getattr(c, "deferred_prol", None) is not None:
            c.deferred_prol()
            c.deferred_prol = None
        tzs = E(P, "gpsimd", c.s_pool, lambda e, a=SSC2.ap: e.memset(a, 0.0),
                waits=SSC2.war())
        SSC2.wrote(tzs)
        for ch in range(NCH0):
            wtr = globals().get("WKV_TRUNC", "full")
            for par in range(2):
                rr = 64 * par
                o3 = (SSC2.ap[rr:rr + 64, :]
                      .rearrange("p (m t) -> p m t", t=128)
                      [:, :, par * 64:par * 64 + 64])
                a3 = (c.S.ap()[rr:rr + 64, :]
                      .rearrange("p (m t) -> p m t", t=64))
                b3 = (EMID.ap[rr:rr + 64, ch * 8:ch * 8 + 8]
                      .rearrange("p (m o) -> p m o", o=1)
                      .broadcast_to([64, 8, 64]))
                tsnap = E(P, "gpsimd", c.s_pool,
                          lambda e, o=o3, a=a3, b=b3:
                          e.tensor_tensor(o, a, b, OP.mult),
                          waits=c.sbuf.raw() + EMID.raw()
                          + (SSC2.war() if par == 0 else ()))
                c.sbuf.read_by(tsnap)
                EMID.read_by(tsnap)
                SSC2.add_write(tsnap)
            if wtr == "ssc":
                continue
            pall = PIBUF.ap.rearrange("p (m t) -> p m t", t=NCH0 * 130)
            ten1 = E(P, "vector", c.s_dve,
                     lambda e, o=ENWS.ap.rearrange("p (m t) -> p m t", t=128),
                     a=pall[:, :, ch * 130:ch * 130 + 128],
                     b=pall[:, :, ch * 130 + 1:ch * 130 + 129]:
                     e.tensor_tensor(o, a, b, OP.subtract),
                     waits=PIBUF.raw() + ENWS.war())
            ENWS.wrote(ten1)
            PIBUF.read_by(ten1)
            ten2 = E(P, "scalar", c.s_act,
                     lambda e, o=ENWB.ap, i=ENWS.ap:
                     e.activation(o, i, AF.Exp),
                     waits=ENWS.raw() + ENWB.war())
            ENWB.wrote(ten2)
            ENWS.read_by(ten2)
            for par, arr, kbl in ((0, KTEA, KTE), (1, KTOA, KTO)):
                rr = 64 * par
                o3 = (RKA.ap[rr:rr + 64, :]
                      .rearrange("p (m t) -> p m t", t=128))
                a3 = (RTA[rr:rr + 64, :]
                      .rearrange("p (m t) -> p m t", t=512)
                      [:, :, ch * 128:(ch + 1) * 128])
                b3 = (arr[rr:rr + 64, :]
                      .rearrange("p (m t) -> p m t", t=512)
                      [:, :, ch * 128:(ch + 1) * 128])
                tr = E(P, "vector", c.s_dve,
                       lambda e, o=o3, a=a3, b=b3:
                       e.tensor_tensor(o, a, b, OP.mult),
                       waits=(RKA.war() if par == 0 else ())
                       + tuple(t for b_ in RT for t in b_.raw())
                       + tuple(t for b_ in kbl for t in b_.raw()))
                (RKA.wrote if par == 0 else RKA.add_write)(tr)
                for b_ in RT:
                    b_.read_by(tr)
                for b_ in kbl:
                    b_.read_by(tr)
            tu = E(P, "vector", c.s_dve,
                   lambda e, o=RKA.ap.rearrange("p (m t) -> p m t", t=128),
                   a=RKA.ap.rearrange("p (m t) -> p m t", t=128),
                   b=c.cols["c0_u"].ap()[:, 0:8]
                   .rearrange("p (m o) -> p m o", o=1)
                   .broadcast_to([128, 8, 128]):
                   e.tensor_tensor(o, a, b, OP.mult),
                   waits=c.tok_init)
            RKA.add_write(tu)
            te2 = E(P, "vector", c.s_dve,
                    lambda e, o=RKA.ap, a=RKA.ap, b=ENWB.ap:
                    e.tensor_tensor(o, a, b, OP.mult),
                    waits=ENWB.raw())
            RKA.add_write(te2)
            ENWB.read_by(te2)
            tdm = None
            for mc in range(NCC):
                tdm = E(P, "tensor", c.s_pe,
                        lambda e, o=c.PS[1].ap()[:, 0:16],
                        l=RKA.ap[:, mc * 128:(mc + 1) * 128],
                        r=IND16.ap[:, mc * 16:(mc + 1) * 16],
                        s=(mc == 0), p=(mc == NCC - 1):
                        e.matmul(o, l, r, start=s, stop=p),
                        waits=RKA.raw() + IND16.raw()
                        + (c.PB[1].war() if mc == 0 else ()))
                RKA.read_by(tdm)
                IND16.read_by(tdm)
            c.PB[1].wrote(tdm)
            tdg = E(P, "scalar", c.s_act,
                    lambda e, o=DGT.ap, i=c.PS[1].ap()[:, 0:16]:
                    e.activation(o, i, AF.Identity),
                    waits=c.PB[1].raw() + DGT.war())
            DGT.wrote(tdg)
            c.PB[1].read_by(tdg)

            if wtr == "diag":
                continue
            for hg in range(4):
                psa, pba = (c.PS[4], c.PB[4]) if hg % 2 == 0 else (c.PS[2], c.PB[2])
                tokA = None
                for h4 in range(4):
                    h = hg * 4 + h4
                    mc = h // 2
                    kb = KTE[mc] if h % 2 == 0 else KTO[mc]
                    tokA = E(P, "tensor", c.s_pe,
                             lambda e, o=psa.ap()[:, h4 * 128:(h4 + 1) * 128],
                             l=kb.ap[:, ch * 128:(ch + 1) * 128],
                             r=RT[mc].ap[:, ch * 128:(ch + 1) * 128]:
                             e.matmul(o, l, r, start=True, stop=True),
                             waits=kb.raw() + RT[mc].raw()
                             + (pba.war() if h4 == 0 else ()))
                    kb.read_by(tokA)
                    RT[mc].read_by(tokA)
                pba.wrote(tokA)
                if wtr == "atmm":
                    continue
                tD = E(P, "gpsimd", c.s_pool,
                       lambda e, o=D4.ap.rearrange("p (m t) -> p m t", t=128),
                       a=c.IDENT.ap()[:, :]
                       .rearrange("p (o t) -> p o t", o=1)
                       .broadcast_to([128, 4, 128]),
                       b=DGT.ap[:, hg * 4:hg * 4 + 4]
                       .rearrange("p (m o) -> p m o", o=1)
                       .broadcast_to([128, 4, 128]):
                       e.tensor_tensor(o, a, b, OP.mult),
                       waits=DGT.raw() + D4.war() + c.tok_init)
                DGT.read_by(tD)
                D4.wrote(tD)
                tA1 = E(P, "vector", c.s_dve,
                        lambda e, o=TMPA.ap, a=psa.ap()[:, :], b=c.LTM.ap()[:, :]:
                        e.tensor_tensor(o, a, b, OP.mult),
                        waits=pba.raw() + TMPA.war() + c.tok_init)
                TMPA.wrote(tA1)
                pba.read_by(tA1)
                tA2 = E(P, "gpsimd", c.s_pool,
                        lambda e, o=ABUF.ap, a=TMPA.ap, b=D4.ap:
                        e.tensor_tensor(o, a, b, OP.add),
                        waits=TMPA.raw() + D4.raw() + ABUF.war())
                ABUF.wrote(tA2)
                TMPA.read_by(tA2)
                D4.read_by(tA2)
                if wtr in ("at", "atmm"):
                    continue
                psy, pby = (c.PS[6], c.PB[6]) if hg < 2 else (c.PS[7], c.PB[7])
                ycol = (hg % 2) * 256
                for h4 in range(4):
                    h = hg * 4 + h4
                    mc, rr = h // 2, 64 * (h % 2)
                    dst = psy.ap()[rr:rr + 64,
                                   ycol + (h4 // 2) * 128:ycol + (h4 // 2) * 128 + 128]
                    t1 = E(P, "tensor", c.s_pe,
                           lambda e, o=dst,
                           l=VTM[ch].ap[:, mc * 128 + rr:mc * 128 + rr + 64],
                           r=ABUF.ap[:, h4 * 128:(h4 + 1) * 128]:
                           e.matmul(o, l, r, start=True, stop=False),
                           waits=VTM[ch].raw() + ABUF.raw()
                           + (pby.war() if (hg % 2 == 0 and h4 == 0) else ()))
                    VTM[ch].read_by(t1)
                    ABUF.read_by(t1)
                    toky = E(P, "tensor", c.s_pe,
                             lambda e, o=dst, l=SSC2.ap[:, h * 64:(h + 1) * 64],
                             r=RT[mc].ap[:, ch * 128:(ch + 1) * 128]:
                             e.matmul(o, l, r, start=False, stop=True),
                             waits=SSC2.raw() + RT[mc].raw())
                    SSC2.read_by(toky)
                    RT[mc].read_by(toky)
                    pby.add_write(toky)
            if wtr in ("y", "atmm"):
                continue
            for half in range(2):
                psy, pby = (c.PS[6], c.PB[6]) if half == 0 else (c.PS[7], c.PB[7])
                base = yb_base + half * 4 * 512
                oview = (c.AB.t.ap()[:, base:base + 2048]
                         .rearrange("p (g f) -> p g f", f=512)
                         [:, :, ch * 128:(ch + 1) * 128])
                ty = E(P, "vector", c.s_dve,
                       lambda e, o=oview,
                       i=psy.ap()[:, :].rearrange("p (g f) -> p g f", f=128):
                       e.tensor_copy(o, i),
                       waits=pby.raw() + tuple(t for b in YB[half * 4:half * 4 + 4]
                                               for t in b.war()))
                for b in YB[half * 4:half * 4 + 4]:
                    (b.wrote if ch == 0 else b.add_write)(ty)
                pby.read_by(ty)
            if wtr in ("yevac", "atmm2"):
                continue
            for par, arr, kbl in ((0, KTEA, KTE), (1, KTOA, KTO)):
                rr = 64 * par
                o3 = (KHA.ap[rr:rr + 64, :]
                      .rearrange("p (m t) -> p m t", t=128))
                a3 = (arr[rr:rr + 64, :]
                      .rearrange("p (m t) -> p m t", t=512)
                      [:, :, ch * 128:(ch + 1) * 128])
                b3 = (CD2.ap[rr:rr + 64, ch * 8:ch * 8 + 8]
                      .rearrange("p (m o) -> p m o", o=1)
                      .broadcast_to([64, 8, 128]))
                tk = E(P, "gpsimd", c.s_pool,
                       lambda e, o=o3, a=a3, b=b3:
                       e.tensor_tensor(o, a, b, OP.mult),
                       waits=(KHA.war() if par == 0 else ()) + CD2.raw()
                       + tuple(t for b_ in kbl for t in b_.raw()))
                (KHA.wrote if par == 0 else KHA.add_write)(tk)
                for b_ in kbl:
                    b_.read_by(tk)
                CD2.read_by(tk)
            tlast = None
            for mc in range(NCC):
                pst = c.PS[1].ap().bitcast(BF16)[:, 0:128]
                tk2 = E(P, "tensor", c.s_pe,
                        lambda e, o=pst, i=KHA.ap[:, mc * 128:(mc + 1) * 128],
                        ident=c.IDENT.ap()[:, :]:
                        e.transpose(o, i, ident),
                        waits=KHA.raw() + c.PB[1].war() + c.tok_init)
                c.PB[1].wrote(tk2)
                KHA.read_by(tk2)
                kht = KHT[mc % 2]
                tk3 = E(P, "scalar", c.s_act,
                        lambda e, o=kht.ap, i=pst:
                        e.activation(o, i, AF.Identity),
                        waits=c.PB[1].raw() + kht.war())
                kht.wrote(tk3)
                c.PB[1].read_by(tk3)
                for hh in range(2):
                    h = mc * 2 + hh
                    rr = 64 * hh
                    dst = c.PS[5].ap()[rr:rr + 64, mc * 64:(mc + 1) * 64]
                    tlast = E(P, "tensor", c.s_pe,
                              lambda e, o=dst, l=kht.ap[:, rr:rr + 64],
                              r=VTM[ch].ap[:, mc * 128 + rr:mc * 128 + rr + 64]:
                              e.matmul(o, l, r, start=True, stop=True),
                              waits=kht.raw() + VTM[ch].raw()
                              + (c.PB[5].war() if (mc == 0 and hh == 0) else ()))
                    kht.read_by(tlast)
                    VTM[ch].read_by(tlast)
                    c.PB[5].add_write(tlast)
            for par in range(2):
                rr = 64 * par
                o3 = (c.S.ap()[rr:rr + 64, :]
                      .rearrange("p (m t) -> p m t", t=64))
                b3 = (CD.ap[rr:rr + 64, ch * 8:ch * 8 + 8]
                      .rearrange("p (m o) -> p m o", o=1)
                      .broadcast_to([64, 8, 64]))
                tS1 = E(P, "gpsimd", c.s_pool,
                        lambda e, o=o3, a=o3, b=b3:
                        e.tensor_tensor(o, a, b, OP.mult),
                        waits=c.sbuf.war() + CD.raw())
                c.sbuf.add_write(tS1)
                CD.read_by(tS1)
            tS2 = E(P, "vector", c.s_dve,
                    lambda e, o=c.S.ap()[:, :], a=c.S.ap()[:, :],
                    b=c.PS[5].ap()[:, :]:
                    e.tensor_tensor(o, a, b, OP.add),
                    waits=c.PB[5].raw() + c.sbuf.war() + c.sbuf.raw())
            c.sbuf.add_write(tS2)
            c.PB[5].read_by(tS2)

        if trunc == "wkv":
            continue
        wgb = None
        for cc in range(NCC):
            if cc % 2 == 0:
                wgb = c.WRA.load(wsup(c, "w_Wg", cc // 2), cols=2048,
                                 extra=(c.prol["w_Wg"],))
            tokg = None
            for kc in range(NCC):
                tokg = E(P, "tensor", c.s_pe,
                         lambda e, o=c.PS[0].ap()[:, :],
                         l=wgb.ap[:, kc * 256 + (cc % 2) * 128:
                                  kc * 256 + (cc % 2) * 128 + 128],
                         r=STRG[kc].ap, s=(kc == 0), p=(kc == NCC - 1):
                         e.matmul(o, l, r, start=s, stop=p),
                         waits=wgb.raw() + STRG[kc].raw()
                         + (c.PB[0].war() if kc == 0 else ()))
                wgb.read_by(tokg)
                STRG[kc].read_by(tokg)
            c.PB[0].wrote(tokg)
            gs = GSIL[0]
            tg = E(P, "scalar", c.s_act,
                   lambda e, o=gs.ap, i=c.PS[0].ap()[:, :]:
                   e.activation(o, i, AF.Silu), waits=c.PB[0].raw() + gs.war())
            gs.wrote(tg)
            c.PB[0].read_by(tg)
            tq = E(P, "scalar", c.s_act,
                   lambda e, o=YSQ.ap, i=YB[cc].ap: e.activation(o, i, AF.Square),
                   waits=YB[cc].raw() + YSQ.war())
            YSQ.wrote(tq)
            YB[cc].read_by(tq)
            tm1 = E(P, "tensor", c.s_pe,
                    lambda e, o=c.PS[2].ap()[0:16, :],
                    l=IND16.ap[:, cc * 16:(cc + 1) * 16], r=YB[cc].ap:
                    e.matmul(o, l, r, start=True, stop=True),
                    waits=YB[cc].raw() + IND16.raw() + c.PB[2].war())
            c.PB[2].wrote(tm1)
            YB[cc].read_by(tm1)
            IND16.read_by(tm1)
            tm2 = E(P, "tensor", c.s_pe,
                    lambda e, o=c.PS[3].ap()[0:16, :],
                    l=IND16.ap[:, cc * 16:(cc + 1) * 16], r=YSQ.ap:
                    e.matmul(o, l, r, start=True, stop=True),
                    waits=YSQ.raw() + c.PB[3].war())
            c.PB[3].wrote(tm2)
            YSQ.read_by(tm2)
            IND16.read_by(tm2)
            gm, gv, gt = GN[0], GN[1], GN[2]
            ts1 = E(P, "vector", c.s_dve,
                    lambda e, o=gm.ap, i=c.PS[2].ap()[0:16, :]:
                    e.tensor_scalar(o, i, 1.0 / 64, None, OP.mult),
                    waits=c.PB[2].raw() + gm.war())
            gm.wrote(ts1)
            c.PB[2].read_by(ts1)
            ts2 = E(P, "vector", c.s_dve,
                    lambda e, o=gv.ap, i=c.PS[3].ap()[0:16, :]:
                    e.tensor_scalar(o, i, 1.0 / 64, None, OP.mult),
                    waits=c.PB[3].raw() + gv.war())
            gv.wrote(ts2)
            c.PB[3].read_by(ts2)
            ts3 = E(P, "vector", c.s_dve,
                    lambda e, o=gt.ap, a=gm.ap, b=gm.ap:
                    e.tensor_tensor(o, a, b, OP.mult), waits=gt.war())
            gt.wrote(ts3)
            gm.read_by(ts3)
            ts4 = E(P, "vector", c.s_dve,
                    lambda e, o=gv.ap, a=gv.ap, b=gt.ap:
                    e.tensor_tensor(o, a, b, OP.subtract))
            ts4 = E(P, "vector", c.s_dve,
                    lambda e, o=gv.ap, i=gv.ap:
                    e.tensor_scalar(o, i, 0.0, None, OP.max))
            gv.wrote(ts4)
            gt.read_by(ts4)
            ts5 = E(P, "scalar", c.s_act,
                    lambda e, o=gv.ap, i=gv.ap:
                    e.activation(o, i, AF.Sqrt, bias=c.EPSGN.ap()[0:16, :]),
                    waits=(ts4,) + c.tok_init)
            gv.wrote(ts5)
            ts6 = E(P, "vector", c.s_dve,
                    lambda e, o=gv.ap, i=gv.ap: e.reciprocal(o, i),
                    waits=(ts5,))
            gv.wrote(ts6)
            tb1 = E(P, "vector", c.s_dve,
                    lambda e, o=GMB.ap, i=gm.ap: e.tensor_copy(o, i),
                    waits=gm.raw() + GMB.war())
            GMB.wrote(tb1)
            gm.read_by(tb1)
            tb2 = E(P, "vector", c.s_dve,
                    lambda e, o=GRB.ap, i=gv.ap: e.tensor_copy(o, i),
                    waits=gv.raw() + GRB.war())
            GRB.wrote(tb2)
            gv.read_by(tb2)
            bm = E(P, "tensor", c.s_pe,
                   lambda e, o=c.PS[2].ap()[:, :],
                   l=IND16T.ap[:, cc * 128:(cc + 1) * 128], r=GMB.ap:
                   e.matmul(o, l, r, start=True, stop=True),
                   waits=GMB.raw() + IND16T.raw() + c.PB[2].war())
            c.PB[2].wrote(bm)
            GMB.read_by(bm)
            IND16T.read_by(bm)
            br = E(P, "tensor", c.s_pe,
                   lambda e, o=c.PS[3].ap()[:, :],
                   l=IND16T.ap[:, cc * 128:(cc + 1) * 128], r=GRB.ap:
                   e.matmul(o, l, r, start=True, stop=True),
                   waits=GRB.raw() + c.PB[3].war())
            c.PB[3].wrote(br)
            GRB.read_by(br)
            IND16T.read_by(br)
            tn1 = E(P, "vector", c.s_dve,
                    lambda e, o=TMPA.ap, a=YB[cc].ap, b=c.PS[2].ap()[:, :]:
                    e.tensor_tensor(o, a, b, OP.subtract),
                    waits=c.PB[2].raw() + YB[cc].raw() + TMPA.war())
            TMPA.wrote(tn1)
            c.PB[2].read_by(tn1)
            YB[cc].read_by(tn1)
            tn2 = E(P, "vector", c.s_dve,
                    lambda e, o=RLW.ap, i=c.PS[3].ap()[:, :],
                    s=c.cols["c0_ln_w"].ap()[:, cc:cc + 1]:
                    e.tensor_scalar(o, i, s, None, OP.mult),
                    waits=c.PB[3].raw() + RLW.war() + c.tok_init)
            RLW.wrote(tn2)
            c.PB[3].read_by(tn2)
            tn3 = E(P, "gpsimd", c.s_pool,
                    lambda e, o=TMPA2.ap, a=TMPA.ap, b=RLW.ap:
                    e.tensor_tensor(o, a, b, OP.mult),
                    waits=TMPA.raw() + RLW.raw() + TMPA2.war())
            TMPA2.wrote(tn3)
            TMPA.read_by(tn3)
            RLW.read_by(tn3)
            tn4 = E(P, "vector", c.s_dve,
                    lambda e, o=YNG[cc].ap, a=TMPA2.ap,
                    s=c.cols["c0_ln_b"].ap()[:, cc:cc + 1], b=gs.ap:
                    e.scalar_tensor_tensor(o, a, s, b, OP.add, OP.mult),
                    waits=TMPA2.raw() + gs.raw() + YNG[cc].war() + c.tok_init)
            YNG[cc].wrote(tn4)
            TMPA2.read_by(tn4)
            gs.read_by(tn4)
        for half in range(4):
            wb = c.WRA.load(wsup(c, "w_Wo", half), cols=2048,
                            extra=(c.prol["w_Wo"],))
            for sub in range(2):
                mc = half * 2 + sub
                psx, pbx = (c.PS[0], c.PB[0]) if mc % 2 == 0 else (c.PS[1], c.PB[1])
                tokm = None
                for kc in range(NCC):
                    tokm = E(P, "tensor", c.s_pe,
                             lambda e, o=psx.ap()[:, :],
                             l=wb.ap[:, kc * 256 + sub * 128:
                                     kc * 256 + sub * 128 + 128],
                             r=YNG[kc].ap, s=(kc == 0), p=(kc == NCC - 1):
                             e.matmul(o, l, r, start=s, stop=p),
                             waits=wb.raw() + YNG[kc].raw()
                             + (pbx.war() if kc == 0 else ()))
                    wb.read_by(tokm)
                    YNG[kc].read_by(tokm)
                pbx.wrote(tokm)
                xap = c.XRES.ap()[:, mc * T_SEQ + t0: mc * T_SEQ + t0 + 512]
                ta = E(P, "vector", c.s_dve,
                       lambda e, o=xap, a=xap, b=psx.ap()[:, :]:
                       e.tensor_tensor(o, a, b, OP.add),
                       waits=pbx.raw() + c.xres[mc].war())
                c.xres[mc].add_write(ta)
                pbx.read_by(ta)

    c.dbg = dict(ZT=ZT, XXT=XXT, XXX=XXX, STRR=STRR, RT=RT, KTE=KTE, KTO=KTO,
                 VTM=VTM, YB=YB, YNG=YNG, PIBUF=PIBUF, EMID=EMID, CD=CD,
                 CD2=CD2, NMID=NMID, STRV=STRV, STRG=STRG, TH=TH,
                 SSC2=SSC2, DGT=DGT, GSIL=GSIL, KHT=KHT,
                 IND16=IND16, IND16T=IND16T)
    c.AF.reset(f0)
    c.AB.reset(b0)
    set_fence(c)


def set_fence(c):
    c.AB.fence = c.AF.fence = (
        (c.s_pe, c.s_pe.val), (c.s_dve, c.s_dve.val),
        (c.s_act, c.s_act.val), (c.s_pool, c.s_pool.val))


Carver.fence = ()


def phase_load_x(c, seq):
    P = c.P
    waits = tuple(t for b in c.xres for t in b.war()) + ((c.s_x, c.s_x.val),)
    tok = E(P, "sync", c.s_x,
            lambda e,
            o=c.XRES.ap()[:, :].rearrange("p (cc t) -> p cc t", cc=NCC),
            i=c.inp["x"][seq].rearrange("(cc p) t -> p cc t", p=128):
            e.dma_start(o, i), waits=waits, k=16)
    for b in c.xres:
        b.wrote(tok)


def phase_store(c, seq):
    P = c.P
    waits = (tuple(t for b in c.xres for t in b.raw())
             + ((c.s_out, c.s_out.val),))
    tok = E(P, "sync", c.s_out,
            lambda e,
            o=c.out[seq].rearrange("(cc p) t -> p cc t", p=128),
            i=c.XRES.ap()[:, :].rearrange("p (cc t) -> p cc t", cc=NCC):
            e.dma_start(o, i), waits=waits, k=16)
    for b in c.xres:
        b.read_by(tok)


def build_program(stage_shapes, n_blocks=4):
    nc = bass.Bass("TRN2", target_bir_lowering=False)
    c = make_ctx(nc, stage_shapes, n_blocks)
    c.WRA = WRing(c, Carver(nc.alloc_sbuf_tensor("WRA", [128, 4096], BF16),
                            4096, "WRA"), 2048, 2, slot0=0)
    c.WRB = WRing(c, Carver(nc.alloc_sbuf_tensor("WRB", [128, 4096], BF16),
                            4096, "WRB"), 2048, 2, slot0=2)
    build_prologue(c)
    for _e in ENGINES:
        c.P.wait(_e, *c.tok_init)
    E(c.P, "scalar", c.s_act, lambda e: e.activation(
        c.DUMMY.ap()[:, :], c.ONESC.ap()[:, :], AF.Square))
    phases = [phase_l0attn]
    if n_blocks > 1:
        phases.append(phase_l0ffn)
    if n_blocks > 2:
        phases.append(phase_l1attn)
    if n_blocks > 3:
        phases.append(phase_l1ffn)
    for seq in range(NB):
        phase_load_x(c, seq)
        for ph in phases:
            ph(c)
        phase_store(c, seq)
    P = c.P
    P.on("sync", lambda e, s=c.s_out, v=c.s_out.val: e.wait_ge(s.h, v))
    P.finalize()
    return nc, c


_BUILT = {}


def kernel(**inputs):
    percore = stage_inputs_per_core(inputs)
    shapes = {k: v.shape for k, v in percore[0].items()}
    key = tuple(sorted((k, tuple(s)) for k, s in shapes.items()))
    nc, _ = build_program(shapes, n_blocks=N_BLOCKS)
    res = run_bass_kernel_spmd(nc, percore, core_ids=list(range(8)))
    out = np.empty((16, 2048, 1024), np.float32)
    for i in range(8):
        y = res.results[i]["y"]
        out[2 * i:2 * i + 2] = np.asarray(y, np.float32).transpose(0, 2, 1)
    return out


N_BLOCKS = 4


def phase_l0ffn(c):
    P = c.P
    f0, b0 = c.AF.mark(), c.AB.mark()
    XN = [c.AB.buf(512) for _ in range(NCC)]
    SQB = c.AB.buf(512)
    TMPB = c.AB.buf(512)
    H = [c.AB.buf(512) for _ in range(32)]
    CB = [c.AB.buf(512) for _ in range(2)]
    TMPF = c.AB.buf(512)
    OH8 = Buf(c.AB.take(1024, rows=8))
    GSR = Buf(c.AF.take(512, rows=32))
    GT = Buf(c.AF.take(32))
    WEF8 = Buf(c.AB.take(512, rows=8))
    PRB = Buf(c.AF.take(8))
    M8 = Buf(c.AF.take(8))
    MSK = Buf(c.AF.take(8))
    W8B = c.AB.buf(8)
    RM = Buf(c.AF.take(4))
    for b in (XN + H + CB + [SQB, TMPB, TMPF, OH8, GSR, GT, WEF8, PRB, M8,
                             MSK, W8B, RM]):
        b.ready = list(c.AB.fence)
    _t = E(P, "sync", c.s_ld, lambda e, d=OH8.ap, s=c.inp["c_oh8"]:
           e.dma_start(d, s), waits=OH8.war() + ((c.s_ld, c.s_ld.val),), k=16)
    OH8.wrote(_t)

    def expert_cfg(e):
        if e < 8:
            return (c.inp["w_ew1"][e], c.inp["w_ew2"][e],
                    None, None,
                    c.cols["c0_eb1"].ap()[:, e * 32:(e + 1) * 32],
                    c.cols["c0_eb2"].ap()[:, e * 8:(e + 1) * 8], False)
        return (c.inp["w_shw1"], c.inp["w_shw2"],
                None, None,
                c.cols["c0_shb1"].ap()[:, :],
                c.cols["c0_shb2"].ap()[:, :], True)

    for tt in range(T_SEQ // 512):
        t0 = tt * 512
        emit_preln(c, t0, 512, XN, SQB, TMPB)
        for mc in range(NCC):
            xap = c.XRES.ap()[:, mc * T_SEQ + t0: mc * T_SEQ + t0 + 512]
            ta = E(P, "vector", c.s_dve,
                   lambda e, o=xap, a=xap, b=XN[mc].ap:
                   e.tensor_tensor(o, a, b, OP.add),
                   waits=XN[mc].raw() + c.xres[mc].war())
            c.xres[mc].add_write(ta)
            XN[mc].read_by(ta)
        tok = None
        for kc in range(NCC):
            tok = E(P, "tensor", c.s_pe,
                    lambda e, o=c.PS[2].ap()[0:8, :],
                    l=c.GATE0.ap()[:, kc * 8:(kc + 1) * 8], r=XN[kc].ap,
                    s=(kc == 0), p=(kc == NCC - 1):
                    e.matmul(o, l, r, start=s, stop=p),
                    waits=XN[kc].raw() + (c.PB[2].war() if kc == 0 else ()))
            XN[kc].read_by(tok)
        c.PB[2].wrote(tok)
        tg = E(P, "scalar", c.s_act,
               lambda e, o=GSR.ap[0:8, :], i=c.PS[2].ap()[0:8, :],
               b=c.GATEB0.ap()[:, :]:
               e.activation(o, i, AF.Identity, bias=b),
               waits=c.PB[2].raw() + GSR.war() + c.tok_init)
        GSR.wrote(tg)
        c.PB[2].read_by(tg)
        for tb in range(4):
            ttr = E(P, "tensor", c.s_pe,
                    lambda e, o=c.PS[3].ap()[:, 0:32],
                    i=GSR.ap[0:32, tb * 128:(tb + 1) * 128],
                    ident=c.IDENTF.ap()[0:32, 0:32]:
                    e.transpose(o, i, ident),
                    waits=GSR.raw() + c.PB[3].war() + c.tok_init)
            c.PB[3].wrote(ttr)
            GSR.read_by(ttr)
            te = E(P, "vector", c.s_dve,
                   lambda e, o=GT.ap[:, tb * 8:(tb + 1) * 8],
                   i=c.PS[3].ap()[:, 0:8]:
                   e.tensor_copy(o, i),
                   waits=c.PB[3].raw() + (GT.war() if tb == 0 else ()))
            (GT.wrote if tb == 0 else GT.add_write)(te)
            c.PB[3].read_by(te)
        for tb in range(4):
            gtb = GT.ap[:, tb * 8:(tb + 1) * 8]
            t1 = E(P, "vector", c.s_dve,
                   lambda e, o=RM.ap[:, 0:1], i=gtb:
                   e.reduce_max(o, i, axis=AX), waits=GT.raw() + RM.war())
            t2 = E(P, "vector", c.s_dve,
                   lambda e, o=RM.ap[:, 1:2], i=RM.ap[:, 0:1]:
                   e.tensor_scalar(o, i, -1.0, None, OP.mult))
            RM.wrote(t2)
            GT.read_by(t1)
            t3 = E(P, "scalar", c.s_act,
                   lambda e, o=PRB.ap, i=gtb, b=RM.ap[:, 1:2],
                   a=RM.ap[:, 2:3]:
                   e.activation(o, i, AF.Exp, bias=b, accum_out=a),
                   waits=(t2,) + PRB.war())
            PRB.wrote(t3)
            RM.add_write(t3)
            t4 = E(P, "vector", c.s_dve,
                   lambda e, o=M8.ap, i=PRB.ap: e.max(o, i),
                   waits=PRB.raw() + M8.war())
            M8.wrote(t4)
            PRB.read_by(t4)
            t5 = E(P, "vector", c.s_dve,
                   lambda e, o=MSK.ap, i=PRB.ap, s=M8.ap[:, 1:2]:
                   e.tensor_single_scalar(o, i, s, OP.is_ge),
                   waits=M8.raw() + MSK.war(), hard=True)
            MSK.wrote(t5)
            M8.read_by(t5)
            t6 = E(P, "vector", c.s_dve,
                   lambda e, o=MSK.ap, a=PRB.ap, b=MSK.ap:
                   e.tensor_tensor(o, a, b, OP.mult))
            MSK.wrote(t6)
            PRB.read_by(t6)
            t7 = E(P, "vector", c.s_dve,
                   lambda e, o=RM.ap[:, 3:4], i=MSK.ap:
                   e.tensor_reduce(o, i, AX, OP.add))
            RM.add_write(t7)
            t8 = E(P, "vector", c.s_dve,
                   lambda e, o=RM.ap[:, 0:1], a=RM.ap[:, 2:3], b=RM.ap[:, 3:4]:
                   e.scalar_tensor_tensor(o, a, 1e-9, b, OP.mult, OP.add),
                   hard=True)
            t9 = E(P, "vector", c.s_dve,
                   lambda e, o=RM.ap[:, 1:2], i=RM.ap[:, 0:1]:
                   e.reciprocal(o, i))
            RM.wrote(t9)
            t10 = E(P, "vector", c.s_dve,
                    lambda e, o=W8B.ap, i=MSK.ap, s=RM.ap[:, 1:2]:
                    e.tensor_scalar(o, i, s, None, OP.mult),
                    waits=W8B.war(), hard=True)
            W8B.wrote(t10)
            MSK.read_by(t10)
            RM.read_by(t10)
            tw = E(P, "tensor", c.s_pe,
                   lambda e, o=c.PS[3].ap().bitcast(BF16)[0:8, 0:128],
                   i=W8B.ap, ident=c.IDENT.ap()[:, :]:
                   e.transpose(o, i, ident),
                   waits=W8B.raw() + c.PB[3].war())
            c.PB[3].wrote(tw)
            W8B.read_by(tw)
            twe = E(P, "vector", c.s_dve,
                    lambda e, o=WEF8.ap[0:8, tb * 128:(tb + 1) * 128],
                    i=c.PS[3].ap().bitcast(BF16)[0:8, 0:128]:
                    e.tensor_copy(o, i),
                    waits=c.PB[3].raw() + (WEF8.war() if tb == 0 else ()))
            (WEF8.wrote if tb == 0 else WEF8.add_write)(twe)
            c.PB[3].read_by(twe)
        for ex in range(9):
            w1s, w2s, p1, p2, b1ap, b2ap, shared = expert_cfg(ex)
            if not shared:
                tcb = E(P, "tensor", c.s_pe,
                        lambda e, o=c.PS[0].ap()[:, :],
                        l=OH8.ap[0:8, ex * 128:(ex + 1) * 128],
                        r=WEF8.ap[0:8, :]:
                        e.matmul(o, l, r, start=True, stop=True),
                        waits=OH8.raw() + WEF8.raw() + c.PB[0].war())
                c.PB[0].wrote(tcb)
                OH8.read_by(tcb)
                WEF8.read_by(tcb)
                cb = CB[ex % 2]
                tcv = E(P, "scalar", c.s_act,
                        lambda e, o=cb.ap, i=c.PS[0].ap()[:, :]:
                        e.activation(o, i, AF.Identity),
                        waits=c.PB[0].raw() + cb.war())
                cb.wrote(tcv)
                c.PB[0].read_by(tcv)
            for sup in range(16):
                wb = c.WRA.load(
                    w1s.rearrange("(kc p) m -> p kc m", p=128)
                    [:, :, sup * 256:(sup + 1) * 256], cols=2048, extra=(p1,))
                for mt in range(2):
                    it = sup * 2 + mt
                    psx, pbx = ((c.PS[4], c.PB[4]) if it % 2 == 0
                                else (c.PS[5], c.PB[5]))
                    tokm = None
                    for kc in range(NCC):
                        tokm = E(P, "tensor", c.s_pe,
                                 lambda e, o=psx.ap()[:, :],
                                 l=wb.ap[:, kc * 256 + mt * 128:
                                         kc * 256 + mt * 128 + 128],
                                 r=XN[kc].ap, s=(kc == 0), p=(kc == NCC - 1):
                                 e.matmul(o, l, r, start=s, stop=p),
                                 waits=wb.raw() + XN[kc].raw()
                                 + (pbx.war() if kc == 0 else ()))
                        wb.read_by(tokm)
                        XN[kc].read_by(tokm)
                    pbx.wrote(tokm)
                    th = E(P, "scalar", c.s_act,
                           lambda e, o=H[it].ap, i=psx.ap()[:, :],
                           b=b1ap[:, it:it + 1]:
                           e.activation(o, i, AF.Gelu_apprx_tanh, bias=b),
                           waits=pbx.raw() + H[it].war() + c.tok_init)
                    H[it].wrote(th)
                    pbx.read_by(th)
            for db in range(4):
                toks = [None, None]
                for kq in range(4):
                    wb2 = c.WRA.load(
                        w2s.rearrange("(kc p) m -> p kc m", p=128)
                        [:, kq * 8:(kq + 1) * 8, db * 256:(db + 1) * 256],
                        cols=2048, extra=(p2,))
                    for dt in range(2):
                        psx, pbx = ((c.PS[6], c.PB[6]) if dt == 0
                                    else (c.PS[7], c.PB[7]))
                        for k8 in range(NCC):
                            kc = kq * 8 + k8
                            tokm = E(P, "tensor", c.s_pe,
                                     lambda e, o=psx.ap()[:, :],
                                     l=wb2.ap[:, k8 * 256 + dt * 128:
                                              k8 * 256 + dt * 128 + 128],
                                     r=H[kc].ap,
                                     s=(kc == 0), p=(kc == 31):
                                     e.matmul(o, l, r, start=s, stop=p),
                                     waits=wb2.raw() + H[kc].raw()
                                     + (pbx.war() if kc == 0 else ()))
                            wb2.read_by(tokm)
                            H[kc].read_by(tokm)
                            toks[dt] = tokm
                for dt in range(2):
                    mc = db * 2 + dt
                    psx, pbx = ((c.PS[6], c.PB[6]) if dt == 0
                                else (c.PS[7], c.PB[7]))
                    pbx.wrote(toks[dt])
                    xap = c.XRES.ap()[:, mc * T_SEQ + t0: mc * T_SEQ + t0 + 512]
                    if shared:
                        tev = E(P, "vector", c.s_dve,
                                lambda e, o=xap, a=psx.ap()[:, :],
                                s=b2ap[:, mc:mc + 1], b=xap:
                                e.scalar_tensor_tensor(o, a, s, b,
                                                       OP.add, OP.add),
                                waits=pbx.raw() + c.xres[mc].war()
                                + c.tok_init)
                        c.xres[mc].add_write(tev)
                        pbx.read_by(tev)
                    else:
                        cb = CB[ex % 2]
                        tev = E(P, "vector", c.s_dve,
                                lambda e, o=TMPF.ap, a=psx.ap()[:, :],
                                s=b2ap[:, mc:mc + 1], b=cb.ap:
                                e.scalar_tensor_tensor(o, a, s, b,
                                                       OP.add, OP.mult),
                                waits=pbx.raw() + cb.raw() + TMPF.war()
                                + c.tok_init)
                        TMPF.wrote(tev)
                        pbx.read_by(tev)
                        cb.read_by(tev)
                        tad = E(P, "vector", c.s_dve,
                                lambda e, o=xap, a=xap, b=TMPF.ap:
                                e.tensor_tensor(o, a, b, OP.add),
                                waits=c.xres[mc].war())
                        c.xres[mc].add_write(tad)
                        TMPF.read_by(tad)
    c.dbgf = dict(XN=XN, H=H, CB=CB, GT=GT, WEF8=WEF8, GSR=GSR, OH8=OH8)
    c.AF.reset(f0)
    c.AB.reset(b0)
    set_fence(c)


def phase_l1attn(c):
    P = c.P
    f0, b0 = c.AF.mark(), c.AB.mark()
    SC = [Buf(c.AF.take(2048)) for _ in range(2)]
    RM = [Buf(c.AF.take(4)) for _ in range(2)]
    KFr = Buf(c.AB.take(16384))
    bA = c.AB.mark()
    XN = [c.AB.buf(512 * 4) for _ in range(NCC)]
    SQB = c.AB.buf(512)
    TMPB = c.AB.buf(512)
    BVT = c.AB.buf(1024)
    STG = [c.AB.buf(512) for _ in range(2)]
    bA2 = c.AB.mark()
    c.AB.reset(bA)
    VFr = Buf(c.AB.take(16384))
    QR = [c.AB.buf(1024) for _ in range(2)]
    PBUF = c.AB.buf(2048)
    PT = c.AB.buf(2048)
    bB2 = c.AB.mark()
    c.AB.reset(max(bA2, bB2))
    partA = XN + [SQB, TMPB, BVT] + STG
    partB = [VFr, PBUF, PT] + QR
    for b in partA:
        b.aliases = partB
    for b in partB:
        b.aliases = partA
    s_kv = [P.sem(f"s_kv{len(P.sems)}") for _ in range(10)]
    for b in (XN + STG + QR + SC + RM + [SQB, TMPB, BVT, PBUF, PT, KFr, VFr]):
        b.ready = list(c.AB.fence)
    qtoks = [[] for _ in range(4)]
    kstores = []
    vstores = []

    for half in range(2):
        tb0 = E(P, "tensor", c.s_pe,
                lambda e, o=c.PS[0].ap()[:, :], l=c.ONESR.ap()[:, :],
                r=c.BVROW.ap()[0:1, half * 512:(half + 1) * 512]:
                e.matmul(o, l, r, start=True, stop=True),
                waits=c.tok_init + c.PB[0].war())
        c.PB[0].wrote(tb0)
        tb1 = E(P, "vector", c.s_dve,
                lambda e, o=BVT.ap[:, half * 512:(half + 1) * 512],
                i=c.PS[0].ap()[:, :]: e.tensor_copy(o, i),
                waits=c.PB[0].raw() + (BVT.war() if half == 0 else ()))
        (BVT.wrote if half == 0 else BVT.add_write)(tb1)
        c.PB[0].read_by(tb1)

    emit_preln(c, 0, T_SEQ, XN, SQB, TMPB)

    def proj(wname, bias_col, out_cb, kstore):
        for sup in range(4):
            wb = c.WRA.load(wsup(c, wname, sup), cols=2048,
                            extra=(c.prol[wname],))
            for mt in range(2):
                mc = sup * 2 + mt
                for nb in range(4):
                    psx, pbx = ((c.PS[4], c.PB[4]) if nb % 2 == 0
                                else (c.PS[5], c.PB[5]))
                    tokm = None
                    for kc in range(NCC):
                        tokm = E(P, "tensor", c.s_pe,
                                 lambda e, o=psx.ap()[:, :],
                                 l=wb.ap[:, kc * 256 + mt * 128:
                                         kc * 256 + mt * 128 + 128],
                                 r=XN[kc].ap[:, nb * 512:(nb + 1) * 512],
                                 s=(kc == 0), p=(kc == NCC - 1):
                                 e.matmul(o, l, r, start=s, stop=p),
                                 waits=wb.raw() + XN[kc].raw()
                                 + (pbx.war() if kc == 0 else ()))
                        wb.read_by(tokm)
                        XN[kc].read_by(tokm)
                    pbx.wrote(tokm)
                    out_cb(mc, nb, psx, pbx)

    def q_out(mc, nb, psx, pbx):
        st = STG[nb % 2]
        tq = E(P, "scalar", c.s_act,
               lambda e, o=st.ap, i=psx.ap()[:, :],
               b=c.cols["c1_bq"].ap()[:, mc:mc + 1]:
               e.activation(o, i, AF.Identity, bias=b),
               waits=pbx.raw() + st.war() + c.tok_init)
        st.wrote(tq)
        pbx.read_by(tq)
        sem = s_kv[nb % 2]
        td = E(P, "sync", sem,
               lambda e, o=c.QS[mc * 128:(mc + 1) * 128,
                              nb * 512:(nb + 1) * 512], i=st.ap:
               e.dma_start(o, i),
               waits=st.raw() + ((sem, sem.val),), k=16)
        st.read_by(td)
        qtoks[nb].append(td)

    def k_out(mc, nb, psx, pbx):
        st = STG[nb % 2]
        tk = E(P, "scalar", c.s_act,
               lambda e, o=st.ap, i=psx.ap()[:, :],
               b=c.cols["c1_bk"].ap()[:, mc:mc + 1]:
               e.activation(o, i, AF.Identity, bias=b),
               waits=pbx.raw() + st.war() + c.tok_init)
        st.wrote(tk)
        pbx.read_by(tk)
        sem = s_kv[2 + nb % 2]
        td = E(P, "sync", sem,
               lambda e, o=c.KS[mc * 128:(mc + 1) * 128,
                              nb * 512:(nb + 1) * 512], i=st.ap:
               e.dma_start(o, i),
               waits=st.raw() + ((sem, sem.val),), k=16)
        st.read_by(td)
        kstores.append(td)

    proj("w_1Wq", None, q_out, False)
    proj("w_1Wk", None, k_out, False)
    tkf = E(P, "sync", s_kv[6],
            lambda e, o=KFr.ap.rearrange("p (kc t) -> p kc t", kc=8),
            i=c.KS.rearrange("(kc p) t -> p kc t", p=128):
            e.dma_start(o, i),
            waits=KFr.war() + tuple(kstores) + ((s_kv[6], 0),), k=16)
    KFr.wrote(tkf)

    for sup in range(4):
        wb = c.WRA.load(wsup(c, "w_1Wv", sup), cols=2048,
                        extra=(None,))
        for tb in range(16):
            psx, pbx = ((c.PS[4], c.PB[4]) if tb % 2 == 0
                        else (c.PS[5], c.PB[5]))
            tokm = None
            for kc in range(NCC):
                tokm = E(P, "tensor", c.s_pe,
                         lambda e, o=psx.ap()[:, 0:256],
                         l=XN[kc].ap[:, tb * 128:(tb + 1) * 128],
                         r=wb.ap[:, kc * 256:(kc + 1) * 256],
                         s=(kc == 0), p=(kc == NCC - 1):
                         e.matmul(o, l, r, start=s, stop=p),
                         waits=wb.raw() + XN[kc].raw()
                         + (pbx.war() if kc == 0 else ()))
                wb.read_by(tokm)
                XN[kc].read_by(tokm)
            pbx.wrote(tokm)
            st = STG[tb % 2]
            tv = E(P, "vector", c.s_dve,
                   lambda e, o=st.ap[:, 0:256], a=psx.ap()[:, 0:256],
                   b=BVT.ap[:, sup * 256:(sup + 1) * 256]:
                   e.tensor_tensor(o, a, b, OP.add),
                   waits=pbx.raw() + BVT.raw() + st.war())
            st.wrote(tv)
            pbx.read_by(tv)
            BVT.read_by(tv)
            sem = s_kv[4 + tb % 2]
            td = E(P, "sync", sem,
                   lambda e, o=c.VS[tb * 128:(tb + 1) * 128,
                                    sup * 256:(sup + 1) * 256],
                   i=st.ap[:, 0:256]:
                   e.dma_start(o, i),
                   waits=st.raw() + ((sem, sem.val),), k=16)
            st.read_by(td)
            vstores.append(td)

    vftoks = []
    for ck in range(4):
        tvf = E(P, "sync", s_kv[7],
                lambda e,
                o=VFr.ap[:, ck * 4096:(ck + 1) * 4096].rearrange(
                    "p (tb e) -> p tb e", e=1024),
                i=c.VS.rearrange("(tb p) e -> p tb e", p=128)
                [:, ck * 4:(ck + 1) * 4, :]:
                e.dma_start(o, i),
                waits=(VFr.war() + tuple(vstores) if ck == 0
                       else ()) + ((s_kv[7], s_kv[7].val),), k=16)
        (VFr.wrote if ck == 0 else VFr.add_write)(tvf)
        vftoks.append(tvf)

    def load_q(qc):
        qb = QR[qc % 2]
        sem = s_kv[8 + qc % 2]
        tql = E(P, "sync", sem,
                lambda e, o=qb.ap.rearrange("p (mc t) -> p mc t", t=128),
                i=c.QS.rearrange("(mc p) t -> p mc t", p=128)
                [:, :, qc * 128:(qc + 1) * 128]:
                e.dma_start(o, i),
                waits=qb.war() + tuple(qtoks[qc // 4])
                + ((sem, sem.val),), k=16)
        qb.wrote(tql)
    load_q(0)
    load_q(1)

    for qc in range(16):
        kr = (qc + 1) * 128
        nkb = (kr + 511) // 512
        qb = QR[qc % 2]
        SCq, RMq = SC[qc % 2], RM[qc % 2]
        for nb in range(nkb):
            w = min(512, kr - nb * 512)
            psx, pbx = ((c.PS[4], c.PB[4]) if nb % 2 == 0
                        else (c.PS[5], c.PB[5]))
            tokm = None
            for kc in range(NCC):
                tokm = E(P, "tensor", c.s_pe,
                         lambda e, o=psx.ap()[:, 0:w],
                         l=qb.ap[:, kc * 128:(kc + 1) * 128],
                         r=KFr.ap[:, kc * 2048 + nb * 512:
                                  kc * 2048 + nb * 512 + w],
                         s=(kc == 0), p=(kc == NCC - 1):
                         e.matmul(o, l, r, start=s, stop=p),
                         waits=KFr.raw() + qb.raw()
                         + (pbx.war() if kc == 0 else ()))
                KFr.read_by(tokm)
                qb.read_by(tokm)
            pbx.wrote(tokm)
            tse = E(P, "scalar", c.s_act,
                    lambda e, o=SCq.ap[:, nb * 512:nb * 512 + w],
                    i=psx.ap()[:, 0:w]: e.activation(o, i, AF.Identity),
                    waits=pbx.raw() + (SCq.war() if nb == 0 else ()))
            (SCq.wrote if nb == 0 else SCq.add_write)(tse)
            pbx.read_by(tse)
        if qc + 2 < 16:
            load_q(qc + 2)
        tmk = E(P, "vector", c.s_dve,
                lambda e, o=SCq.ap[:, qc * 128:kr], a=SCq.ap[:, qc * 128:kr],
                b=c.CAUS.ap()[:, :]:
                e.tensor_tensor(o, a, b, OP.add),
                waits=SCq.raw() + c.tok_init)
        SCq.add_write(tmk)
        t1 = E(P, "vector", c.s_dve,
               lambda e, o=RMq.ap[:, 0:1], i=SCq.ap[:, 0:kr]:
               e.reduce_max(o, i, axis=AX), waits=SCq.raw() + RMq.war())
        t2 = E(P, "vector", c.s_dve,
               lambda e, o=RMq.ap[:, 1:2], i=RMq.ap[:, 0:1]:
               e.tensor_scalar(o, i, -1.0 / 32, None, OP.mult))
        RMq.wrote(t2)
        SCq.read_by(t1)
        t3 = E(P, "scalar", c.s_act,
               lambda e, o=PBUF.ap[:, 0:kr], i=SCq.ap[:, 0:kr],
               b=RMq.ap[:, 1:2], a=RMq.ap[:, 2:3]:
               e.activation(o, i, AF.Exp, bias=b, scale=1.0 / 32, accum_out=a),
               waits=(t2,) + PBUF.war() + SCq.raw())
        PBUF.wrote(t3)
        RMq.add_write(t3)
        SCq.read_by(t3)
        t4 = E(P, "vector", c.s_dve,
               lambda e, o=RMq.ap[:, 3:4], i=RMq.ap[:, 2:3]:
               e.reciprocal(o, i), waits=RMq.raw())
        RMq.add_write(t4)
        t5 = E(P, "vector", c.s_dve,
               lambda e, o=PBUF.ap[:, 0:kr], i=PBUF.ap[:, 0:kr],
               s=RMq.ap[:, 3:4]:
               e.tensor_scalar(o, i, s, None, OP.mult),
               waits=PBUF.raw() + (t4,), hard=True)
        PBUF.wrote(t5)
        RMq.read_by(t5)
        for kb in range(qc + 1):
            psb = c.PS[1] if kb % 2 == 0 else c.PS[2]
            pbb = c.PB[1] if kb % 2 == 0 else c.PB[2]
            ttr = E(P, "tensor", c.s_pe,
                    lambda e, o=psb.ap().bitcast(BF16)[:, 0:128],
                    i=PBUF.ap[:, kb * 128:(kb + 1) * 128],
                    ident=c.IDENT.ap()[:, :]:
                    e.transpose(o, i, ident),
                    waits=PBUF.raw() + pbb.war() + c.tok_init)
            pbb.wrote(ttr)
            PBUF.read_by(ttr)
            if kb % 2 == 0:
                tte = E(P, "vector", c.s_dve,
                        lambda e, o=PT.ap[:, kb * 128:(kb + 1) * 128],
                        i=psb.ap().bitcast(BF16)[:, 0:128]:
                        e.tensor_copy(o, i),
                        waits=pbb.raw() + (PT.war() if kb == 0 else ()))
            else:
                tte = E(P, "scalar", c.s_act,
                        lambda e, o=PT.ap[:, kb * 128:(kb + 1) * 128],
                        i=psb.ap().bitcast(BF16)[:, 0:128]:
                        e.activation(o, i, AF.Identity),
                        waits=pbb.raw())
            (PT.wrote if kb == 0 else PT.add_write)(tte)
            pbb.read_by(tte)
        toks = [None, None]
        for mt in range(8):
            psx, pbx = ((c.PS[6], c.PB[6]) if mt < 4 else (c.PS[7], c.PB[7]))
            col = (mt % 4) * 128
            for kb in range(qc + 1):
                tokm = E(P, "tensor", c.s_pe,
                         lambda e, o=psx.ap()[:, col:col + 128],
                         l=VFr.ap[:, kb * 1024 + mt * 128:
                                  kb * 1024 + mt * 128 + 128],
                         r=PT.ap[:, kb * 128:(kb + 1) * 128],
                         s=(kb == 0), p=(kb == qc):
                         e.matmul(o, l, r, start=s, stop=p),
                         waits=(vftoks[kb // 4],) + PT.raw()
                         + (pbx.war() if (kb == 0 and mt % 4 == 0) else ()))
                VFr.read_by(tokm)
                PT.read_by(tokm)
            toks[mt // 4] = tokm
        for half in range(2):
            psx, pbx = ((c.PS[6], c.PB[6]) if half == 0 else (c.PS[7], c.PB[7]))
            pbx.wrote(toks[half])
            cc0 = half * 4
            base = cc0 * T_SEQ + qc * 128
            oview = (c.XRES.ap()[:, base:base + 3 * T_SEQ + 128]
                     .rearrange("p (g f) -> p g f", g=4)
                     [:, :, 0:128] if False else
                     c.XRES.ap()[:, cc0 * T_SEQ + qc * 128:
                                 cc0 * T_SEQ + qc * 128 + 3 * T_SEQ + 128])
            ovw = (c.XRES.ap()[:, cc0 * T_SEQ:(cc0 + 4) * T_SEQ]
                   .rearrange("p (g f) -> p g f", g=4)
                   [:, :, qc * 128:(qc + 1) * 128])
            waits = (pbx.raw()
                     + tuple(t for b in c.xres[cc0:cc0 + 4] for t in b.war()))
            tev = E(P, "vector", c.s_dve,
                    lambda e, o=ovw,
                    a=ovw,
                    b=psx.ap()[:, :].rearrange("p (g f) -> p g f", g=4):
                    e.tensor_tensor(o, a, b, OP.add), waits=waits)
            for b in c.xres[cc0:cc0 + 4]:
                b.add_write(tev)
            pbx.read_by(tev)
    c.AF.reset(f0)
    c.AB.reset(b0)
    set_fence(c)


def phase_l1ffn(c):
    P = c.P
    f0, b0 = c.AF.mark(), c.AB.mark()
    XN = [c.AB.buf(2048) for _ in range(NCC)]
    SQB = c.AB.buf(512)
    TMPB = c.AB.buf(512)
    SGF = Buf(c.AF.take(2048, rows=32))
    SGT = Buf(c.AF.take(256))
    WEF = Buf(c.AB.take(1024, rows=16))
    OH16 = Buf(c.AB.take(2048, rows=16))
    GA = c.AB.buf(4096)
    GB = c.AB.buf(4096)
    UU = c.AB.buf(4096)
    CBp = [c.AB.buf(2048) for _ in range(2)]
    W16B = c.AB.buf(16)
    GRP = Buf(c.AF.take(64))
    M8A = Buf(c.AF.take(8))
    MS16 = Buf(c.AF.take(16))
    RC = Buf(c.AF.take(4))
    for b in (XN + CBp + [SQB, TMPB, SGF, SGT, WEF, OH16, GA, GB, UU, W16B,
                          GRP, M8A, MS16, RC]):
        b.ready = list(c.AB.fence)
    _t = E(P, "sync", c.s_ld, lambda e, d=OH16.ap, s=c.inp["c_oh16"]:
           e.dma_start(d, s), waits=OH16.war() + ((c.s_ld, c.s_ld.val),), k=16)
    OH16.wrote(_t)
    emit_preln(c, 0, T_SEQ, XN, SQB, TMPB)
    for nb in range(4):
        tok = None
        for kc in range(NCC):
            tok = E(P, "tensor", c.s_pe,
                    lambda e, o=c.PS[2].ap()[0:16, :],
                    l=c.GATE1.ap()[:, kc * 16:(kc + 1) * 16],
                    r=XN[kc].ap[:, nb * 512:(nb + 1) * 512],
                    s=(kc == 0), p=(kc == NCC - 1):
                    e.matmul(o, l, r, start=s, stop=p),
                    waits=XN[kc].raw() + (c.PB[2].war() if kc == 0 else ())
                    + c.tok_init)
            XN[kc].read_by(tok)
        c.PB[2].wrote(tok)
        tg = E(P, "scalar", c.s_act,
               lambda e, o=SGF.ap[0:16, nb * 512:(nb + 1) * 512],
               i=c.PS[2].ap()[0:16, :]:
               e.activation(o, i, AF.Sigmoid),
               waits=c.PB[2].raw() + (SGF.war() if nb == 0 else ()))
        (SGF.wrote if nb == 0 else SGF.add_write)(tg)
        c.PB[2].read_by(tg)

    for half in range(2):
        t0 = half * 1024
        for tb in range(8):
            tbg = half * 8 + tb
            ttr = E(P, "tensor", c.s_pe,
                    lambda e, o=c.PS[3].ap()[:, 0:32],
                    i=SGF.ap[0:32, tbg * 128:(tbg + 1) * 128],
                    ident=c.IDENTF.ap()[0:32, 0:32]:
                    e.transpose(o, i, ident),
                    waits=SGF.raw() + c.PB[3].war() + c.tok_init)
            c.PB[3].wrote(ttr)
            SGF.read_by(ttr)
            te = E(P, "vector", c.s_dve,
                   lambda e, o=SGT.ap[:, tb * 16:(tb + 1) * 16],
                   i=c.PS[3].ap()[:, 0:16]:
                   e.tensor_copy(o, i),
                   waits=c.PB[3].raw() + (SGT.war() if tb == 0 else ()))
            (SGT.wrote if tb == 0 else SGT.add_write)(te)
            c.PB[3].read_by(te)
        for tb in range(8):
            Bap = SGT.ap[:, tb * 16:(tb + 1) * 16]
            g3 = GRP.ap
            x_ = [Bap.rearrange("p (g i) -> p g i", i=4)[:, :, i]
                  for i in range(4)]
            ops = [
                ("s1", OP.max, x_[0], x_[1]), ("t1", OP.min, x_[0], x_[1]),
                ("s2", OP.max, x_[2], x_[3]), ("t2", OP.min, x_[2], x_[3]),
            ]
            cols = {}
            off = 0
            tokl = None
            for nm, op, a, b in ops:
                cols[nm] = g3[:, off:off + 4]
                tokl = E(P, "vector", c.s_dve,
                         lambda e, o=cols[nm], a=a, b=b:
                         e.tensor_tensor(o, a, b, op),
                         waits=SGT.raw() + (GRP.war() if off == 0 else ()))
                SGT.read_by(tokl)
                off += 4
            for nm, op, a, b in (("m1", OP.max, cols["s1"], cols["s2"]),
                                 ("ge", OP.min, cols["s1"], cols["s2"]),
                                 ("gf", OP.max, cols["t1"], cols["t2"])):
                cols[nm] = g3[:, off:off + 4]
                E(P, "vector", c.s_dve,
                  lambda e, o=cols[nm], a=a, b=b: e.tensor_tensor(o, a, b, op))
                off += 4
            cols["m2"] = g3[:, off:off + 4]; off += 4
            E(P, "vector", c.s_dve,
              lambda e, o=cols["m2"], a=cols["ge"], b=cols["gf"]:
              e.tensor_tensor(o, a, b, OP.max))
            cols["gsum"] = g3[:, off:off + 4]; off += 4
            tgs = E(P, "vector", c.s_dve,
                    lambda e, o=cols["gsum"], a=cols["m1"], b=cols["m2"]:
                    e.tensor_tensor(o, a, b, OP.add))
            GRP.wrote(tgs)
            gs = cols["gsum"]
            pp = g3[:, off:off + 2]; off += 2
            qq = g3[:, off:off + 2]; off += 2
            m2g = g3[:, off:off + 1]; off += 1
            E(P, "vector", c.s_dve,
              lambda e, o=pp, a=gs[:, 0::2], b=gs[:, 1::2]:
              e.tensor_tensor(o, a, b, OP.max))
            E(P, "vector", c.s_dve,
              lambda e, o=qq, a=gs[:, 0::2], b=gs[:, 1::2]:
              e.tensor_tensor(o, a, b, OP.min))
            E(P, "vector", c.s_dve,
              lambda e, o=m2g, a=pp[:, 0:1], b=pp[:, 1:2]:
              e.tensor_tensor(o, a, b, OP.min))
            E(P, "vector", c.s_dve,
              lambda e, o=qq[:, 0:1], a=qq[:, 0:1], b=qq[:, 1:2]:
              e.tensor_tensor(o, a, b, OP.max))
            tm2g = E(P, "vector", c.s_dve,
                     lambda e, o=m2g, a=m2g, b=qq[:, 0:1]:
                     e.tensor_tensor(o, a, b, OP.max))
            GRP.add_write(tm2g)
            cols["ag"] = g3[:, off:off + 4]; off += 4
            tag = E(P, "vector", c.s_dve,
                    lambda e, o=cols["ag"], i=cols["gsum"], s=m2g:
                    e.tensor_single_scalar(o, i, s, OP.is_ge),
                    hard=True)
            GRP.add_write(tag)
            tms = None
            for g in range(4):
                tp1 = E(P, "vector", c.s_dve,
                        lambda e, o=MS16.ap[:, g * 4:(g + 1) * 4],
                        i=Bap[:, g * 4:(g + 1) * 4]:
                        e.tensor_scalar(o, i, 1e9, None, OP.add),
                        waits=SGT.raw() + (MS16.war() if g == 0 else ()))
                MS16.add_write(tp1)
                SGT.read_by(tp1)
                tms = E(P, "vector", c.s_dve,
                        lambda e, o=MS16.ap[:, g * 4:(g + 1) * 4],
                        i=MS16.ap[:, g * 4:(g + 1) * 4],
                        s=cols["ag"][:, g:g + 1]:
                        e.tensor_scalar(o, i, s, -1e9, OP.mult, OP.add),
                        waits=GRP.raw(), hard=True)
                MS16.add_write(tms)
                GRP.read_by(tms)
            tmi = E(P, "vector", c.s_dve,
                    lambda e, o=MS16.ap, a=MS16.ap, b=c.IOTAEPS.ap()[:, :]:
                    e.tensor_tensor(o, a, b, OP.add), waits=c.tok_init)
            MS16.wrote(tmi)
            tm84 = E(P, "vector", c.s_dve,
                     lambda e, o=M8A.ap, i=MS16.ap: e.max(o, i),
                     waits=M8A.war())
            M8A.wrote(tm84)
            MS16.read_by(tm84)
            ttk = E(P, "vector", c.s_dve,
                    lambda e, o=MS16.ap, i=MS16.ap, s=M8A.ap[:, 3:4]:
                    e.tensor_single_scalar(o, i, s, OP.is_ge),
                    waits=M8A.raw(), hard=True)
            MS16.wrote(ttk)
            M8A.read_by(ttk)
            tws = E(P, "vector", c.s_dve,
                    lambda e, o=MS16.ap, a=SGT.ap[:, tb * 16:(tb + 1) * 16],
                    b=MS16.ap:
                    e.tensor_tensor(o, a, b, OP.mult))
            MS16.wrote(tws)
            SGT.read_by(tws)
            trs = E(P, "vector", c.s_dve,
                    lambda e, o=RC.ap[:, 0:1], i=MS16.ap:
                    e.tensor_reduce(o, i, AX, OP.add), waits=RC.war())
            trs2 = E(P, "vector", c.s_dve,
                     lambda e, o=RC.ap[:, 1:2], i=RC.ap[:, 0:1]:
                     e.tensor_scalar(o, i, 1e-6, None, OP.add))
            trs3 = E(P, "vector", c.s_dve,
                     lambda e, o=RC.ap[:, 2:3], i=RC.ap[:, 1:2]:
                     e.reciprocal(o, i))
            RC.wrote(trs3)
            tw16 = E(P, "vector", c.s_dve,
                     lambda e, o=W16B.ap, i=MS16.ap, s=RC.ap[:, 2:3]:
                     e.tensor_scalar(o, i, s, None, OP.mult),
                     waits=W16B.war(), hard=True)
            W16B.wrote(tw16)
            MS16.read_by(tw16)
            RC.read_by(tw16)
            twt = E(P, "tensor", c.s_pe,
                    lambda e, o=c.PS[3].ap().bitcast(BF16)[0:16, 0:128],
                    i=W16B.ap, ident=c.IDENT.ap()[:, :]:
                    e.transpose(o, i, ident),
                    waits=W16B.raw() + c.PB[3].war())
            c.PB[3].wrote(twt)
            W16B.read_by(twt)
            twe = E(P, "vector", c.s_dve,
                    lambda e, o=WEF.ap[0:16, tb * 128:(tb + 1) * 128],
                    i=c.PS[3].ap().bitcast(BF16)[0:16, 0:128]:
                    e.tensor_copy(o, i),
                    waits=c.PB[3].raw() + (WEF.war() if tb == 0 else ()))
            (WEF.wrote if tb == 0 else WEF.add_write)(twe)
            c.PB[3].read_by(twe)

        def gu_compute(gbuf, w1s, w3s, p1, p3, cb, kwidth):
            for wname, prol, act in ((w1s, p1, True), (w3s, p3, False)):
                for sup in range(2):
                    wb = c.WRA.load(wname[:, sup * 256:(sup + 1) * 256]
                                    .rearrange("(kc p) m -> p kc m", p=128),
                                    cols=2048, extra=(prol,))
                    for mt in range(2):
                        fc = sup * 2 + mt
                        for nb in range(2):
                            psx, pbx = ((c.PS[4], c.PB[4]) if nb == 0
                                        else (c.PS[5], c.PB[5]))
                            tokm = None
                            for kc in range(NCC):
                                tokm = E(P, "tensor", c.s_pe,
                                         lambda e, o=psx.ap()[:, :],
                                         l=wb.ap[:, kc * 256 + mt * 128:
                                                 kc * 256 + mt * 128 + 128],
                                         r=XN[kc].ap[:, t0 + nb * 512:
                                                     t0 + (nb + 1) * 512],
                                         s=(kc == 0), p=(kc == NCC - 1):
                                         e.matmul(o, l, r, start=s, stop=p),
                                         waits=wb.raw() + XN[kc].raw()
                                         + (pbx.war() if kc == 0 else ()))
                                wb.read_by(tokm)
                                XN[kc].read_by(tokm)
                            pbx.wrote(tokm)
                            if act:
                                tev = E(P, "scalar", c.s_act,
                                        lambda e,
                                        o=gbuf.ap[:, fc * 1024 + nb * 512:
                                                  fc * 1024 + (nb + 1) * 512],
                                        i=psx.ap()[:, :]:
                                        e.activation(o, i, AF.Silu),
                                        waits=pbx.raw()
                                        + (gbuf.war() if fc == 0 and nb == 0
                                           else ()))
                                (gbuf.wrote if (fc == 0 and nb == 0)
                                 else gbuf.add_write)(tev)
                                pbx.read_by(tev)
                            else:
                                tev = E(P, "scalar", c.s_act,
                                        lambda e,
                                        o=UU.ap[:, fc * 1024 + nb * 512:
                                                fc * 1024 + (nb + 1) * 512],
                                        i=psx.ap()[:, :]:
                                        e.activation(o, i, AF.Identity),
                                        waits=pbx.raw()
                                        + (UU.war() if fc == 0 and nb == 0
                                           else ()))
                                (UU.wrote if (fc == 0 and nb == 0)
                                 else UU.add_write)(tev)
                                pbx.read_by(tev)
            for fc in range(4):
                tgu = E(P, "vector", c.s_dve,
                        lambda e, o=gbuf.ap[:, fc * 1024:(fc + 1) * 1024],
                        a=gbuf.ap[:, fc * 1024:(fc + 1) * 1024],
                        b=UU.ap[:, fc * 1024:(fc + 1) * 1024]:
                        e.tensor_tensor(o, a, b, OP.mult),
                        waits=gbuf.raw() + UU.raw())
                gbuf.add_write(tgu)
                UU.read_by(tgu)
                if cb is not None:
                    tcb = E(P, "vector", c.s_dve,
                            lambda e, o=gbuf.ap[:, fc * 1024:(fc + 1) * 1024],
                            a=gbuf.ap[:, fc * 1024:(fc + 1) * 1024],
                            b=cb.ap[:, 0:1024]:
                            e.tensor_tensor(o, a, b, OP.mult),
                            waits=cb.raw())
                    gbuf.add_write(tcb)
                    cb.read_by(tcb)

        def build_cb(e_idx, cb):
            for nb in range(2):
                tcm = E(P, "tensor", c.s_pe,
                        lambda e, o=c.PS[0].ap()[:, :],
                        l=OH16.ap[0:16, e_idx * 128:(e_idx + 1) * 128],
                        r=WEF.ap[0:16, nb * 512:(nb + 1) * 512]:
                        e.matmul(o, l, r, start=True, stop=True),
                        waits=OH16.raw() + WEF.raw() + c.PB[0].war())
                c.PB[0].wrote(tcm)
                OH16.read_by(tcm)
                WEF.read_by(tcm)
                tce = E(P, "scalar", c.s_act,
                        lambda e, o=cb.ap[:, nb * 512:(nb + 1) * 512],
                        i=c.PS[0].ap()[:, :]:
                        e.activation(o, i, AF.Identity),
                        waits=c.PB[0].raw() + (cb.war() if nb == 0 else ()))
                (cb.wrote if nb == 0 else cb.add_write)(tce)
                c.PB[0].read_by(tce)

        for pr in range(9):
            if pr < 8:
                members = []
                for m in range(2):
                    e_idx = pr * 2 + m
                    cb = CBp[m]
                    build_cb(e_idx, cb)
                    gbuf = GA if m == 0 else GB
                    gu_compute(gbuf, c.inp["w_1w1e"][e_idx],
                               c.inp["w_1w3e"][e_idx],
                               None, None, cb, 512)
                    members.append((gbuf, c.inp["w_1w2e"][e_idx],
                                    None))
            else:
                gu_compute(GA, c.inp["w_1w1s"][:, 0:512],
                           c.inp["w_1w3s"][:, 0:512],
                           None, None, None, 512)
                gu_compute(GB, c.inp["w_1w1s"][:, 512:1024],
                           c.inp["w_1w3s"][:, 512:1024],
                           None, None, None, 512)
                members = [(GA, c.inp["w_1w2s"][0:512, :], None),
                           (GB, c.inp["w_1w2s"][512:1024, :],
                            None)]
            for sup in range(4):
                wbs = []
                for m, (gbuf, w2s, p2) in enumerate(members):
                    wbs.append(c.WRA.load(
                        w2s[:, sup * 256:(sup + 1) * 256]
                        .rearrange("(kc p) m -> p kc m", p=128),
                        cols=1024, extra=(p2,)))
                for mt in range(2):
                    mc = sup * 2 + mt
                    for nb in range(2):
                        psx, pbx = ((c.PS[6], c.PB[6]) if nb == 0
                                    else (c.PS[7], c.PB[7]))
                        tokm = None
                        nmm = 0
                        for m, (gbuf, w2s, p2) in enumerate(members):
                            wb2 = wbs[m]
                            for kc in range(4):
                                tokm = E(P, "tensor", c.s_pe,
                                         lambda e, o=psx.ap()[:, :],
                                         l=wb2.ap[:, kc * 256 + mt * 128:
                                                  kc * 256 + mt * 128 + 128],
                                         r=gbuf.ap[:, kc * 1024 + nb * 512:
                                                   kc * 1024 + (nb + 1) * 512],
                                         s=(nmm == 0), p=(nmm == 7):
                                         e.matmul(o, l, r, start=s, stop=p),
                                         waits=wb2.raw() + gbuf.raw()
                                         + (pbx.war() if nmm == 0 else ()))
                                wb2.read_by(tokm)
                                gbuf.read_by(tokm)
                                nmm += 1
                        pbx.wrote(tokm)
                        xap = c.XRES.ap()[:, mc * T_SEQ + t0 + nb * 512:
                                          mc * T_SEQ + t0 + (nb + 1) * 512]
                        tev = E(P, "vector", c.s_dve,
                                lambda e, o=xap, a=psx.ap()[:, :], b=xap:
                                e.scalar_tensor_tensor(o, a, 0.0, b,
                                                       OP.add, OP.add),
                                waits=pbx.raw() + c.xres[mc].war())
                        c.xres[mc].add_write(tev)
                        pbx.read_by(tev)
    c.AF.reset(f0)
    c.AB.reset(b0)
    set_fence(c)


# version 11 (final)
# speedup vs baseline: 1.2922x; 1.2922x over previous
import numpy as np
import ml_dtypes
import concourse.bass as bass
import concourse.mybir as mybir
from concourse.bass_utils import run_bass_kernel_spmd

F32 = mybir.dt.float32
BF16 = mybir.dt.bfloat16
AF = mybir.ActivationFunctionType
OP = mybir.AluOpType
AX = mybir.AxisListType.X

ENGINES = ("sync", "tensor", "vector", "scalar", "gpsimd")
T_SEQ = 2048
NCC = 8
NB = 2
TT0 = 512
CH = 128
NCH0 = TT0 // CH
NT0 = T_SEQ // TT0
LN_EPS = 1e-5
GN_EPS = 6.4e-4


class Sem:
    __slots__ = ("name", "val", "h", "owner")

    def __init__(self, name, owner=None):
        self.name, self.val, self.h, self.owner = name, 0, None, owner


class Prog:
    def __init__(self, nc):
        self.nc = nc
        self.stream = {e: [] for e in ENGINES}
        self.sems = []
        self.last_wait = {e: {} for e in ENGINES}

    def sem(self, name, owner=None):
        s = Sem(name, owner)
        self.sems.append(s)
        return s

    def on(self, eng, fn):
        self.stream[eng].append(fn)

    def wait(self, eng, *tokens, hard=False):
        need = {}
        for t in tokens:
            if not t:
                continue
            sem, val = t
            if sem.owner == eng and not hard:
                continue
            if not globals().get("NO_DEDUP") and val <= self.last_wait[eng].get(sem, 0):
                continue
            need[sem] = max(need.get(sem, 0), val)
        for sem, val in need.items():
            self.last_wait[eng][sem] = val
            self.on(eng, lambda e, s=sem, v=val: e.wait_ge(s.h, v))

    def finalize(self):
        nc = self.nc
        for s in self.sems:
            s.h = nc.alloc_semaphore(s.name)
        with nc.Block() as block:
            for e in ENGINES:
                ops = self.stream[e]

                def runner(engine, ops=ops):
                    for fn in ops:
                        fn(engine)

                getattr(block, e)(runner)


class Buf:
    __slots__ = ("ap", "ready", "readers", "aliases")

    def __init__(self, ap=None):
        self.ap = ap
        self.ready = []
        self.readers = []
        self.aliases = []

    def war(self):
        t = tuple(self.readers) + tuple(self.ready)
        for a in self.aliases:
            t += tuple(a.readers) + tuple(a.ready)
        return t

    def wrote(self, tok):
        self.ready = [tok] if tok else []
        self.readers = []

    def add_write(self, tok):
        if tok:
            self.ready.append(tok)

    def raw(self):
        return tuple(self.ready)

    def read_by(self, tok):
        if tok:
            self.readers.append(tok)


def alias(*bufs):
    for b in bufs:
        b.aliases = [x for x in bufs if x is not b]


class Carver:
    def __init__(self, tensor, ncols, name):
        self.t, self.n, self.off, self.name = tensor, ncols, 0, name

    def take(self, cols, rows=128):
        assert self.off + cols <= self.n, (
            f"{self.name}: overflow {self.off}+{cols}>{self.n}")
        ap = self.t.ap()[0:rows, self.off:self.off + cols]
        self.off += cols
        return ap

    def buf(self, cols, rows=128):
        b = Buf(self.take(cols, rows))
        b.ready = list(getattr(self, "fence", ()))
        return b

    def mark(self):
        return self.off

    def reset(self, off):
        self.off = off


def _inc(P, sem, k):
    sem.val += k
    return (sem, sem.val)


SIM_SAFE = True
_SELFWAIT_ENGINES = ("vector", "scalar", "gpsimd")


def E(P, eng, sem, build, waits=(), k=1, hard=False):
    if SIM_SAFE and eng in _SELFWAIT_ENGINES and k == 1:
        waits = tuple(waits) + ((sem, sem.val),)
        hard = True
    P.wait(eng, *waits, hard=hard)
    tok = _inc(P, sem, k)
    P.on(eng, lambda e, b=build, s=sem, kk=k: b(e).then_inc(s.h, kk))
    return tok


def _pc(v):
    return np.ascontiguousarray(np.asarray(v, np.float32).reshape(8, 128).T)


def _stage_host(inp):
    g = {}
    g["c0_maa_x"] = _pc(inp["L0_attn_time_maa_x"])
    g["c0_maa_w"] = _pc(inp["L0_attn_time_maa_w"])
    g["c0_maa_k"] = _pc(inp["L0_attn_time_maa_k"])
    g["c0_maa_v"] = _pc(inp["L0_attn_time_maa_v"])
    g["c0_maa_r"] = _pc(inp["L0_attn_time_maa_r"])
    g["c0_maa_g"] = _pc(inp["L0_attn_time_maa_g"])
    g["c0_decay"] = _pc(inp["L0_attn_time_decay"])
    g["c0_ln_w"] = _pc(inp["L0_attn_ln_w"])
    g["c0_ln_b"] = _pc(inp["L0_attn_ln_b"])
    g["c0_u"] = _pc(np.asarray(inp["L0_attn_time_faaaa"], np.float32).reshape(1024))
    w1 = np.asarray(inp["L0_attn_time_maa_w1"], np.float32)
    g["w_maa1"] = np.ascontiguousarray(
        w1.reshape(8, 128, 160).transpose(1, 0, 2).reshape(128, 8 * 160))
    w2r = np.asarray(inp["L0_attn_time_maa_w2"], np.float32)
    w2p = np.zeros((5, 128, 1024), np.float32)
    for s in range(5):
        r0 = 32 * s if s < 4 else 0
        w2p[s, r0:r0 + 32] = w2r[s]
    g["w_maa2"] = np.ascontiguousarray(w2p.reshape(640, 1024))
    d1 = np.asarray(inp["L0_attn_time_decay_w1"], np.float32)
    g["w_dw1"] = np.ascontiguousarray(
        d1.reshape(8, 128, 64).transpose(1, 0, 2).reshape(128, 8 * 64))
    g["w_dw2"] = np.asarray(inp["L0_attn_time_decay_w2"], np.float32)
    for nm in ("Wr", "Wk", "Wv", "Wg", "Wo"):
        g["w_" + nm] = np.asarray(inp["L0_attn_" + nm], np.float32)
    gw = np.asarray(inp["L0_ffn_gate_w"], np.float32)
    g["w_gate0"] = np.ascontiguousarray(
        gw.reshape(8, 128, 8).transpose(1, 0, 2).reshape(128, 64))
    g["c0_gate_b"] = np.asarray(inp["L0_ffn_gate_b"], np.float32).reshape(8, 1)
    g["c0_shb1"] = np.ascontiguousarray(
        np.asarray(inp["L0_ffn_sh_b1"], np.float32).reshape(32, 128).T)
    g["c0_shb2"] = _pc(inp["L0_ffn_sh_b2"])
    eb1 = np.asarray(inp["L0_ffn_eb1"], np.float32)
    g["c0_eb1"] = np.ascontiguousarray(
        eb1.reshape(8, 32, 128).transpose(2, 0, 1).reshape(128, 256))
    eb2 = np.asarray(inp["L0_ffn_eb2"], np.float32)
    g["c0_eb2"] = np.ascontiguousarray(
        eb2.reshape(8, 8, 128).transpose(2, 0, 1).reshape(128, 64))
    g["w_shw1"] = np.asarray(inp["L0_ffn_sh_w1"], np.float32)
    g["w_shw2"] = np.asarray(inp["L0_ffn_sh_w2"], np.float32)
    g["w_ew1"] = np.asarray(inp["L0_ffn_ew1"], np.float32)
    g["w_ew2"] = np.asarray(inp["L0_ffn_ew2"], np.float32)
    g["w_1Wq"] = np.ascontiguousarray(np.asarray(inp["L1_attn_Wq"], np.float32).T)
    g["w_1Wk"] = np.ascontiguousarray(np.asarray(inp["L1_attn_Wk"], np.float32).T)
    g["w_1Wv"] = np.ascontiguousarray(np.asarray(inp["L1_attn_Wv"], np.float32).T)
    g["c1_bq"] = _pc(inp["L1_attn_bq"])
    g["c1_bk"] = _pc(inp["L1_attn_bk"])
    g["c1_bv_row"] = np.asarray(inp["L1_attn_bv"], np.float32).reshape(1, 1024)
    wg1 = np.asarray(inp["L1_ffn_w_gate"], np.float32)
    g["w_gate1"] = np.ascontiguousarray(
        wg1.T.reshape(8, 128, 16).transpose(1, 0, 2).reshape(128, 128))
    g["w_1w1e"] = np.ascontiguousarray(
        np.asarray(inp["L1_ffn_w1_e"], np.float32).transpose(0, 2, 1))
    g["w_1w3e"] = np.ascontiguousarray(
        np.asarray(inp["L1_ffn_w3_e"], np.float32).transpose(0, 2, 1))
    g["w_1w2e"] = np.ascontiguousarray(
        np.asarray(inp["L1_ffn_w2_e"], np.float32).transpose(0, 2, 1))
    g["w_1w1s"] = np.ascontiguousarray(np.asarray(inp["L1_ffn_w1_s"], np.float32).T)
    g["w_1w3s"] = np.ascontiguousarray(np.asarray(inp["L1_ffn_w3_s"], np.float32).T)
    g["w_1w2s"] = np.ascontiguousarray(np.asarray(inp["L1_ffn_w2_s"], np.float32).T)
    g["c_ident"] = np.eye(128, dtype=np.float32)
    lt = (np.arange(128)[:, None] < np.arange(128)[None, :]).astype(np.float32)
    g["c_ltm"] = np.ascontiguousarray(np.tile(lt, (1, 4)))
    g["c_caus"] = np.where(np.arange(128)[None, :] <= np.arange(128)[:, None],
                           0.0, -1e30).astype(np.float32)
    oh8 = np.zeros((8, 1024), np.float32)
    for e in range(8):
        oh8[e, e * 128:(e + 1) * 128] = 1.0
    g["c_oh8"] = oh8
    oh16 = np.zeros((16, 2048), np.float32)
    for e in range(16):
        oh16[e, e * 128:(e + 1) * 128] = 1.0
    g["c_oh16"] = oh16
    ind = np.zeros((128, 128), np.float32)
    for cc in range(8):
        for p in range(128):
            ind[p, cc * 16 + 2 * cc + p // 64] = 1.0
    g["c_ind16"] = ind
    indt = np.zeros((16, 1024), np.float32)
    for cc in range(8):
        for p in range(128):
            indt[2 * cc + p // 64, cc * 128 + p] = 1.0
    g["c_ind16t"] = indt
    g["c_iotaeps"] = np.tile((-1e-7 * np.arange(16, dtype=np.float32))[None, :],
                             (128, 1))
    g["c_identf"] = np.eye(128, dtype=np.float32)
    f32names = ("c0_maa_x", "c0_maa_w", "c0_maa_k", "c0_maa_v", "c0_maa_r",
                "c0_maa_g", "c0_decay", "c0_ln_w", "c0_ln_b", "c0_u",
                "c0_shb1", "c0_shb2", "c0_eb1", "c0_eb2", "c1_bq", "c1_bk",
                "c0_gate_b", "c1_bv_row", "c_iotaeps", "c_identf")
    for nm in ("c_ind16", "c_ind16t", "c_oh8", "c_oh16"):
        g[nm] = g[nm].astype(ml_dtypes.bfloat16)
    bfnames = ("c_ident", "c_ltm", "c_caus", "w_maa1", "w_dw1", "w_dw2",
               "w_gate0", "w_gate1")
    def cat(names):
        parts, offs, off = [], {}, 0
        for nm in names:
            a = np.asarray(g[nm], np.float32)
            pad = np.zeros((128, a.shape[1]), np.float32)
            pad[0:a.shape[0]] = a
            parts.append(pad)
            offs[nm] = (off, a.shape[1], a.shape[0])
            off += a.shape[1]
        return np.ascontiguousarray(np.concatenate(parts, 1)), offs
    g["c_init32"], g["_offs32"] = cat(f32names)
    g["c_initbf"], g["_offsbf"] = cat(bfnames)
    return g


INIT_OFFS = {}


def stage_inputs_per_core(inputs):
    shared = _stage_host(inputs)
    INIT_OFFS["f32"] = shared.pop("_offs32")
    INIT_OFFS["bf"] = shared.pop("_offsbf")
    x = np.asarray(inputs["L0_attn_x"], np.float32)
    percore = []
    for i in range(8):
        d = dict(shared)
        d["x"] = np.ascontiguousarray(x[2 * i:2 * i + 2].transpose(0, 2, 1))
        percore.append(d)
    return percore


class Ctx:
    pass


PROLOGUE_ORDER = (
    "w_maa2", "w_Wr", "w_Wk", "w_Wv", "w_Wg", "w_Wo",
    "w_shw1", "w_shw2", "w_ew1", "w_ew2",
    "w_1Wq", "w_1Wk", "w_1Wv",
    "w_1w1s", "w_1w3s", "w_1w2s", "w_1w1e", "w_1w3e", "w_1w2e",
)


def make_ctx(nc, stage_shapes, n_blocks=4):
    c = Ctx()
    c.nc = nc
    c.n_blocks = n_blocks
    P = c.P = Prog(nc)
    c.inp = {}
    for name, shp in stage_shapes.items():
        dt = BF16 if (hasattr(shp, "dtype") or name in
                      ("c_ind16", "c_ind16t", "c_oh8", "c_oh16")) else F32
        c.inp[name] = nc.dram_tensor(name, list(shp), dt,
                                     kind="ExternalInput").ap()
    c.out = nc.dram_tensor("y", [NB, 1024, T_SEQ], F32, kind="ExternalOutput").ap()
    c.scr = {n: nc.dram_tensor("s" + n, list(s), BF16, kind="Internal").ap()
             for n, s in stage_shapes.items() if n.startswith("w_")}
    c.KS = nc.dram_tensor("KS", [1024, T_SEQ], BF16, kind="Internal").ap()
    c.VS = nc.dram_tensor("VS", [T_SEQ, 1024], BF16, kind="Internal").ap()
    c.QS = nc.dram_tensor("QS", [1024, T_SEQ], BF16, kind="Internal").ap()

    c.s_pe = P.sem("s_pe", "tensor")
    c.s_dve = P.sem("s_dve", "vector")
    c.s_act = P.sem("s_act", "scalar")
    c.s_pool = P.sem("s_pool", "gpsimd")
    c.s_gd = P.sem("s_gd")
    c.s_ld = P.sem("s_ld")
    c.s_out = P.sem("s_out")
    c.s_ws = [P.sem(f"s_ws{i}") for i in range(6)]
    c.s_x = P.sem("s_x")

    class APView:
        def __init__(self, ap_):
            self._ap = ap_

        def ap(self):
            return self._ap

    c.APView = APView
    c.XRES = nc.alloc_sbuf_tensor("XRES", [128, NCC * T_SEQ], F32)
    c.S = nc.alloc_sbuf_tensor("Swkv", [128, 512], F32)
    c.ONESC = nc.alloc_sbuf_tensor("ONESC", [128, 1], F32)
    c.ONESCB = nc.alloc_sbuf_tensor("ONESCB", [128, 1], BF16)
    c.ONESR = nc.alloc_sbuf_tensor("ONESR", [1, 128], F32)
    c.ZERO = nc.alloc_sbuf_tensor("ZEROt", [128, 128], F32)
    c.EPSLN = nc.alloc_sbuf_tensor("EPSLN", [128, 1], F32)
    c.DUMMY = nc.alloc_sbuf_tensor("DUMMYT", [128, 1], F32)
    c.EPSGN = nc.alloc_sbuf_tensor("EPSGN", [128, 1], F32)
    o32, obf = INIT_OFFS["f32"], INIT_OFFS["bf"]
    c.INIT32 = nc.alloc_sbuf_tensor("INIT32", [128, stage_shapes["c_init32"][1]],
                                    F32)
    c.INITBF = nc.alloc_sbuf_tensor("INITBF", [128, stage_shapes["c_initbf"][1]],
                                    BF16)

    def v32(nm):
        off, w, rows = o32[nm]
        return APView(c.INIT32.ap()[0:rows, off:off + w])

    def vbf(nm):
        off, w, rows = obf[nm]
        return APView(c.INITBF.ap()[0:rows, off:off + w])

    c.cols = {nm: v32(nm) for nm in
              ("c0_maa_x", "c0_maa_w", "c0_maa_k", "c0_maa_v", "c0_maa_r",
               "c0_maa_g", "c0_decay", "c0_ln_w", "c0_ln_b", "c0_u",
               "c0_shb1", "c0_shb2", "c0_eb1", "c0_eb2", "c1_bq", "c1_bk")}
    c.GATEB0 = v32("c0_gate_b")
    c.BVROW = v32("c1_bv_row")
    c.IOTAEPS = v32("c_iotaeps")
    c.IDENTF = v32("c_identf")
    c.IDENT = vbf("c_ident")
    c.LTM = vbf("c_ltm")
    c.CAUS = vbf("c_caus")
    c.MAA1 = vbf("w_maa1")
    c.DW1 = vbf("w_dw1")
    c.DW2 = vbf("w_dw2")
    c.GATE0 = vbf("w_gate0")
    c.GATE1 = vbf("w_gate1")


    c.AF = Carver(nc.alloc_sbuf_tensor("AF32", [128, 8448], F32), 8448, "AF32")
    c.AB = Carver(nc.alloc_sbuf_tensor("ABF", [128, 39808], BF16), 39808, "ABF")
    c.ROWS = [Buf(c.AF.take(512, rows=1)) for _ in range(4)]
    c._row_i = 0

    c.PS = [nc.alloc_psum_tensor(f"pb{i}", [128, 512], F32) for i in range(8)]
    c.PB = [Buf(t.ap()) for t in c.PS]
    c.xres = [Buf(c.XRES.ap()[:, cc * T_SEQ:(cc + 1) * T_SEQ]) for cc in range(NCC)]
    c.sbuf = Buf(c.S.ap())

    _init_consts(c)
    return c


def row(c):
    b = c.ROWS[c._row_i % len(c.ROWS)]
    c._row_i += 1
    return b


def _init_consts(c):
    P = c.P
    for t, v in ((c.ONESC, 1.0), (c.ONESCB, 1.0), (c.ONESR, 1.0), (c.ZERO, 0.0),
                 (c.EPSLN, LN_EPS), (c.EPSGN, GN_EPS)):
        E(P, "gpsimd", c.s_pool, lambda e, tt=t, vv=v: e.memset(tt.ap()[:, :], vv))

    E(P, "sync", c.s_ld, lambda e, d=c.INIT32.ap()[:, :],
      s=c.inp["c_init32"]: e.dma_start(d, s), k=16)
    E(P, "gpsimd", c.s_gd, lambda e, d=c.INITBF.ap()[:, :],
      s=c.inp["c_initbf"]: e.dma_start(d, s), k=16)
    c.tok_init = ((c.s_ld, c.s_ld.val), (c.s_gd, c.s_gd.val),
                  (c.s_pool, c.s_pool.val))


def build_prologue(c):
    P = c.P
    c.prol = {}
    for i, name in enumerate(PROLOGUE_ORDER):
        s_w = P.sem("s_prol_" + name)
        w = ((c.s_act, 1), (c.s_x, 16)) if i >= 6 else ()
        tok = E(P, "gpsimd", s_w,
                lambda e, s=c.inp[name].flatten().unsqueeze(0),
                d=c.scr[name].flatten().unsqueeze(0):
                e.dma_start(d, s), waits=w, k=16)
        c.prol[name] = tok


class WRing:

    def __init__(self, c, carver, cols, nbuf=2, slot0=0):
        self.c = c
        self.bufs = [carver.buf(cols) for _ in range(nbuf)]
        self.sems = [c.s_ws[slot0 + i] for i in range(nbuf)]
        self.i = 0

    def load(self, src_ap, rows=128, cols=None, extra=(), row0=0):
        c = self.c
        b = self.bufs[self.i % len(self.bufs)]
        sem = self.sems[self.i % len(self.bufs)]
        self.i += 1
        dap = b.ap if cols is None else b.ap[row0:row0 + rows, 0:cols]
        if len(src_ap.shape) == 3:
            dap = dap.rearrange("p (a b) -> p a b", a=src_ap.shape[1])
        tok = E(c.P, "sync", sem, lambda e, d=dap, s=src_ap: e.dma_start(d, s),
                waits=b.war() + tuple(extra) + ((sem, sem.val),), k=16)
        b.wrote(tok)
        return b


def emit_preln(c, t0, width, out_tiles, sq_buf, tmp_buf):
    P = c.P
    for blk in range(width // 512):
        tb = t0 + blk * 512
        pbs, pbq = c.PB[2], c.PB[3]
        tok = tok2 = None
        for cc in range(NCC):
            xap = c.XRES.ap()[:, cc * T_SEQ + tb: cc * T_SEQ + tb + 512]
            tsq = E(P, "scalar", c.s_act,
                    lambda e, o=sq_buf.ap, i=xap: e.activation(o, i, AF.Square),
                    waits=c.xres[cc].raw() + sq_buf.war())
            sq_buf.wrote(tsq)
            tok = E(P, "tensor", c.s_pe,
                    lambda e, o=c.PS[2].ap()[0:1, :], l=c.ONESC.ap()[:, :],
                    r=xap, s=(cc == 0), p=(cc == NCC - 1):
                    e.matmul(o, l, r, start=s, stop=p),
                    waits=c.xres[cc].raw() + (pbs.war() if cc == 0 else ()))
            c.xres[cc].read_by(tok)
            tok2 = E(P, "tensor", c.s_pe,
                     lambda e, o=c.PS[3].ap()[0:1, :], l=c.ONESCB.ap()[:, :],
                     r=sq_buf.ap, s=(cc == 0), p=(cc == NCC - 1):
                     e.matmul(o, l, r, start=s, stop=p),
                     waits=sq_buf.raw() + (pbq.war() if cc == 0 else ()))
            sq_buf.read_by(tok2)
        pbs.wrote(tok)
        pbq.wrote(tok2)
        mr, msq, sd, rst = row(c), row(c), row(c), row(c)
        t1 = E(P, "vector", c.s_dve, lambda e, o=mr.ap, i=c.PS[2].ap()[0:1, :]:
               e.tensor_scalar(o, i, 1.0 / 1024, None, OP.mult),
               waits=pbs.raw() + mr.war())
        pbs.read_by(t1)
        mr.wrote(t1)
        t2 = E(P, "vector", c.s_dve, lambda e, o=msq.ap, i=c.PS[3].ap()[0:1, :]:
               e.tensor_scalar(o, i, 1.0 / 1024, None, OP.mult),
               waits=pbq.raw() + msq.war())
        pbq.read_by(t2)
        msq.wrote(t2)
        t3 = E(P, "vector", c.s_dve, lambda e, o=sd.ap, a=mr.ap, b=mr.ap:
               e.tensor_tensor(o, a, b, OP.mult), waits=sd.war())
        t4 = E(P, "vector", c.s_dve, lambda e, o=msq.ap, a=msq.ap, b=sd.ap:
               e.tensor_tensor(o, a, b, OP.subtract))
        t4 = E(P, "vector", c.s_dve, lambda e, o=msq.ap, i=msq.ap:
               e.tensor_scalar(o, i, 0.0, None, OP.max))
        msq.wrote(t4)
        t5 = E(P, "scalar", c.s_act, lambda e, o=sd.ap, i=msq.ap:
               e.activation(o, i, AF.Sqrt, bias=c.EPSLN.ap()[0:1, :]),
               waits=(t4,) + c.tok_init)
        sd.wrote(t5)
        msq.read_by(t5)
        t6 = E(P, "vector", c.s_dve, lambda e, o=rst.ap, i=sd.ap:
               e.reciprocal(o, i), waits=(t5,) + rst.war())
        rst.wrote(t6)
        sd.read_by(t6)
        bm = E(P, "tensor", c.s_pe,
               lambda e, o=c.PS[2].ap()[:, :], l=c.ONESR.ap()[:, :], r=mr.ap:
               e.matmul(o, l, r, start=True, stop=True),
               waits=(t1,) + pbs.war())
        pbs.wrote(bm)
        mr.read_by(bm)
        br = E(P, "tensor", c.s_pe,
               lambda e, o=c.PS[3].ap()[:, :], l=c.ONESR.ap()[:, :], r=rst.ap:
               e.matmul(o, l, r, start=True, stop=True),
               waits=(t6,) + pbq.war())
        pbq.wrote(br)
        rst.read_by(br)
        for cc in range(NCC):
            xap = c.XRES.ap()[:, cc * T_SEQ + tb: cc * T_SEQ + tb + 512]
            ta = E(P, "vector", c.s_dve,
                   lambda e, o=tmp_buf.ap, a=xap, b=c.PS[2].ap()[:, :]:
                   e.tensor_tensor(o, a, b, OP.subtract),
                   waits=pbs.raw() + c.xres[cc].raw() + tmp_buf.war())
            pbs.read_by(ta)
            c.xres[cc].read_by(ta)
            tmp_buf.wrote(ta)
            oap = out_tiles[cc].ap[:, blk * 512:blk * 512 + 512]
            tz = E(P, "vector", c.s_dve,
                   lambda e, o=oap, a=tmp_buf.ap, b=c.PS[3].ap()[:, :]:
                   e.tensor_tensor(o, a, b, OP.mult),
                   waits=pbq.raw() + out_tiles[cc].war())
            pbq.read_by(tz)
            tmp_buf.read_by(tz)
            if blk == 0:
                out_tiles[cc].wrote(tz)
            else:
                out_tiles[cc].add_write(tz)


def pib(PIBUF, mc, ch, off, w=1):
    base = (mc * NCH0 + ch) * 130
    return PIBUF.ap[:, base + off: base + off + w]


def pib_tab(PIBUF, ch, off):
    v = PIBUF.ap.rearrange("p (mc ch w) -> p mc ch w", mc=NCC, w=130)
    return v[:, :, ch:ch + 1, off:off + 1].rearrange("p mc ch w -> p (mc ch w)")


def wsup(c, wname, half, mwidth=256):
    return (c.scr[wname].rearrange("(kc p) m -> p kc m", p=128)
            [:, :, half * mwidth:(half + 1) * mwidth])


def phase_l0attn(c):
    P = c.P
    f0, b0 = c.AF.mark(), c.AB.mark()

    STRV = [c.AB.buf(512) for _ in range(NCC)]
    STRG = [c.AB.buf(512) for _ in range(NCC)]
    RTA = c.AB.take(NCC * 512)
    RT = [Buf(RTA[:, mc * 512:(mc + 1) * 512]) for mc in range(NCC)]
    KTEA = c.AB.take(NCC * 512)
    KTE = [Buf(KTEA[:, mc * 512:(mc + 1) * 512]) for mc in range(NCC)]
    KTOA = c.AB.take(NCC * 512)
    KTO = [Buf(KTOA[:, mc * 512:(mc + 1) * 512]) for mc in range(NCC)]
    PIBUF = Buf(c.AF.take(NCC * NCH0 * 130))
    CARRY = [Buf(c.AF.take(8)) for _ in range(2)]
    EMID = Buf(c.AF.take(32))
    CD = Buf(c.AF.take(32))
    CD2 = Buf(c.AF.take(32))
    NMID = Buf(c.AF.take(32))
    RCD = Buf(c.AF.take(32))
    IND16 = c.AB.buf(128)
    IND16T = Buf(c.AB.take(1024, rows=16))
    for _b, _nm in ((IND16, "c_ind16"), (IND16T, "c_ind16t")):
        _t = E(P, "sync", c.s_ld, lambda e, d=_b.ap, s=c.inp[_nm]:
               e.dma_start(d, s),
               waits=_b.war() + tuple(c.AB.fence)
               + ((c.s_ld, c.s_ld.val),), k=16)
        _b.wrote(_t)

    fi, bi = c.AF.mark(), c.AB.mark()
    ZT = [c.AB.buf(512) for _ in range(NCC)]
    XXT = [c.AB.buf(512) for _ in range(NCC)]
    XXX = [c.AB.buf(512) for _ in range(2)]
    STRR = [c.AB.buf(512) for _ in range(NCC)]
    SQB = XXX[0]
    TMPB = XXX[1]
    MXALL = c.AB.buf(512)
    MXALL2 = c.AB.buf(512)
    TH = c.AB.buf(512, rows=64)
    TMP2 = [c.AB.buf(512) for _ in range(2)]
    EXPWR = [Buf(c.AF.take(512)) for _ in range(2)]
    EXPX = [Buf(c.AF.take(512)) for _ in range(2)]
    fi2, bi2 = c.AF.mark(), c.AB.mark()

    c.AF.reset(fi)
    c.AB.reset(bi)
    VTM = [c.AB.buf(1024) for _ in range(NCH0)]
    yb_base = c.AB.mark()
    YBALL = c.AB.take(NCC * 512)
    YB = [Buf(c.AB.t.ap()[:, yb_base + cc * 512: yb_base + (cc + 1) * 512])
          for cc in range(NCC)]
    abuf_base = c.AB.mark()
    ABUF = c.AB.buf(512)
    TMPA = c.AB.buf(512)
    D4 = c.AB.buf(512)
    TMPA2 = D4
    KHT = [c.AB.buf(128) for _ in range(2)]
    RKA = c.AB.buf(1024)
    KHA = RKA
    YSQ = TMPA
    SSC2 = c.AB.buf(1024)
    ENWS = Buf(c.AF.t.ap()[:, fi:fi + 1024])
    ENWB = Buf(c.AB.t.ap()[:, abuf_base + 512:abuf_base + 1536])
    GMB = c.AB.buf(512, rows=16)
    GRB = c.AB.buf(512, rows=16)
    YNG = [c.AB.buf(512) for _ in range(NCC)]
    GSIL = [c.AB.buf(512)]
    RLW = ABUF
    GN = [Buf(c.AF.take(512, rows=16)) for _ in range(3)]
    DGT = Buf(c.AF.take(16))
    fi3, bi3 = c.AF.mark(), c.AB.mark()

    inner = (ZT + XXT + XXX + STRR + [MXALL, MXALL2, TH]
             + TMP2 + EXPWR + EXPX)
    post = (VTM + YB + YNG + GSIL + [ABUF, TMPA, D4, SSC2, ENWS, ENWB,
                                     GMB, GRB, DGT, RKA] + KHT + GN)
    for b in inner:
        b.aliases = post
        b.ready = list(c.AB.fence)
    for b in post:
        b.aliases = inner
        b.ready = list(c.AB.fence)
    for b in (STRV + STRG + RT + KTE + KTO + CARRY
              + [EMID, CD, CD2, NMID, RCD, PIBUF]):
        b.ready = list(c.AB.fence)
    ENWB.aliases = ENWB.aliases + [TMPA, D4]
    TMPA.aliases = TMPA.aliases + [ENWB]
    D4.aliases = D4.aliases + [ENWB]
    c.AF.reset(max(fi2, fi3))
    c.AB.reset(max(bi2, bi3))

    tS = E(P, "gpsimd", c.s_pool, lambda e: e.memset(c.S.ap()[:, :], 0.0),
           waits=c.sbuf.war())
    c.sbuf.wrote(tS)
    for i in range(2):
        tC = E(P, "gpsimd", c.s_pool, lambda e, a=CARRY[i].ap: e.memset(a, 0.0),
               waits=CARRY[i].war())
        CARRY[i].wrote(tC)
    for mc in range(NCC):
        tz1 = E(P, "gpsimd", c.s_pool,
                lambda e, a=KTE[mc].ap: e.memset(a, 0.0),
                waits=KTE[mc].war())
        KTE[mc].add_write(tz1)
        tz2 = E(P, "gpsimd", c.s_pool,
                lambda e, a=KTO[mc].ap: e.memset(a, 0.0),
                waits=KTO[mc].war())
        KTO[mc].add_write(tz2)
    tz3 = E(P, "gpsimd", c.s_pool, lambda e, a=SSC2.ap: e.memset(a, 0.0),
            waits=SSC2.war())
    SSC2.add_write(tz3)
    tz4 = E(P, "gpsimd", c.s_pool, lambda e, a=MXALL2.ap:
            e.memset(a, 0.0), waits=MXALL2.war())
    MXALL2.add_write(tz4)

    maa_s_col = [c.cols["c0_maa_w"], c.cols["c0_maa_k"], c.cols["c0_maa_v"],
                 c.cols["c0_maa_r"], c.cols["c0_maa_g"]]
    trunc = globals().get("L0_TRUNC", "full")

    def stream_tile(s_idx, dest_tiles):
        mxb = MXALL if s_idx < 4 else MXALL2
        lb = c.WRB.load(c.scr["w_maa2"][s_idx * 128:(s_idx + 1) * 128, :],
                        rows=128, cols=1024,
                        extra=(c.prol["w_maa2"],))
        for mc in range(NCC):
            psx, pbx = (c.PS[4], c.PB[4]) if mc % 2 == 0 else (c.PS[5], c.PB[5])
            tmm = E(P, "tensor", c.s_pe,
                    lambda e, o=psx.ap()[:, :],
                    l=lb.ap[0:128, mc * 128:(mc + 1) * 128],
                    r=mxb.ap:
                    e.matmul(o, l, r, start=True, stop=True),
                    waits=lb.raw() + mxb.raw() + pbx.war())
            lb.read_by(tmm)
            mxb.read_by(tmm)
            pbx.wrote(tmm)
            t2b = TMP2[mc % 2]
            tm2 = E(P, "vector", c.s_dve,
                    lambda e, o=t2b.ap, a=psx.ap()[:, :],
                    s=maa_s_col[s_idx].ap()[:, mc:mc + 1], b=XXT[mc].ap:
                    e.scalar_tensor_tensor(o, a, s, b, OP.add, OP.mult),
                    waits=pbx.raw() + XXT[mc].raw() + t2b.war() + c.tok_init)
            t2b.wrote(tm2)
            pbx.read_by(tm2)
            XXT[mc].read_by(tm2)
            ts3 = E(P, "gpsimd", c.s_pool,
                    lambda e, o=dest_tiles[mc].ap, a=t2b.ap, b=ZT[mc].ap:
                    e.tensor_tensor(o, a, b, OP.add),
                    waits=t2b.raw() + ZT[mc].raw() + dest_tiles[mc].war())
            dest_tiles[mc].wrote(ts3)
            t2b.read_by(ts3)
            ZT[mc].read_by(ts3)

    def project_scaled(s_idx, wname, out_tiles, kind):
        stream_tile(s_idx, STRR)
        for half in range(4):
            wb = c.WRA.load(wsup(c, wname, half), cols=2048,
                            extra=(c.prol[wname],))
            for sub in range(2):
                mc = half * 2 + sub
                psx, pbx = (c.PS[6], c.PB[6]) if mc % 2 == 0 else (c.PS[7], c.PB[7])
                tokm = None
                for kc in range(NCC):
                    tokm = E(P, "tensor", c.s_pe,
                             lambda e, o=psx.ap()[:, :],
                             l=wb.ap[:, kc * 256 + sub * 128:
                                     kc * 256 + sub * 128 + 128],
                             r=STRR[kc].ap, s=(kc == 0), p=(kc == NCC - 1):
                             e.matmul(o, l, r, start=s, stop=p),
                             waits=wb.raw() + STRR[kc].raw()
                             + (pbx.war() if kc == 0 else ()))
                    wb.read_by(tokm)
                    STRR[kc].read_by(tokm)
                pbx.wrote(tokm)
                ex = EXPX[mc % 2]
                for ch in range(NCH0):
                    if kind == "r":
                        iap, sc, bap = pib(PIBUF, mc, ch, 0, 128), -1.0, \
                            pib(PIBUF, mc, ch, 64)
                    else:
                        iap, sc, bap = pib(PIBUF, mc, ch, 1, 128), 1.0, \
                            NMID.ap[:, ch * 8 + mc:ch * 8 + mc + 1]
                    tex = E(P, "scalar", c.s_act,
                            lambda e, o=ex.ap[:, ch * 128:(ch + 1) * 128],
                            i=iap, s=sc, b=bap:
                            e.activation(o, i, AF.Exp, scale=s, bias=b),
                            waits=PIBUF.raw() + NMID.raw()
                            + (ex.war() if ch == 0 else ()))
                    PIBUF.read_by(tex)
                    NMID.read_by(tex)
                    (ex.wrote if ch == 0 else ex.add_write)(tex)
                tcl = E(P, "vector", c.s_dve,
                        lambda e, o=ex.ap, i=ex.ap:
                        e.tensor_scalar(o, i, 1e17, None, OP.min),
                        waits=ex.raw())
                ex.add_write(tcl)
                if kind == "r":
                    tev = E(P, "vector", c.s_dve,
                            lambda e, o=out_tiles[mc].ap, a=psx.ap()[:, :],
                            b=ex.ap:
                            e.tensor_tensor(o, a, b, OP.mult),
                            waits=pbx.raw() + ex.raw() + out_tiles[mc].war())
                    out_tiles[mc].wrote(tev)
                    pbx.read_by(tev)
                    ex.read_by(tev)
                else:
                    be, bo = KTE[mc], KTO[mc]
                    t1 = E(P, "vector", c.s_dve,
                           lambda e, o=be.ap[0:64, :], a=psx.ap()[0:64, :],
                           b=ex.ap[0:64, :]:
                           e.tensor_tensor(o, a, b, OP.mult),
                           waits=pbx.raw() + ex.raw() + be.war())
                    be.add_write(t1)
                    pbx.read_by(t1)
                    ex.read_by(t1)
                    t2 = E(P, "vector", c.s_dve,
                           lambda e, o=bo.ap[64:128, :], a=psx.ap()[64:128, :],
                           b=ex.ap[64:128, :]:
                           e.tensor_tensor(o, a, b, OP.mult),
                           waits=bo.war())
                    bo.add_write(t2)
                    pbx.read_by(t2)
                    ex.read_by(t2)

    for tile in range(NT0):
        t0 = tile * TT0
        cin, cout = CARRY[tile % 2], CARRY[(tile + 1) % 2]
        if trunc == "none":
            continue
        emit_preln(c, t0, TT0, ZT, SQB, TMPB)
        if trunc == "preln":
            continue
        for cc in range(NCC):
            tcs = E(P, "vector", c.s_dve,
                    lambda e, o=cout.ap[:, cc:cc + 1], i=ZT[cc].ap[:, 511:512]:
                    e.tensor_copy(o, i), waits=ZT[cc].raw() + cout.war())
            (cout.wrote if cc == 0 else cout.add_write)(tcs)
            ZT[cc].read_by(tcs)
            tx = E(P, "vector", c.s_dve,
                   lambda e, o=XXT[cc].ap[:, 1:512], a=ZT[cc].ap[:, 0:511],
                   b=ZT[cc].ap[:, 1:512]:
                   e.tensor_tensor(o, a, b, OP.subtract),
                   waits=ZT[cc].raw() + XXT[cc].war() + cin.raw())
            tx0 = E(P, "vector", c.s_dve,
                    lambda e, o=XXT[cc].ap[:, 0:1], a=cin.ap[:, cc:cc + 1],
                    b=ZT[cc].ap[:, 0:1]:
                    e.tensor_tensor(o, a, b, OP.subtract))
            XXT[cc].wrote(tx)
            XXT[cc].add_write(tx0)
            ZT[cc].read_by(tx0)
            cin.read_by(tx0)
            xb = XXX[cc % 2]
            txx = E(P, "vector", c.s_dve,
                    lambda e, o=xb.ap, a=XXT[cc].ap,
                    s=c.cols["c0_maa_x"].ap()[:, cc:cc + 1], b=ZT[cc].ap:
                    e.scalar_tensor_tensor(o, a, s, b, OP.mult, OP.add),
                    waits=xb.war() + c.tok_init)
            xb.wrote(txx)
            XXT[cc].read_by(txx)
            ZT[cc].read_by(txx)
            pA, pB = c.PB[0], c.PB[1]
            tokA = E(P, "tensor", c.s_pe,
                     lambda e, o=c.PS[0].ap()[:, :],
                     l=c.MAA1.ap()[:, cc * 160:cc * 160 + 128],
                     r=xb.ap, s=(cc == 0), p=(cc == NCC - 1):
                     e.matmul(o, l, r, start=s, stop=p),
                     waits=xb.raw() + (pA.war() if cc == 0 else ())
                     + c.tok_init)
            xb.read_by(tokA)
            tokB = E(P, "tensor", c.s_pe,
                     lambda e, o=c.PS[1].ap()[0:32, :],
                     l=c.MAA1.ap()[:, cc * 160 + 128:cc * 160 + 160],
                     r=xb.ap, s=(cc == 0), p=(cc == NCC - 1):
                     e.matmul(o, l, r, start=s, stop=p),
                     waits=(pB.war() if cc == 0 else ()))
            xb.read_by(tokB)
        pA.wrote(tokA)
        pB.wrote(tokB)
        for dst, rows, src_ap, pb_ in (
                (MXALL, (0, 64), c.PS[0].ap()[0:64, :], pA),
                (MXALL, (64, 128), c.PS[0].ap()[64:128, :], pA),
                (MXALL2, (0, 32), c.PS[1].ap()[0:32, :], pB)):
            tm = E(P, "scalar", c.s_act,
                   lambda e, o=dst.ap[rows[0]:rows[1], :], i=src_ap:
                   e.activation(o, i, AF.Tanh),
                   waits=pb_.raw() + (dst.war() if (rows[0] == 0 or dst is MXALL2)
                                      else ()))
            (dst.wrote if (dst is MXALL and rows[0] == 0) else dst.add_write)(tm)
            pb_.read_by(tm)

        tz = E(P, "gpsimd", c.s_pool,
               lambda e, a=PIBUF.ap.rearrange("p (q w) -> p q w", w=130)
               [:, :, 0:1].rearrange("p q w -> p (q w)"):
               e.memset(a, 0.0), waits=PIBUF.war())
        PIBUF.wrote(tz)
        stream_tile(0, STRR)
        tth = None
        for cc in range(NCC):
            tth = E(P, "tensor", c.s_pe,
                    lambda e, o=c.PS[5].ap()[0:64, :],
                    l=c.DW1.ap()[:, cc * 64:(cc + 1) * 64], r=STRR[cc].ap,
                    s=(cc == 0), p=(cc == NCC - 1):
                    e.matmul(o, l, r, start=s, stop=p),
                    waits=STRR[cc].raw() + (c.PB[5].war() if cc == 0 else ())
                    + c.tok_init)
            STRR[cc].read_by(tth)
        c.PB[5].wrote(tth)
        tte = E(P, "scalar", c.s_act,
                lambda e, o=TH.ap, i=c.PS[5].ap()[0:64, :]:
                e.activation(o, i, AF.Tanh), waits=c.PB[5].raw() + TH.war())
        TH.wrote(tte)
        c.PB[5].read_by(tte)
        for mc in range(NCC):
            psx, pbx = (c.PS[4], c.PB[4]) if mc % 2 == 0 else (c.PS[5], c.PB[5])
            tw = E(P, "tensor", c.s_pe,
                   lambda e, o=psx.ap()[:, :],
                   l=c.DW2.ap()[:, mc * 128:(mc + 1) * 128], r=TH.ap:
                   e.matmul(o, l, r, start=True, stop=True),
                   waits=TH.raw() + pbx.war() + c.tok_init)
            TH.read_by(tw)
            pbx.wrote(tw)
            ew = EXPWR[mc % 2]
            te = E(P, "scalar", c.s_act,
                   lambda e, o=ew.ap, i=psx.ap()[:, :],
                   b=c.cols["c0_decay"].ap()[:, mc:mc + 1]:
                   e.activation(o, i, AF.Exp, bias=b),
                   waits=pbx.raw() + ew.war() + c.tok_init)
            ew.wrote(te)
            pbx.read_by(te)
            for ch in range(NCH0):
                base = (mc * NCH0 + ch) * 130
                tsc = E(P, "vector", c.s_dve,
                        lambda e, o=PIBUF.ap[:, base + 1:base + 129],
                        d0=ew.ap[:, ch * 128:(ch + 1) * 128],
                        d1=c.ZERO.ap()[:, 0:128]:
                        e.tensor_tensor_scan(o, d0, d1, 0.0, OP.add, OP.add),
                        waits=ew.raw() + PIBUF.raw() + c.tok_init)
                PIBUF.add_write(tsc)
                ew.read_by(tsc)
        for ch in range(NCH0):
            te1 = E(P, "scalar", c.s_act,
                    lambda e, o=EMID.ap[:, ch * 8:ch * 8 + 8],
                    i=pib_tab(PIBUF, ch, 64):
                    e.activation(o, i, AF.Exp, scale=-1.0),
                    waits=PIBUF.raw() + EMID.war())
            (EMID.wrote if ch == 0 else EMID.add_write)(te1)
            PIBUF.read_by(te1)
            te2 = E(P, "scalar", c.s_act,
                    lambda e, o=CD.ap[:, ch * 8:ch * 8 + 8],
                    i=pib_tab(PIBUF, ch, 128):
                    e.activation(o, i, AF.Exp, scale=-1.0),
                    waits=PIBUF.raw() + CD.war())
            (CD.wrote if ch == 0 else CD.add_write)(te2)
            PIBUF.read_by(te2)
            tnm = E(P, "vector", c.s_dve,
                    lambda e, o=NMID.ap[:, ch * 8:ch * 8 + 8],
                    i=pib_tab(PIBUF, ch, 64):
                    e.tensor_scalar(o, i, -1.0, None, OP.mult),
                    waits=PIBUF.raw() + NMID.war())
            (NMID.wrote if ch == 0 else NMID.add_write)(tnm)
            PIBUF.read_by(tnm)
        tr1 = E(P, "vector", c.s_dve, lambda e, o=RCD.ap, i=EMID.ap:
                e.reciprocal(o, i), waits=EMID.raw() + RCD.war())
        RCD.wrote(tr1)
        EMID.read_by(tr1)
        tr2 = E(P, "vector", c.s_dve, lambda e, o=CD2.ap, a=CD.ap, b=RCD.ap:
                e.tensor_tensor(o, a, b, OP.mult),
                waits=CD.raw() + CD2.war())
        CD2.wrote(tr2)
        RCD.read_by(tr2)
        CD.read_by(tr2)

        if trunc == "wpath":
            continue
        project_scaled(3, "w_Wr", RT, "r")
        project_scaled(1, "w_Wk", None, "k")
        stream_tile(2, STRV)
        stream_tile(4, STRG)

        for half in range(4):
            wb = c.WRA.load(wsup(c, "w_Wv", half), cols=2048,
                            extra=(c.prol["w_Wv"],))
            for tb in range(NCH0):
                psx, pbx = (c.PS[6], c.PB[6]) if tb % 2 == 0 else (c.PS[7], c.PB[7])
                tokm = None
                for kc in range(NCC):
                    tokm = E(P, "tensor", c.s_pe,
                             lambda e, o=psx.ap()[:, 0:256],
                             l=STRV[kc].ap[:, tb * 128:(tb + 1) * 128],
                             r=wb.ap[:, kc * 256:(kc + 1) * 256],
                             s=(kc == 0), p=(kc == NCC - 1):
                             e.matmul(o, l, r, start=s, stop=p),
                             waits=wb.raw() + STRV[kc].raw()
                             + (pbx.war() if kc == 0 else ()))
                    wb.read_by(tokm)
                    STRV[kc].read_by(tokm)
                pbx.wrote(tokm)
                tv = E(P, "scalar", c.s_act,
                       lambda e, o=VTM[tb].ap[:, half * 256:(half + 1) * 256],
                       i=psx.ap()[:, 0:256]: e.activation(o, i, AF.Copy),
                       waits=pbx.raw() + (VTM[tb].war() if half == 0 else ()))
                (VTM[tb].wrote if half == 0 else VTM[tb].add_write)(tv)
                pbx.read_by(tv)

        if trunc == "streams":
            continue
        tzs = E(P, "gpsimd", c.s_pool, lambda e, a=SSC2.ap: e.memset(a, 0.0),
                waits=SSC2.war())
        SSC2.wrote(tzs)
        for ch in range(NCH0):
            wtr = globals().get("WKV_TRUNC", "full")
            for par in range(2):
                rr = 64 * par
                o3 = (SSC2.ap[rr:rr + 64, :]
                      .rearrange("p (m t) -> p m t", t=128)
                      [:, :, par * 64:par * 64 + 64])
                a3 = (c.S.ap()[rr:rr + 64, :]
                      .rearrange("p (m t) -> p m t", t=64))
                b3 = (EMID.ap[rr:rr + 64, ch * 8:ch * 8 + 8]
                      .rearrange("p (m o) -> p m o", o=1)
                      .broadcast_to([64, 8, 64]))
                tsnap = E(P, "gpsimd", c.s_pool,
                          lambda e, o=o3, a=a3, b=b3:
                          e.tensor_tensor(o, a, b, OP.mult),
                          waits=c.sbuf.raw() + EMID.raw()
                          + (SSC2.war() if par == 0 else ()))
                c.sbuf.read_by(tsnap)
                EMID.read_by(tsnap)
                SSC2.add_write(tsnap)
            if wtr == "ssc":
                continue
            pall = PIBUF.ap.rearrange("p (m t) -> p m t", t=NCH0 * 130)
            ten1 = E(P, "vector", c.s_dve,
                     lambda e, o=ENWS.ap.rearrange("p (m t) -> p m t", t=128),
                     a=pall[:, :, ch * 130:ch * 130 + 128],
                     b=pall[:, :, ch * 130 + 1:ch * 130 + 129]:
                     e.tensor_tensor(o, a, b, OP.subtract),
                     waits=PIBUF.raw() + ENWS.war())
            ENWS.wrote(ten1)
            PIBUF.read_by(ten1)
            ten2 = E(P, "scalar", c.s_act,
                     lambda e, o=ENWB.ap, i=ENWS.ap:
                     e.activation(o, i, AF.Exp),
                     waits=ENWS.raw() + ENWB.war())
            ENWB.wrote(ten2)
            ENWS.read_by(ten2)
            for par, arr, kbl in ((0, KTEA, KTE), (1, KTOA, KTO)):
                rr = 64 * par
                o3 = (RKA.ap[rr:rr + 64, :]
                      .rearrange("p (m t) -> p m t", t=128))
                a3 = (RTA[rr:rr + 64, :]
                      .rearrange("p (m t) -> p m t", t=512)
                      [:, :, ch * 128:(ch + 1) * 128])
                b3 = (arr[rr:rr + 64, :]
                      .rearrange("p (m t) -> p m t", t=512)
                      [:, :, ch * 128:(ch + 1) * 128])
                tr = E(P, "vector", c.s_dve,
                       lambda e, o=o3, a=a3, b=b3:
                       e.tensor_tensor(o, a, b, OP.mult),
                       waits=(RKA.war() if par == 0 else ())
                       + tuple(t for b_ in RT for t in b_.raw())
                       + tuple(t for b_ in kbl for t in b_.raw()))
                (RKA.wrote if par == 0 else RKA.add_write)(tr)
                for b_ in RT:
                    b_.read_by(tr)
                for b_ in kbl:
                    b_.read_by(tr)
            tu = E(P, "vector", c.s_dve,
                   lambda e, o=RKA.ap.rearrange("p (m t) -> p m t", t=128),
                   a=RKA.ap.rearrange("p (m t) -> p m t", t=128),
                   b=c.cols["c0_u"].ap()[:, 0:8]
                   .rearrange("p (m o) -> p m o", o=1)
                   .broadcast_to([128, 8, 128]):
                   e.tensor_tensor(o, a, b, OP.mult),
                   waits=c.tok_init)
            RKA.add_write(tu)
            te2 = E(P, "vector", c.s_dve,
                    lambda e, o=RKA.ap, a=RKA.ap, b=ENWB.ap:
                    e.tensor_tensor(o, a, b, OP.mult),
                    waits=ENWB.raw())
            RKA.add_write(te2)
            ENWB.read_by(te2)
            tdm = None
            for mc in range(NCC):
                tdm = E(P, "tensor", c.s_pe,
                        lambda e, o=c.PS[1].ap()[:, 0:16],
                        l=RKA.ap[:, mc * 128:(mc + 1) * 128],
                        r=IND16.ap[:, mc * 16:(mc + 1) * 16],
                        s=(mc == 0), p=(mc == NCC - 1):
                        e.matmul(o, l, r, start=s, stop=p),
                        waits=RKA.raw() + IND16.raw()
                        + (c.PB[1].war() if mc == 0 else ()))
                RKA.read_by(tdm)
                IND16.read_by(tdm)
            c.PB[1].wrote(tdm)
            tdg = E(P, "scalar", c.s_act,
                    lambda e, o=DGT.ap, i=c.PS[1].ap()[:, 0:16]:
                    e.activation(o, i, AF.Identity),
                    waits=c.PB[1].raw() + DGT.war())
            DGT.wrote(tdg)
            c.PB[1].read_by(tdg)

            if wtr == "diag":
                continue
            for hg in range(4):
                psa, pba = (c.PS[4], c.PB[4]) if hg % 2 == 0 else (c.PS[2], c.PB[2])
                tokA = None
                for h4 in range(4):
                    h = hg * 4 + h4
                    mc = h // 2
                    kb = KTE[mc] if h % 2 == 0 else KTO[mc]
                    tokA = E(P, "tensor", c.s_pe,
                             lambda e, o=psa.ap()[:, h4 * 128:(h4 + 1) * 128],
                             l=kb.ap[:, ch * 128:(ch + 1) * 128],
                             r=RT[mc].ap[:, ch * 128:(ch + 1) * 128]:
                             e.matmul(o, l, r, start=True, stop=True),
                             waits=kb.raw() + RT[mc].raw()
                             + (pba.war() if h4 == 0 else ()))
                    kb.read_by(tokA)
                    RT[mc].read_by(tokA)
                pba.wrote(tokA)
                if wtr == "atmm":
                    continue
                tD = E(P, "gpsimd", c.s_pool,
                       lambda e, o=D4.ap.rearrange("p (m t) -> p m t", t=128),
                       a=c.IDENT.ap()[:, :]
                       .rearrange("p (o t) -> p o t", o=1)
                       .broadcast_to([128, 4, 128]),
                       b=DGT.ap[:, hg * 4:hg * 4 + 4]
                       .rearrange("p (m o) -> p m o", o=1)
                       .broadcast_to([128, 4, 128]):
                       e.tensor_tensor(o, a, b, OP.mult),
                       waits=DGT.raw() + D4.war() + c.tok_init)
                DGT.read_by(tD)
                D4.wrote(tD)
                tA1 = E(P, "vector", c.s_dve,
                        lambda e, o=TMPA.ap, a=psa.ap()[:, :], b=c.LTM.ap()[:, :]:
                        e.tensor_tensor(o, a, b, OP.mult),
                        waits=pba.raw() + TMPA.war() + c.tok_init)
                TMPA.wrote(tA1)
                pba.read_by(tA1)
                tA2 = E(P, "gpsimd", c.s_pool,
                        lambda e, o=ABUF.ap, a=TMPA.ap, b=D4.ap:
                        e.tensor_tensor(o, a, b, OP.add),
                        waits=TMPA.raw() + D4.raw() + ABUF.war())
                ABUF.wrote(tA2)
                TMPA.read_by(tA2)
                D4.read_by(tA2)
                if wtr in ("at", "atmm"):
                    continue
                psy, pby = (c.PS[6], c.PB[6]) if hg < 2 else (c.PS[7], c.PB[7])
                ycol = (hg % 2) * 256
                for h4 in range(4):
                    h = hg * 4 + h4
                    mc, rr = h // 2, 64 * (h % 2)
                    dst = psy.ap()[rr:rr + 64,
                                   ycol + (h4 // 2) * 128:ycol + (h4 // 2) * 128 + 128]
                    t1 = E(P, "tensor", c.s_pe,
                           lambda e, o=dst,
                           l=VTM[ch].ap[:, mc * 128 + rr:mc * 128 + rr + 64],
                           r=ABUF.ap[:, h4 * 128:(h4 + 1) * 128]:
                           e.matmul(o, l, r, start=True, stop=False),
                           waits=VTM[ch].raw() + ABUF.raw()
                           + (pby.war() if (hg % 2 == 0 and h4 == 0) else ()))
                    VTM[ch].read_by(t1)
                    ABUF.read_by(t1)
                    toky = E(P, "tensor", c.s_pe,
                             lambda e, o=dst, l=SSC2.ap[:, h * 64:(h + 1) * 64],
                             r=RT[mc].ap[:, ch * 128:(ch + 1) * 128]:
                             e.matmul(o, l, r, start=False, stop=True),
                             waits=SSC2.raw() + RT[mc].raw())
                    SSC2.read_by(toky)
                    RT[mc].read_by(toky)
                    pby.add_write(toky)
            if wtr in ("y", "atmm"):
                continue
            for half in range(2):
                psy, pby = (c.PS[6], c.PB[6]) if half == 0 else (c.PS[7], c.PB[7])
                base = yb_base + half * 4 * 512
                oview = (c.AB.t.ap()[:, base:base + 2048]
                         .rearrange("p (g f) -> p g f", f=512)
                         [:, :, ch * 128:(ch + 1) * 128])
                ty = E(P, "vector", c.s_dve,
                       lambda e, o=oview,
                       i=psy.ap()[:, :].rearrange("p (g f) -> p g f", f=128):
                       e.tensor_copy(o, i),
                       waits=pby.raw() + tuple(t for b in YB[half * 4:half * 4 + 4]
                                               for t in b.war()))
                for b in YB[half * 4:half * 4 + 4]:
                    (b.wrote if ch == 0 else b.add_write)(ty)
                pby.read_by(ty)
            if wtr in ("yevac", "atmm2"):
                continue
            for par, arr, kbl in ((0, KTEA, KTE), (1, KTOA, KTO)):
                rr = 64 * par
                o3 = (KHA.ap[rr:rr + 64, :]
                      .rearrange("p (m t) -> p m t", t=128))
                a3 = (arr[rr:rr + 64, :]
                      .rearrange("p (m t) -> p m t", t=512)
                      [:, :, ch * 128:(ch + 1) * 128])
                b3 = (CD2.ap[rr:rr + 64, ch * 8:ch * 8 + 8]
                      .rearrange("p (m o) -> p m o", o=1)
                      .broadcast_to([64, 8, 128]))
                tk = E(P, "gpsimd", c.s_pool,
                       lambda e, o=o3, a=a3, b=b3:
                       e.tensor_tensor(o, a, b, OP.mult),
                       waits=(KHA.war() if par == 0 else ()) + CD2.raw()
                       + tuple(t for b_ in kbl for t in b_.raw()))
                (KHA.wrote if par == 0 else KHA.add_write)(tk)
                for b_ in kbl:
                    b_.read_by(tk)
                CD2.read_by(tk)
            tlast = None
            for mc in range(NCC):
                pst = c.PS[1].ap().bitcast(BF16)[:, 0:128]
                tk2 = E(P, "tensor", c.s_pe,
                        lambda e, o=pst, i=KHA.ap[:, mc * 128:(mc + 1) * 128],
                        ident=c.IDENT.ap()[:, :]:
                        e.transpose(o, i, ident),
                        waits=KHA.raw() + c.PB[1].war() + c.tok_init)
                c.PB[1].wrote(tk2)
                KHA.read_by(tk2)
                kht = KHT[mc % 2]
                tk3 = E(P, "scalar", c.s_act,
                        lambda e, o=kht.ap, i=pst:
                        e.activation(o, i, AF.Identity),
                        waits=c.PB[1].raw() + kht.war())
                kht.wrote(tk3)
                c.PB[1].read_by(tk3)
                for hh in range(2):
                    h = mc * 2 + hh
                    rr = 64 * hh
                    dst = c.PS[5].ap()[rr:rr + 64, mc * 64:(mc + 1) * 64]
                    tlast = E(P, "tensor", c.s_pe,
                              lambda e, o=dst, l=kht.ap[:, rr:rr + 64],
                              r=VTM[ch].ap[:, mc * 128 + rr:mc * 128 + rr + 64]:
                              e.matmul(o, l, r, start=True, stop=True),
                              waits=kht.raw() + VTM[ch].raw()
                              + (c.PB[5].war() if (mc == 0 and hh == 0) else ()))
                    kht.read_by(tlast)
                    VTM[ch].read_by(tlast)
                    c.PB[5].add_write(tlast)
            for par in range(2):
                rr = 64 * par
                o3 = (c.S.ap()[rr:rr + 64, :]
                      .rearrange("p (m t) -> p m t", t=64))
                b3 = (CD.ap[rr:rr + 64, ch * 8:ch * 8 + 8]
                      .rearrange("p (m o) -> p m o", o=1)
                      .broadcast_to([64, 8, 64]))
                tS1 = E(P, "gpsimd", c.s_pool,
                        lambda e, o=o3, a=o3, b=b3:
                        e.tensor_tensor(o, a, b, OP.mult),
                        waits=c.sbuf.war() + CD.raw())
                c.sbuf.add_write(tS1)
                CD.read_by(tS1)
            tS2 = E(P, "vector", c.s_dve,
                    lambda e, o=c.S.ap()[:, :], a=c.S.ap()[:, :],
                    b=c.PS[5].ap()[:, :]:
                    e.tensor_tensor(o, a, b, OP.add),
                    waits=c.PB[5].raw() + c.sbuf.war() + c.sbuf.raw())
            c.sbuf.add_write(tS2)
            c.PB[5].read_by(tS2)

        if trunc == "wkv":
            continue
        wgb = None
        for cc in range(NCC):
            if cc % 2 == 0:
                wgb = c.WRA.load(wsup(c, "w_Wg", cc // 2), cols=2048,
                                 extra=(c.prol["w_Wg"],))
            tokg = None
            for kc in range(NCC):
                tokg = E(P, "tensor", c.s_pe,
                         lambda e, o=c.PS[0].ap()[:, :],
                         l=wgb.ap[:, kc * 256 + (cc % 2) * 128:
                                  kc * 256 + (cc % 2) * 128 + 128],
                         r=STRG[kc].ap, s=(kc == 0), p=(kc == NCC - 1):
                         e.matmul(o, l, r, start=s, stop=p),
                         waits=wgb.raw() + STRG[kc].raw()
                         + (c.PB[0].war() if kc == 0 else ()))
                wgb.read_by(tokg)
                STRG[kc].read_by(tokg)
            c.PB[0].wrote(tokg)
            gs = GSIL[0]
            tg = E(P, "scalar", c.s_act,
                   lambda e, o=gs.ap, i=c.PS[0].ap()[:, :]:
                   e.activation(o, i, AF.Silu), waits=c.PB[0].raw() + gs.war())
            gs.wrote(tg)
            c.PB[0].read_by(tg)
            tq = E(P, "scalar", c.s_act,
                   lambda e, o=YSQ.ap, i=YB[cc].ap: e.activation(o, i, AF.Square),
                   waits=YB[cc].raw() + YSQ.war())
            YSQ.wrote(tq)
            YB[cc].read_by(tq)
            tm1 = E(P, "tensor", c.s_pe,
                    lambda e, o=c.PS[2].ap()[0:16, :],
                    l=IND16.ap[:, cc * 16:(cc + 1) * 16], r=YB[cc].ap:
                    e.matmul(o, l, r, start=True, stop=True),
                    waits=YB[cc].raw() + IND16.raw() + c.PB[2].war())
            c.PB[2].wrote(tm1)
            YB[cc].read_by(tm1)
            IND16.read_by(tm1)
            tm2 = E(P, "tensor", c.s_pe,
                    lambda e, o=c.PS[3].ap()[0:16, :],
                    l=IND16.ap[:, cc * 16:(cc + 1) * 16], r=YSQ.ap:
                    e.matmul(o, l, r, start=True, stop=True),
                    waits=YSQ.raw() + c.PB[3].war())
            c.PB[3].wrote(tm2)
            YSQ.read_by(tm2)
            IND16.read_by(tm2)
            gm, gv, gt = GN[0], GN[1], GN[2]
            ts1 = E(P, "vector", c.s_dve,
                    lambda e, o=gm.ap, i=c.PS[2].ap()[0:16, :]:
                    e.tensor_scalar(o, i, 1.0 / 64, None, OP.mult),
                    waits=c.PB[2].raw() + gm.war())
            gm.wrote(ts1)
            c.PB[2].read_by(ts1)
            ts2 = E(P, "vector", c.s_dve,
                    lambda e, o=gv.ap, i=c.PS[3].ap()[0:16, :]:
                    e.tensor_scalar(o, i, 1.0 / 64, None, OP.mult),
                    waits=c.PB[3].raw() + gv.war())
            gv.wrote(ts2)
            c.PB[3].read_by(ts2)
            ts3 = E(P, "vector", c.s_dve,
                    lambda e, o=gt.ap, a=gm.ap, b=gm.ap:
                    e.tensor_tensor(o, a, b, OP.mult), waits=gt.war())
            gt.wrote(ts3)
            gm.read_by(ts3)
            ts4 = E(P, "vector", c.s_dve,
                    lambda e, o=gv.ap, a=gv.ap, b=gt.ap:
                    e.tensor_tensor(o, a, b, OP.subtract))
            ts4 = E(P, "vector", c.s_dve,
                    lambda e, o=gv.ap, i=gv.ap:
                    e.tensor_scalar(o, i, 0.0, None, OP.max))
            gv.wrote(ts4)
            gt.read_by(ts4)
            ts5 = E(P, "scalar", c.s_act,
                    lambda e, o=gv.ap, i=gv.ap:
                    e.activation(o, i, AF.Sqrt, bias=c.EPSGN.ap()[0:16, :]),
                    waits=(ts4,) + c.tok_init)
            gv.wrote(ts5)
            ts6 = E(P, "vector", c.s_dve,
                    lambda e, o=gv.ap, i=gv.ap: e.reciprocal(o, i),
                    waits=(ts5,))
            gv.wrote(ts6)
            tb1 = E(P, "vector", c.s_dve,
                    lambda e, o=GMB.ap, i=gm.ap: e.tensor_copy(o, i),
                    waits=gm.raw() + GMB.war())
            GMB.wrote(tb1)
            gm.read_by(tb1)
            tb2 = E(P, "vector", c.s_dve,
                    lambda e, o=GRB.ap, i=gv.ap: e.tensor_copy(o, i),
                    waits=gv.raw() + GRB.war())
            GRB.wrote(tb2)
            gv.read_by(tb2)
            bm = E(P, "tensor", c.s_pe,
                   lambda e, o=c.PS[2].ap()[:, :],
                   l=IND16T.ap[:, cc * 128:(cc + 1) * 128], r=GMB.ap:
                   e.matmul(o, l, r, start=True, stop=True),
                   waits=GMB.raw() + IND16T.raw() + c.PB[2].war())
            c.PB[2].wrote(bm)
            GMB.read_by(bm)
            IND16T.read_by(bm)
            br = E(P, "tensor", c.s_pe,
                   lambda e, o=c.PS[3].ap()[:, :],
                   l=IND16T.ap[:, cc * 128:(cc + 1) * 128], r=GRB.ap:
                   e.matmul(o, l, r, start=True, stop=True),
                   waits=GRB.raw() + c.PB[3].war())
            c.PB[3].wrote(br)
            GRB.read_by(br)
            IND16T.read_by(br)
            tn1 = E(P, "vector", c.s_dve,
                    lambda e, o=TMPA.ap, a=YB[cc].ap, b=c.PS[2].ap()[:, :]:
                    e.tensor_tensor(o, a, b, OP.subtract),
                    waits=c.PB[2].raw() + YB[cc].raw() + TMPA.war())
            TMPA.wrote(tn1)
            c.PB[2].read_by(tn1)
            YB[cc].read_by(tn1)
            tn2 = E(P, "vector", c.s_dve,
                    lambda e, o=RLW.ap, i=c.PS[3].ap()[:, :],
                    s=c.cols["c0_ln_w"].ap()[:, cc:cc + 1]:
                    e.tensor_scalar(o, i, s, None, OP.mult),
                    waits=c.PB[3].raw() + RLW.war() + c.tok_init)
            RLW.wrote(tn2)
            c.PB[3].read_by(tn2)
            tn3 = E(P, "gpsimd", c.s_pool,
                    lambda e, o=TMPA2.ap, a=TMPA.ap, b=RLW.ap:
                    e.tensor_tensor(o, a, b, OP.mult),
                    waits=TMPA.raw() + RLW.raw() + TMPA2.war())
            TMPA2.wrote(tn3)
            TMPA.read_by(tn3)
            RLW.read_by(tn3)
            tn4 = E(P, "vector", c.s_dve,
                    lambda e, o=YNG[cc].ap, a=TMPA2.ap,
                    s=c.cols["c0_ln_b"].ap()[:, cc:cc + 1], b=gs.ap:
                    e.scalar_tensor_tensor(o, a, s, b, OP.add, OP.mult),
                    waits=TMPA2.raw() + gs.raw() + YNG[cc].war() + c.tok_init)
            YNG[cc].wrote(tn4)
            TMPA2.read_by(tn4)
            gs.read_by(tn4)
        for half in range(4):
            wb = c.WRA.load(wsup(c, "w_Wo", half), cols=2048,
                            extra=(c.prol["w_Wo"],))
            for sub in range(2):
                mc = half * 2 + sub
                psx, pbx = (c.PS[0], c.PB[0]) if mc % 2 == 0 else (c.PS[1], c.PB[1])
                tokm = None
                for kc in range(NCC):
                    tokm = E(P, "tensor", c.s_pe,
                             lambda e, o=psx.ap()[:, :],
                             l=wb.ap[:, kc * 256 + sub * 128:
                                     kc * 256 + sub * 128 + 128],
                             r=YNG[kc].ap, s=(kc == 0), p=(kc == NCC - 1):
                             e.matmul(o, l, r, start=s, stop=p),
                             waits=wb.raw() + YNG[kc].raw()
                             + (pbx.war() if kc == 0 else ()))
                    wb.read_by(tokm)
                    YNG[kc].read_by(tokm)
                pbx.wrote(tokm)
                xap = c.XRES.ap()[:, mc * T_SEQ + t0: mc * T_SEQ + t0 + 512]
                ta = E(P, "vector", c.s_dve,
                       lambda e, o=xap, a=xap, b=psx.ap()[:, :]:
                       e.tensor_tensor(o, a, b, OP.add),
                       waits=pbx.raw() + c.xres[mc].war())
                c.xres[mc].add_write(ta)
                pbx.read_by(ta)

    c.dbg = dict(ZT=ZT, XXT=XXT, XXX=XXX, STRR=STRR, RT=RT, KTE=KTE, KTO=KTO,
                 VTM=VTM, YB=YB, YNG=YNG, PIBUF=PIBUF, EMID=EMID, CD=CD,
                 CD2=CD2, NMID=NMID, STRV=STRV, STRG=STRG, TH=TH,
                 SSC2=SSC2, DGT=DGT, GSIL=GSIL, KHT=KHT,
                 IND16=IND16, IND16T=IND16T)
    c.AF.reset(f0)
    c.AB.reset(b0)
    set_fence(c)


def set_fence(c):
    c.AB.fence = c.AF.fence = (
        (c.s_pe, c.s_pe.val), (c.s_dve, c.s_dve.val),
        (c.s_act, c.s_act.val), (c.s_pool, c.s_pool.val))


Carver.fence = ()


def phase_load_x(c, seq):
    P = c.P
    waits = tuple(t for b in c.xres for t in b.war()) + ((c.s_x, c.s_x.val),)
    tok = E(P, "sync", c.s_x,
            lambda e,
            o=c.XRES.ap()[:, :].rearrange("p (cc t) -> p cc t", cc=NCC),
            i=c.inp["x"][seq].rearrange("(cc p) t -> p cc t", p=128):
            e.dma_start(o, i), waits=waits, k=16)
    for b in c.xres:
        b.wrote(tok)


def phase_store(c, seq):
    P = c.P
    waits = (tuple(t for b in c.xres for t in b.raw())
             + ((c.s_out, c.s_out.val),))
    tok = E(P, "sync", c.s_out,
            lambda e,
            o=c.out[seq].rearrange("(cc p) t -> p cc t", p=128),
            i=c.XRES.ap()[:, :].rearrange("p (cc t) -> p cc t", cc=NCC):
            e.dma_start(o, i), waits=waits, k=16)
    for b in c.xres:
        b.read_by(tok)


def build_program(stage_shapes, n_blocks=4):
    nc = bass.Bass("TRN2", target_bir_lowering=False)
    c = make_ctx(nc, stage_shapes, n_blocks)
    c.WRA = WRing(c, Carver(nc.alloc_sbuf_tensor("WRA", [128, 6144], BF16),
                            6144, "WRA"), 2048, 3, slot0=0)
    c.WRB = WRing(c, Carver(nc.alloc_sbuf_tensor("WRB", [128, 2048], BF16),
                            2048, "WRB"), 1024, 2, slot0=3)
    build_prologue(c)
    for _e in ENGINES:
        c.P.wait(_e, *c.tok_init)
    E(c.P, "scalar", c.s_act, lambda e: e.activation(
        c.DUMMY.ap()[:, :], c.ONESC.ap()[:, :], AF.Square))
    phases = [phase_l0attn]
    if n_blocks > 1:
        phases.append(phase_l0ffn)
    if n_blocks > 2:
        phases.append(phase_l1attn)
    if n_blocks > 3:
        phases.append(phase_l1ffn)
    for seq in range(NB):
        phase_load_x(c, seq)
        for ph in phases:
            ph(c)
        phase_store(c, seq)
    P = c.P
    P.on("sync", lambda e, s=c.s_out, v=c.s_out.val: e.wait_ge(s.h, v))
    P.finalize()
    return nc, c


_BUILT = {}


def kernel(**inputs):
    percore = stage_inputs_per_core(inputs)
    shapes = {k: v.shape for k, v in percore[0].items()}
    key = tuple(sorted((k, tuple(s)) for k, s in shapes.items()))
    nc, _ = build_program(shapes, n_blocks=N_BLOCKS)
    res = run_bass_kernel_spmd(nc, percore, core_ids=list(range(8)))
    out = np.empty((16, 2048, 1024), np.float32)
    for i in range(8):
        y = res.results[i]["y"]
        out[2 * i:2 * i + 2] = np.asarray(y, np.float32).transpose(0, 2, 1)
    return out


N_BLOCKS = 4


def phase_l0ffn(c):
    P = c.P
    f0, b0 = c.AF.mark(), c.AB.mark()
    XN = [c.AB.buf(512) for _ in range(NCC)]
    SQB = c.AB.buf(512)
    TMPB = c.AB.buf(512)
    H = [c.AB.buf(512) for _ in range(32)]
    CB = [c.AB.buf(512) for _ in range(2)]
    TMPF = c.AB.buf(512)
    OH8 = Buf(c.AB.take(1024, rows=8))
    GSR = Buf(c.AF.take(512, rows=32))
    GT = Buf(c.AF.take(32))
    WEF8 = Buf(c.AB.take(512, rows=8))
    PRB = Buf(c.AF.take(8))
    M8 = Buf(c.AF.take(8))
    MSK = Buf(c.AF.take(8))
    W8B = c.AB.buf(8)
    RM = Buf(c.AF.take(4))
    for b in (XN + H + CB + [SQB, TMPB, TMPF, OH8, GSR, GT, WEF8, PRB, M8,
                             MSK, W8B, RM]):
        b.ready = list(c.AB.fence)
    _t = E(P, "sync", c.s_ld, lambda e, d=OH8.ap, s=c.inp["c_oh8"]:
           e.dma_start(d, s), waits=OH8.war() + ((c.s_ld, c.s_ld.val),), k=16)
    OH8.wrote(_t)

    def expert_cfg(e):
        if e < 8:
            return (c.scr["w_ew1"][e], c.scr["w_ew2"][e],
                    c.prol["w_ew1"], c.prol["w_ew2"],
                    c.cols["c0_eb1"].ap()[:, e * 32:(e + 1) * 32],
                    c.cols["c0_eb2"].ap()[:, e * 8:(e + 1) * 8], False)
        return (c.scr["w_shw1"], c.scr["w_shw2"],
                c.prol["w_shw1"], c.prol["w_shw2"],
                c.cols["c0_shb1"].ap()[:, :],
                c.cols["c0_shb2"].ap()[:, :], True)

    for tt in range(T_SEQ // 512):
        t0 = tt * 512
        emit_preln(c, t0, 512, XN, SQB, TMPB)
        for mc in range(NCC):
            xap = c.XRES.ap()[:, mc * T_SEQ + t0: mc * T_SEQ + t0 + 512]
            ta = E(P, "vector", c.s_dve,
                   lambda e, o=xap, a=xap, b=XN[mc].ap:
                   e.tensor_tensor(o, a, b, OP.add),
                   waits=XN[mc].raw() + c.xres[mc].war())
            c.xres[mc].add_write(ta)
            XN[mc].read_by(ta)
        tok = None
        for kc in range(NCC):
            tok = E(P, "tensor", c.s_pe,
                    lambda e, o=c.PS[2].ap()[0:8, :],
                    l=c.GATE0.ap()[:, kc * 8:(kc + 1) * 8], r=XN[kc].ap,
                    s=(kc == 0), p=(kc == NCC - 1):
                    e.matmul(o, l, r, start=s, stop=p),
                    waits=XN[kc].raw() + (c.PB[2].war() if kc == 0 else ()))
            XN[kc].read_by(tok)
        c.PB[2].wrote(tok)
        tg = E(P, "scalar", c.s_act,
               lambda e, o=GSR.ap[0:8, :], i=c.PS[2].ap()[0:8, :],
               b=c.GATEB0.ap()[:, :]:
               e.activation(o, i, AF.Identity, bias=b),
               waits=c.PB[2].raw() + GSR.war() + c.tok_init)
        GSR.wrote(tg)
        c.PB[2].read_by(tg)
        for tb in range(4):
            ttr = E(P, "tensor", c.s_pe,
                    lambda e, o=c.PS[3].ap()[:, 0:32],
                    i=GSR.ap[0:32, tb * 128:(tb + 1) * 128],
                    ident=c.IDENTF.ap()[0:32, 0:32]:
                    e.transpose(o, i, ident),
                    waits=GSR.raw() + c.PB[3].war() + c.tok_init)
            c.PB[3].wrote(ttr)
            GSR.read_by(ttr)
            te = E(P, "vector", c.s_dve,
                   lambda e, o=GT.ap[:, tb * 8:(tb + 1) * 8],
                   i=c.PS[3].ap()[:, 0:8]:
                   e.tensor_copy(o, i),
                   waits=c.PB[3].raw() + (GT.war() if tb == 0 else ()))
            (GT.wrote if tb == 0 else GT.add_write)(te)
            c.PB[3].read_by(te)
        for tb in range(4):
            gtb = GT.ap[:, tb * 8:(tb + 1) * 8]
            t1 = E(P, "vector", c.s_dve,
                   lambda e, o=RM.ap[:, 0:1], i=gtb:
                   e.reduce_max(o, i, axis=AX), waits=GT.raw() + RM.war())
            t2 = E(P, "vector", c.s_dve,
                   lambda e, o=RM.ap[:, 1:2], i=RM.ap[:, 0:1]:
                   e.tensor_scalar(o, i, -1.0, None, OP.mult))
            RM.wrote(t2)
            GT.read_by(t1)
            t3 = E(P, "scalar", c.s_act,
                   lambda e, o=PRB.ap, i=gtb, b=RM.ap[:, 1:2],
                   a=RM.ap[:, 2:3]:
                   e.activation(o, i, AF.Exp, bias=b, accum_out=a),
                   waits=(t2,) + PRB.war())
            PRB.wrote(t3)
            RM.add_write(t3)
            t4 = E(P, "vector", c.s_dve,
                   lambda e, o=M8.ap, i=PRB.ap: e.max(o, i),
                   waits=PRB.raw() + M8.war())
            M8.wrote(t4)
            PRB.read_by(t4)
            t5 = E(P, "vector", c.s_dve,
                   lambda e, o=MSK.ap, i=PRB.ap, s=M8.ap[:, 1:2]:
                   e.tensor_single_scalar(o, i, s, OP.is_ge),
                   waits=M8.raw() + MSK.war(), hard=True)
            MSK.wrote(t5)
            M8.read_by(t5)
            t6 = E(P, "vector", c.s_dve,
                   lambda e, o=MSK.ap, a=PRB.ap, b=MSK.ap:
                   e.tensor_tensor(o, a, b, OP.mult))
            MSK.wrote(t6)
            PRB.read_by(t6)
            t7 = E(P, "vector", c.s_dve,
                   lambda e, o=RM.ap[:, 3:4], i=MSK.ap:
                   e.tensor_reduce(o, i, AX, OP.add))
            RM.add_write(t7)
            t8 = E(P, "vector", c.s_dve,
                   lambda e, o=RM.ap[:, 0:1], a=RM.ap[:, 2:3], b=RM.ap[:, 3:4]:
                   e.scalar_tensor_tensor(o, a, 1e-9, b, OP.mult, OP.add),
                   hard=True)
            t9 = E(P, "vector", c.s_dve,
                   lambda e, o=RM.ap[:, 1:2], i=RM.ap[:, 0:1]:
                   e.reciprocal(o, i))
            RM.wrote(t9)
            t10 = E(P, "vector", c.s_dve,
                    lambda e, o=W8B.ap, i=MSK.ap, s=RM.ap[:, 1:2]:
                    e.tensor_scalar(o, i, s, None, OP.mult),
                    waits=W8B.war(), hard=True)
            W8B.wrote(t10)
            MSK.read_by(t10)
            RM.read_by(t10)
            tw = E(P, "tensor", c.s_pe,
                   lambda e, o=c.PS[3].ap().bitcast(BF16)[0:8, 0:128],
                   i=W8B.ap, ident=c.IDENT.ap()[:, :]:
                   e.transpose(o, i, ident),
                   waits=W8B.raw() + c.PB[3].war())
            c.PB[3].wrote(tw)
            W8B.read_by(tw)
            twe = E(P, "vector", c.s_dve,
                    lambda e, o=WEF8.ap[0:8, tb * 128:(tb + 1) * 128],
                    i=c.PS[3].ap().bitcast(BF16)[0:8, 0:128]:
                    e.tensor_copy(o, i),
                    waits=c.PB[3].raw() + (WEF8.war() if tb == 0 else ()))
            (WEF8.wrote if tb == 0 else WEF8.add_write)(twe)
            c.PB[3].read_by(twe)
        for ex in range(9):
            w1s, w2s, p1, p2, b1ap, b2ap, shared = expert_cfg(ex)
            if not shared:
                tcb = E(P, "tensor", c.s_pe,
                        lambda e, o=c.PS[0].ap()[:, :],
                        l=OH8.ap[0:8, ex * 128:(ex + 1) * 128],
                        r=WEF8.ap[0:8, :]:
                        e.matmul(o, l, r, start=True, stop=True),
                        waits=OH8.raw() + WEF8.raw() + c.PB[0].war())
                c.PB[0].wrote(tcb)
                OH8.read_by(tcb)
                WEF8.read_by(tcb)
                cb = CB[ex % 2]
                tcv = E(P, "scalar", c.s_act,
                        lambda e, o=cb.ap, i=c.PS[0].ap()[:, :]:
                        e.activation(o, i, AF.Identity),
                        waits=c.PB[0].raw() + cb.war())
                cb.wrote(tcv)
                c.PB[0].read_by(tcv)
            for sup in range(16):
                wb = c.WRA.load(
                    w1s.rearrange("(kc p) m -> p kc m", p=128)
                    [:, :, sup * 256:(sup + 1) * 256], cols=2048, extra=(p1,))
                for mt in range(2):
                    it = sup * 2 + mt
                    psx, pbx = ((c.PS[4], c.PB[4]) if it % 2 == 0
                                else (c.PS[5], c.PB[5]))
                    tokm = None
                    for kc in range(NCC):
                        tokm = E(P, "tensor", c.s_pe,
                                 lambda e, o=psx.ap()[:, :],
                                 l=wb.ap[:, kc * 256 + mt * 128:
                                         kc * 256 + mt * 128 + 128],
                                 r=XN[kc].ap, s=(kc == 0), p=(kc == NCC - 1):
                                 e.matmul(o, l, r, start=s, stop=p),
                                 waits=wb.raw() + XN[kc].raw()
                                 + (pbx.war() if kc == 0 else ()))
                        wb.read_by(tokm)
                        XN[kc].read_by(tokm)
                    pbx.wrote(tokm)
                    th = E(P, "scalar", c.s_act,
                           lambda e, o=H[it].ap, i=psx.ap()[:, :],
                           b=b1ap[:, it:it + 1]:
                           e.activation(o, i, AF.Gelu_apprx_tanh, bias=b),
                           waits=pbx.raw() + H[it].war() + c.tok_init)
                    H[it].wrote(th)
                    pbx.read_by(th)
            for db in range(4):
                toks = [None, None]
                for kq in range(4):
                    wb2 = c.WRA.load(
                        w2s.rearrange("(kc p) m -> p kc m", p=128)
                        [:, kq * 8:(kq + 1) * 8, db * 256:(db + 1) * 256],
                        cols=2048, extra=(p2,))
                    for dt in range(2):
                        psx, pbx = ((c.PS[6], c.PB[6]) if dt == 0
                                    else (c.PS[7], c.PB[7]))
                        for k8 in range(NCC):
                            kc = kq * 8 + k8
                            tokm = E(P, "tensor", c.s_pe,
                                     lambda e, o=psx.ap()[:, :],
                                     l=wb2.ap[:, k8 * 256 + dt * 128:
                                              k8 * 256 + dt * 128 + 128],
                                     r=H[kc].ap,
                                     s=(kc == 0), p=(kc == 31):
                                     e.matmul(o, l, r, start=s, stop=p),
                                     waits=wb2.raw() + H[kc].raw()
                                     + (pbx.war() if kc == 0 else ()))
                            wb2.read_by(tokm)
                            H[kc].read_by(tokm)
                            toks[dt] = tokm
                for dt in range(2):
                    mc = db * 2 + dt
                    psx, pbx = ((c.PS[6], c.PB[6]) if dt == 0
                                else (c.PS[7], c.PB[7]))
                    pbx.wrote(toks[dt])
                    xap = c.XRES.ap()[:, mc * T_SEQ + t0: mc * T_SEQ + t0 + 512]
                    if shared:
                        tev = E(P, "vector", c.s_dve,
                                lambda e, o=xap, a=psx.ap()[:, :],
                                s=b2ap[:, mc:mc + 1], b=xap:
                                e.scalar_tensor_tensor(o, a, s, b,
                                                       OP.add, OP.add),
                                waits=pbx.raw() + c.xres[mc].war()
                                + c.tok_init)
                        c.xres[mc].add_write(tev)
                        pbx.read_by(tev)
                    else:
                        cb = CB[ex % 2]
                        tev = E(P, "vector", c.s_dve,
                                lambda e, o=TMPF.ap, a=psx.ap()[:, :],
                                s=b2ap[:, mc:mc + 1], b=cb.ap:
                                e.scalar_tensor_tensor(o, a, s, b,
                                                       OP.add, OP.mult),
                                waits=pbx.raw() + cb.raw() + TMPF.war()
                                + c.tok_init)
                        TMPF.wrote(tev)
                        pbx.read_by(tev)
                        cb.read_by(tev)
                        tad = E(P, "vector", c.s_dve,
                                lambda e, o=xap, a=xap, b=TMPF.ap:
                                e.tensor_tensor(o, a, b, OP.add),
                                waits=c.xres[mc].war())
                        c.xres[mc].add_write(tad)
                        TMPF.read_by(tad)
    c.dbgf = dict(XN=XN, H=H, CB=CB, GT=GT, WEF8=WEF8, GSR=GSR, OH8=OH8)
    c.AF.reset(f0)
    c.AB.reset(b0)
    set_fence(c)


def phase_l1attn(c):
    P = c.P
    f0, b0 = c.AF.mark(), c.AB.mark()
    SC = [Buf(c.AF.take(2048)) for _ in range(2)]
    RM = [Buf(c.AF.take(4)) for _ in range(2)]
    KFr = Buf(c.AB.take(16384))
    bA = c.AB.mark()
    XN = [c.AB.buf(512 * 4) for _ in range(NCC)]
    SQB = c.AB.buf(512)
    TMPB = c.AB.buf(512)
    BVT = c.AB.buf(1024)
    STG = [c.AB.buf(512) for _ in range(2)]
    bA2 = c.AB.mark()
    c.AB.reset(bA)
    VFr = Buf(c.AB.take(16384))
    QR = [c.AB.buf(1024) for _ in range(2)]
    PBUF = c.AB.buf(2048)
    PT = c.AB.buf(2048)
    bB2 = c.AB.mark()
    c.AB.reset(max(bA2, bB2))
    partA = XN + [SQB, TMPB, BVT] + STG
    partB = [VFr, PBUF, PT] + QR
    for b in partA:
        b.aliases = partB
    for b in partB:
        b.aliases = partA
    s_kv = [P.sem(f"s_kv{len(P.sems)}") for _ in range(10)]
    for b in (XN + STG + QR + SC + RM + [SQB, TMPB, BVT, PBUF, PT, KFr, VFr]):
        b.ready = list(c.AB.fence)
    qtoks = [[] for _ in range(4)]
    kstores = []
    vstores = []

    for half in range(2):
        tb0 = E(P, "tensor", c.s_pe,
                lambda e, o=c.PS[0].ap()[:, :], l=c.ONESR.ap()[:, :],
                r=c.BVROW.ap()[0:1, half * 512:(half + 1) * 512]:
                e.matmul(o, l, r, start=True, stop=True),
                waits=c.tok_init + c.PB[0].war())
        c.PB[0].wrote(tb0)
        tb1 = E(P, "vector", c.s_dve,
                lambda e, o=BVT.ap[:, half * 512:(half + 1) * 512],
                i=c.PS[0].ap()[:, :]: e.tensor_copy(o, i),
                waits=c.PB[0].raw() + (BVT.war() if half == 0 else ()))
        (BVT.wrote if half == 0 else BVT.add_write)(tb1)
        c.PB[0].read_by(tb1)

    emit_preln(c, 0, T_SEQ, XN, SQB, TMPB)

    def proj(wname, bias_col, out_cb, kstore):
        for sup in range(4):
            wb = c.WRA.load(wsup(c, wname, sup), cols=2048,
                            extra=(c.prol[wname],))
            for mt in range(2):
                mc = sup * 2 + mt
                for nb in range(4):
                    psx, pbx = ((c.PS[4], c.PB[4]) if nb % 2 == 0
                                else (c.PS[5], c.PB[5]))
                    tokm = None
                    for kc in range(NCC):
                        tokm = E(P, "tensor", c.s_pe,
                                 lambda e, o=psx.ap()[:, :],
                                 l=wb.ap[:, kc * 256 + mt * 128:
                                         kc * 256 + mt * 128 + 128],
                                 r=XN[kc].ap[:, nb * 512:(nb + 1) * 512],
                                 s=(kc == 0), p=(kc == NCC - 1):
                                 e.matmul(o, l, r, start=s, stop=p),
                                 waits=wb.raw() + XN[kc].raw()
                                 + (pbx.war() if kc == 0 else ()))
                        wb.read_by(tokm)
                        XN[kc].read_by(tokm)
                    pbx.wrote(tokm)
                    out_cb(mc, nb, psx, pbx)

    def q_out(mc, nb, psx, pbx):
        st = STG[nb % 2]
        tq = E(P, "scalar", c.s_act,
               lambda e, o=st.ap, i=psx.ap()[:, :],
               b=c.cols["c1_bq"].ap()[:, mc:mc + 1]:
               e.activation(o, i, AF.Identity, bias=b),
               waits=pbx.raw() + st.war() + c.tok_init)
        st.wrote(tq)
        pbx.read_by(tq)
        sem = s_kv[nb % 2]
        td = E(P, "sync", sem,
               lambda e, o=c.QS[mc * 128:(mc + 1) * 128,
                              nb * 512:(nb + 1) * 512], i=st.ap:
               e.dma_start(o, i),
               waits=st.raw() + ((sem, sem.val),), k=16)
        st.read_by(td)
        qtoks[nb].append(td)

    def k_out(mc, nb, psx, pbx):
        st = STG[nb % 2]
        tk = E(P, "scalar", c.s_act,
               lambda e, o=st.ap, i=psx.ap()[:, :],
               b=c.cols["c1_bk"].ap()[:, mc:mc + 1]:
               e.activation(o, i, AF.Identity, bias=b),
               waits=pbx.raw() + st.war() + c.tok_init)
        st.wrote(tk)
        pbx.read_by(tk)
        sem = s_kv[2 + nb % 2]
        td = E(P, "sync", sem,
               lambda e, o=c.KS[mc * 128:(mc + 1) * 128,
                              nb * 512:(nb + 1) * 512], i=st.ap:
               e.dma_start(o, i),
               waits=st.raw() + ((sem, sem.val),), k=16)
        st.read_by(td)
        kstores.append(td)

    proj("w_1Wq", None, q_out, False)
    proj("w_1Wk", None, k_out, False)
    tkf = E(P, "sync", s_kv[6],
            lambda e, o=KFr.ap.rearrange("p (kc t) -> p kc t", kc=8),
            i=c.KS.rearrange("(kc p) t -> p kc t", p=128):
            e.dma_start(o, i),
            waits=KFr.war() + tuple(kstores) + ((s_kv[6], 0),), k=16)
    KFr.wrote(tkf)

    for sup in range(4):
        wb = c.WRA.load(wsup(c, "w_1Wv", sup), cols=2048,
                        extra=(c.prol["w_1Wv"],))
        for tb in range(16):
            psx, pbx = ((c.PS[4], c.PB[4]) if tb % 2 == 0
                        else (c.PS[5], c.PB[5]))
            tokm = None
            for kc in range(NCC):
                tokm = E(P, "tensor", c.s_pe,
                         lambda e, o=psx.ap()[:, 0:256],
                         l=XN[kc].ap[:, tb * 128:(tb + 1) * 128],
                         r=wb.ap[:, kc * 256:(kc + 1) * 256],
                         s=(kc == 0), p=(kc == NCC - 1):
                         e.matmul(o, l, r, start=s, stop=p),
                         waits=wb.raw() + XN[kc].raw()
                         + (pbx.war() if kc == 0 else ()))
                wb.read_by(tokm)
                XN[kc].read_by(tokm)
            pbx.wrote(tokm)
            st = STG[tb % 2]
            tv = E(P, "vector", c.s_dve,
                   lambda e, o=st.ap[:, 0:256], a=psx.ap()[:, 0:256],
                   b=BVT.ap[:, sup * 256:(sup + 1) * 256]:
                   e.tensor_tensor(o, a, b, OP.add),
                   waits=pbx.raw() + BVT.raw() + st.war())
            st.wrote(tv)
            pbx.read_by(tv)
            BVT.read_by(tv)
            sem = s_kv[4 + tb % 2]
            td = E(P, "sync", sem,
                   lambda e, o=c.VS[tb * 128:(tb + 1) * 128,
                                    sup * 256:(sup + 1) * 256],
                   i=st.ap[:, 0:256]:
                   e.dma_start(o, i),
                   waits=st.raw() + ((sem, sem.val),), k=16)
            st.read_by(td)
            vstores.append(td)

    vftoks = []
    for ck in range(4):
        tvf = E(P, "sync", s_kv[7],
                lambda e,
                o=VFr.ap[:, ck * 4096:(ck + 1) * 4096].rearrange(
                    "p (tb e) -> p tb e", e=1024),
                i=c.VS.rearrange("(tb p) e -> p tb e", p=128)
                [:, ck * 4:(ck + 1) * 4, :]:
                e.dma_start(o, i),
                waits=(VFr.war() + tuple(vstores) if ck == 0
                       else ()) + ((s_kv[7], s_kv[7].val),), k=16)
        (VFr.wrote if ck == 0 else VFr.add_write)(tvf)
        vftoks.append(tvf)

    def load_q(qc):
        qb = QR[qc % 2]
        sem = s_kv[8 + qc % 2]
        tql = E(P, "sync", sem,
                lambda e, o=qb.ap.rearrange("p (mc t) -> p mc t", t=128),
                i=c.QS.rearrange("(mc p) t -> p mc t", p=128)
                [:, :, qc * 128:(qc + 1) * 128]:
                e.dma_start(o, i),
                waits=qb.war() + tuple(qtoks[qc // 4])
                + ((sem, sem.val),), k=16)
        qb.wrote(tql)
    load_q(0)
    load_q(1)

    for qc in range(16):
        kr = (qc + 1) * 128
        nkb = (kr + 511) // 512
        qb = QR[qc % 2]
        SCq, RMq = SC[qc % 2], RM[qc % 2]
        for nb in range(nkb):
            w = min(512, kr - nb * 512)
            psx, pbx = ((c.PS[4], c.PB[4]) if nb % 2 == 0
                        else (c.PS[5], c.PB[5]))
            tokm = None
            for kc in range(NCC):
                tokm = E(P, "tensor", c.s_pe,
                         lambda e, o=psx.ap()[:, 0:w],
                         l=qb.ap[:, kc * 128:(kc + 1) * 128],
                         r=KFr.ap[:, kc * 2048 + nb * 512:
                                  kc * 2048 + nb * 512 + w],
                         s=(kc == 0), p=(kc == NCC - 1):
                         e.matmul(o, l, r, start=s, stop=p),
                         waits=KFr.raw() + qb.raw()
                         + (pbx.war() if kc == 0 else ()))
                KFr.read_by(tokm)
                qb.read_by(tokm)
            pbx.wrote(tokm)
            tse = E(P, "scalar", c.s_act,
                    lambda e, o=SCq.ap[:, nb * 512:nb * 512 + w],
                    i=psx.ap()[:, 0:w]: e.activation(o, i, AF.Identity),
                    waits=pbx.raw() + (SCq.war() if nb == 0 else ()))
            (SCq.wrote if nb == 0 else SCq.add_write)(tse)
            pbx.read_by(tse)
        if qc + 2 < 16:
            load_q(qc + 2)
        tmk = E(P, "vector", c.s_dve,
                lambda e, o=SCq.ap[:, qc * 128:kr], a=SCq.ap[:, qc * 128:kr],
                b=c.CAUS.ap()[:, :]:
                e.tensor_tensor(o, a, b, OP.add),
                waits=SCq.raw() + c.tok_init)
        SCq.add_write(tmk)
        t1 = E(P, "vector", c.s_dve,
               lambda e, o=RMq.ap[:, 0:1], i=SCq.ap[:, 0:kr]:
               e.reduce_max(o, i, axis=AX), waits=SCq.raw() + RMq.war())
        t2 = E(P, "vector", c.s_dve,
               lambda e, o=RMq.ap[:, 1:2], i=RMq.ap[:, 0:1]:
               e.tensor_scalar(o, i, -1.0 / 32, None, OP.mult))
        RMq.wrote(t2)
        SCq.read_by(t1)
        t3 = E(P, "scalar", c.s_act,
               lambda e, o=PBUF.ap[:, 0:kr], i=SCq.ap[:, 0:kr],
               b=RMq.ap[:, 1:2], a=RMq.ap[:, 2:3]:
               e.activation(o, i, AF.Exp, bias=b, scale=1.0 / 32, accum_out=a),
               waits=(t2,) + PBUF.war() + SCq.raw())
        PBUF.wrote(t3)
        RMq.add_write(t3)
        SCq.read_by(t3)
        t4 = E(P, "vector", c.s_dve,
               lambda e, o=RMq.ap[:, 3:4], i=RMq.ap[:, 2:3]:
               e.reciprocal(o, i), waits=RMq.raw())
        RMq.add_write(t4)
        t5 = E(P, "vector", c.s_dve,
               lambda e, o=PBUF.ap[:, 0:kr], i=PBUF.ap[:, 0:kr],
               s=RMq.ap[:, 3:4]:
               e.tensor_scalar(o, i, s, None, OP.mult),
               waits=PBUF.raw() + (t4,), hard=True)
        PBUF.wrote(t5)
        RMq.read_by(t5)
        for kb in range(qc + 1):
            psb = c.PS[1] if kb % 2 == 0 else c.PS[2]
            pbb = c.PB[1] if kb % 2 == 0 else c.PB[2]
            ttr = E(P, "tensor", c.s_pe,
                    lambda e, o=psb.ap().bitcast(BF16)[:, 0:128],
                    i=PBUF.ap[:, kb * 128:(kb + 1) * 128],
                    ident=c.IDENT.ap()[:, :]:
                    e.transpose(o, i, ident),
                    waits=PBUF.raw() + pbb.war() + c.tok_init)
            pbb.wrote(ttr)
            PBUF.read_by(ttr)
            if kb % 2 == 0:
                tte = E(P, "vector", c.s_dve,
                        lambda e, o=PT.ap[:, kb * 128:(kb + 1) * 128],
                        i=psb.ap().bitcast(BF16)[:, 0:128]:
                        e.tensor_copy(o, i),
                        waits=pbb.raw() + (PT.war() if kb == 0 else ()))
            else:
                tte = E(P, "scalar", c.s_act,
                        lambda e, o=PT.ap[:, kb * 128:(kb + 1) * 128],
                        i=psb.ap().bitcast(BF16)[:, 0:128]:
                        e.activation(o, i, AF.Identity),
                        waits=pbb.raw())
            (PT.wrote if kb == 0 else PT.add_write)(tte)
            pbb.read_by(tte)
        toks = [None, None]
        for mt in range(8):
            psx, pbx = ((c.PS[6], c.PB[6]) if mt < 4 else (c.PS[7], c.PB[7]))
            col = (mt % 4) * 128
            for kb in range(qc + 1):
                tokm = E(P, "tensor", c.s_pe,
                         lambda e, o=psx.ap()[:, col:col + 128],
                         l=VFr.ap[:, kb * 1024 + mt * 128:
                                  kb * 1024 + mt * 128 + 128],
                         r=PT.ap[:, kb * 128:(kb + 1) * 128],
                         s=(kb == 0), p=(kb == qc):
                         e.matmul(o, l, r, start=s, stop=p),
                         waits=(vftoks[kb // 4],) + PT.raw()
                         + (pbx.war() if (kb == 0 and mt % 4 == 0) else ()))
                VFr.read_by(tokm)
                PT.read_by(tokm)
            toks[mt // 4] = tokm
        for half in range(2):
            psx, pbx = ((c.PS[6], c.PB[6]) if half == 0 else (c.PS[7], c.PB[7]))
            pbx.wrote(toks[half])
            cc0 = half * 4
            base = cc0 * T_SEQ + qc * 128
            oview = (c.XRES.ap()[:, base:base + 3 * T_SEQ + 128]
                     .rearrange("p (g f) -> p g f", g=4)
                     [:, :, 0:128] if False else
                     c.XRES.ap()[:, cc0 * T_SEQ + qc * 128:
                                 cc0 * T_SEQ + qc * 128 + 3 * T_SEQ + 128])
            ovw = (c.XRES.ap()[:, cc0 * T_SEQ:(cc0 + 4) * T_SEQ]
                   .rearrange("p (g f) -> p g f", g=4)
                   [:, :, qc * 128:(qc + 1) * 128])
            waits = (pbx.raw()
                     + tuple(t for b in c.xres[cc0:cc0 + 4] for t in b.war()))
            tev = E(P, "vector", c.s_dve,
                    lambda e, o=ovw,
                    a=ovw,
                    b=psx.ap()[:, :].rearrange("p (g f) -> p g f", g=4):
                    e.tensor_tensor(o, a, b, OP.add), waits=waits)
            for b in c.xres[cc0:cc0 + 4]:
                b.add_write(tev)
            pbx.read_by(tev)
    c.AF.reset(f0)
    c.AB.reset(b0)
    set_fence(c)


def phase_l1ffn(c):
    P = c.P
    f0, b0 = c.AF.mark(), c.AB.mark()
    XN = [c.AB.buf(2048) for _ in range(NCC)]
    SQB = c.AB.buf(512)
    TMPB = c.AB.buf(512)
    SGF = Buf(c.AF.take(2048, rows=32))
    SGT = Buf(c.AF.take(256))
    WEF = Buf(c.AB.take(1024, rows=16))
    OH16 = Buf(c.AB.take(2048, rows=16))
    GA = c.AB.buf(4096)
    GB = c.AB.buf(4096)
    UU = c.AB.buf(4096)
    CBp = [c.AB.buf(2048) for _ in range(2)]
    W16B = c.AB.buf(16)
    GRP = Buf(c.AF.take(64))
    M8A = Buf(c.AF.take(8))
    MS16 = Buf(c.AF.take(16))
    RC = Buf(c.AF.take(4))
    for b in (XN + CBp + [SQB, TMPB, SGF, SGT, WEF, OH16, GA, GB, UU, W16B,
                          GRP, M8A, MS16, RC]):
        b.ready = list(c.AB.fence)
    _t = E(P, "sync", c.s_ld, lambda e, d=OH16.ap, s=c.inp["c_oh16"]:
           e.dma_start(d, s), waits=OH16.war() + ((c.s_ld, c.s_ld.val),), k=16)
    OH16.wrote(_t)
    emit_preln(c, 0, T_SEQ, XN, SQB, TMPB)
    for nb in range(4):
        tok = None
        for kc in range(NCC):
            tok = E(P, "tensor", c.s_pe,
                    lambda e, o=c.PS[2].ap()[0:16, :],
                    l=c.GATE1.ap()[:, kc * 16:(kc + 1) * 16],
                    r=XN[kc].ap[:, nb * 512:(nb + 1) * 512],
                    s=(kc == 0), p=(kc == NCC - 1):
                    e.matmul(o, l, r, start=s, stop=p),
                    waits=XN[kc].raw() + (c.PB[2].war() if kc == 0 else ())
                    + c.tok_init)
            XN[kc].read_by(tok)
        c.PB[2].wrote(tok)
        tg = E(P, "scalar", c.s_act,
               lambda e, o=SGF.ap[0:16, nb * 512:(nb + 1) * 512],
               i=c.PS[2].ap()[0:16, :]:
               e.activation(o, i, AF.Sigmoid),
               waits=c.PB[2].raw() + (SGF.war() if nb == 0 else ()))
        (SGF.wrote if nb == 0 else SGF.add_write)(tg)
        c.PB[2].read_by(tg)

    for half in range(2):
        t0 = half * 1024
        for tb in range(8):
            tbg = half * 8 + tb
            ttr = E(P, "tensor", c.s_pe,
                    lambda e, o=c.PS[3].ap()[:, 0:32],
                    i=SGF.ap[0:32, tbg * 128:(tbg + 1) * 128],
                    ident=c.IDENTF.ap()[0:32, 0:32]:
                    e.transpose(o, i, ident),
                    waits=SGF.raw() + c.PB[3].war() + c.tok_init)
            c.PB[3].wrote(ttr)
            SGF.read_by(ttr)
            te = E(P, "vector", c.s_dve,
                   lambda e, o=SGT.ap[:, tb * 16:(tb + 1) * 16],
                   i=c.PS[3].ap()[:, 0:16]:
                   e.tensor_copy(o, i),
                   waits=c.PB[3].raw() + (SGT.war() if tb == 0 else ()))
            (SGT.wrote if tb == 0 else SGT.add_write)(te)
            c.PB[3].read_by(te)
        for tb in range(8):
            Bap = SGT.ap[:, tb * 16:(tb + 1) * 16]
            g3 = GRP.ap
            x_ = [Bap.rearrange("p (g i) -> p g i", i=4)[:, :, i]
                  for i in range(4)]
            ops = [
                ("s1", OP.max, x_[0], x_[1]), ("t1", OP.min, x_[0], x_[1]),
                ("s2", OP.max, x_[2], x_[3]), ("t2", OP.min, x_[2], x_[3]),
            ]
            cols = {}
            off = 0
            tokl = None
            for nm, op, a, b in ops:
                cols[nm] = g3[:, off:off + 4]
                tokl = E(P, "vector", c.s_dve,
                         lambda e, o=cols[nm], a=a, b=b:
                         e.tensor_tensor(o, a, b, op),
                         waits=SGT.raw() + (GRP.war() if off == 0 else ()))
                SGT.read_by(tokl)
                off += 4
            for nm, op, a, b in (("m1", OP.max, cols["s1"], cols["s2"]),
                                 ("ge", OP.min, cols["s1"], cols["s2"]),
                                 ("gf", OP.max, cols["t1"], cols["t2"])):
                cols[nm] = g3[:, off:off + 4]
                E(P, "vector", c.s_dve,
                  lambda e, o=cols[nm], a=a, b=b: e.tensor_tensor(o, a, b, op))
                off += 4
            cols["m2"] = g3[:, off:off + 4]; off += 4
            E(P, "vector", c.s_dve,
              lambda e, o=cols["m2"], a=cols["ge"], b=cols["gf"]:
              e.tensor_tensor(o, a, b, OP.max))
            cols["gsum"] = g3[:, off:off + 4]; off += 4
            tgs = E(P, "vector", c.s_dve,
                    lambda e, o=cols["gsum"], a=cols["m1"], b=cols["m2"]:
                    e.tensor_tensor(o, a, b, OP.add))
            GRP.wrote(tgs)
            gs = cols["gsum"]
            pp = g3[:, off:off + 2]; off += 2
            qq = g3[:, off:off + 2]; off += 2
            m2g = g3[:, off:off + 1]; off += 1
            E(P, "vector", c.s_dve,
              lambda e, o=pp, a=gs[:, 0::2], b=gs[:, 1::2]:
              e.tensor_tensor(o, a, b, OP.max))
            E(P, "vector", c.s_dve,
              lambda e, o=qq, a=gs[:, 0::2], b=gs[:, 1::2]:
              e.tensor_tensor(o, a, b, OP.min))
            E(P, "vector", c.s_dve,
              lambda e, o=m2g, a=pp[:, 0:1], b=pp[:, 1:2]:
              e.tensor_tensor(o, a, b, OP.min))
            E(P, "vector", c.s_dve,
              lambda e, o=qq[:, 0:1], a=qq[:, 0:1], b=qq[:, 1:2]:
              e.tensor_tensor(o, a, b, OP.max))
            tm2g = E(P, "vector", c.s_dve,
                     lambda e, o=m2g, a=m2g, b=qq[:, 0:1]:
                     e.tensor_tensor(o, a, b, OP.max))
            GRP.add_write(tm2g)
            cols["ag"] = g3[:, off:off + 4]; off += 4
            tag = E(P, "vector", c.s_dve,
                    lambda e, o=cols["ag"], i=cols["gsum"], s=m2g:
                    e.tensor_single_scalar(o, i, s, OP.is_ge),
                    hard=True)
            GRP.add_write(tag)
            tms = None
            for g in range(4):
                tp1 = E(P, "vector", c.s_dve,
                        lambda e, o=MS16.ap[:, g * 4:(g + 1) * 4],
                        i=Bap[:, g * 4:(g + 1) * 4]:
                        e.tensor_scalar(o, i, 1e9, None, OP.add),
                        waits=SGT.raw() + (MS16.war() if g == 0 else ()))
                MS16.add_write(tp1)
                SGT.read_by(tp1)
                tms = E(P, "vector", c.s_dve,
                        lambda e, o=MS16.ap[:, g * 4:(g + 1) * 4],
                        i=MS16.ap[:, g * 4:(g + 1) * 4],
                        s=cols["ag"][:, g:g + 1]:
                        e.tensor_scalar(o, i, s, -1e9, OP.mult, OP.add),
                        waits=GRP.raw(), hard=True)
                MS16.add_write(tms)
                GRP.read_by(tms)
            tmi = E(P, "vector", c.s_dve,
                    lambda e, o=MS16.ap, a=MS16.ap, b=c.IOTAEPS.ap()[:, :]:
                    e.tensor_tensor(o, a, b, OP.add), waits=c.tok_init)
            MS16.wrote(tmi)
            tm84 = E(P, "vector", c.s_dve,
                     lambda e, o=M8A.ap, i=MS16.ap: e.max(o, i),
                     waits=M8A.war())
            M8A.wrote(tm84)
            MS16.read_by(tm84)
            ttk = E(P, "vector", c.s_dve,
                    lambda e, o=MS16.ap, i=MS16.ap, s=M8A.ap[:, 3:4]:
                    e.tensor_single_scalar(o, i, s, OP.is_ge),
                    waits=M8A.raw(), hard=True)
            MS16.wrote(ttk)
            M8A.read_by(ttk)
            tws = E(P, "vector", c.s_dve,
                    lambda e, o=MS16.ap, a=SGT.ap[:, tb * 16:(tb + 1) * 16],
                    b=MS16.ap:
                    e.tensor_tensor(o, a, b, OP.mult))
            MS16.wrote(tws)
            SGT.read_by(tws)
            trs = E(P, "vector", c.s_dve,
                    lambda e, o=RC.ap[:, 0:1], i=MS16.ap:
                    e.tensor_reduce(o, i, AX, OP.add), waits=RC.war())
            trs2 = E(P, "vector", c.s_dve,
                     lambda e, o=RC.ap[:, 1:2], i=RC.ap[:, 0:1]:
                     e.tensor_scalar(o, i, 1e-6, None, OP.add))
            trs3 = E(P, "vector", c.s_dve,
                     lambda e, o=RC.ap[:, 2:3], i=RC.ap[:, 1:2]:
                     e.reciprocal(o, i))
            RC.wrote(trs3)
            tw16 = E(P, "vector", c.s_dve,
                     lambda e, o=W16B.ap, i=MS16.ap, s=RC.ap[:, 2:3]:
                     e.tensor_scalar(o, i, s, None, OP.mult),
                     waits=W16B.war(), hard=True)
            W16B.wrote(tw16)
            MS16.read_by(tw16)
            RC.read_by(tw16)
            twt = E(P, "tensor", c.s_pe,
                    lambda e, o=c.PS[3].ap().bitcast(BF16)[0:16, 0:128],
                    i=W16B.ap, ident=c.IDENT.ap()[:, :]:
                    e.transpose(o, i, ident),
                    waits=W16B.raw() + c.PB[3].war())
            c.PB[3].wrote(twt)
            W16B.read_by(twt)
            twe = E(P, "vector", c.s_dve,
                    lambda e, o=WEF.ap[0:16, tb * 128:(tb + 1) * 128],
                    i=c.PS[3].ap().bitcast(BF16)[0:16, 0:128]:
                    e.tensor_copy(o, i),
                    waits=c.PB[3].raw() + (WEF.war() if tb == 0 else ()))
            (WEF.wrote if tb == 0 else WEF.add_write)(twe)
            c.PB[3].read_by(twe)

        def gu_compute(gbuf, w1s, w3s, p1, p3, cb, kwidth):
            for wname, prol, act in ((w1s, p1, True), (w3s, p3, False)):
                for sup in range(2):
                    wb = c.WRA.load(wname[:, sup * 256:(sup + 1) * 256]
                                    .rearrange("(kc p) m -> p kc m", p=128),
                                    cols=2048, extra=(prol,))
                    for mt in range(2):
                        fc = sup * 2 + mt
                        for nb in range(2):
                            psx, pbx = ((c.PS[4], c.PB[4]) if nb == 0
                                        else (c.PS[5], c.PB[5]))
                            tokm = None
                            for kc in range(NCC):
                                tokm = E(P, "tensor", c.s_pe,
                                         lambda e, o=psx.ap()[:, :],
                                         l=wb.ap[:, kc * 256 + mt * 128:
                                                 kc * 256 + mt * 128 + 128],
                                         r=XN[kc].ap[:, t0 + nb * 512:
                                                     t0 + (nb + 1) * 512],
                                         s=(kc == 0), p=(kc == NCC - 1):
                                         e.matmul(o, l, r, start=s, stop=p),
                                         waits=wb.raw() + XN[kc].raw()
                                         + (pbx.war() if kc == 0 else ()))
                                wb.read_by(tokm)
                                XN[kc].read_by(tokm)
                            pbx.wrote(tokm)
                            if act:
                                tev = E(P, "scalar", c.s_act,
                                        lambda e,
                                        o=gbuf.ap[:, fc * 1024 + nb * 512:
                                                  fc * 1024 + (nb + 1) * 512],
                                        i=psx.ap()[:, :]:
                                        e.activation(o, i, AF.Silu),
                                        waits=pbx.raw()
                                        + (gbuf.war() if fc == 0 and nb == 0
                                           else ()))
                                (gbuf.wrote if (fc == 0 and nb == 0)
                                 else gbuf.add_write)(tev)
                                pbx.read_by(tev)
                            else:
                                tev = E(P, "scalar", c.s_act,
                                        lambda e,
                                        o=UU.ap[:, fc * 1024 + nb * 512:
                                                fc * 1024 + (nb + 1) * 512],
                                        i=psx.ap()[:, :]:
                                        e.activation(o, i, AF.Identity),
                                        waits=pbx.raw()
                                        + (UU.war() if fc == 0 and nb == 0
                                           else ()))
                                (UU.wrote if (fc == 0 and nb == 0)
                                 else UU.add_write)(tev)
                                pbx.read_by(tev)
            for fc in range(4):
                tgu = E(P, "vector", c.s_dve,
                        lambda e, o=gbuf.ap[:, fc * 1024:(fc + 1) * 1024],
                        a=gbuf.ap[:, fc * 1024:(fc + 1) * 1024],
                        b=UU.ap[:, fc * 1024:(fc + 1) * 1024]:
                        e.tensor_tensor(o, a, b, OP.mult),
                        waits=gbuf.raw() + UU.raw())
                gbuf.add_write(tgu)
                UU.read_by(tgu)
                if cb is not None:
                    tcb = E(P, "vector", c.s_dve,
                            lambda e, o=gbuf.ap[:, fc * 1024:(fc + 1) * 1024],
                            a=gbuf.ap[:, fc * 1024:(fc + 1) * 1024],
                            b=cb.ap[:, 0:1024]:
                            e.tensor_tensor(o, a, b, OP.mult),
                            waits=cb.raw())
                    gbuf.add_write(tcb)
                    cb.read_by(tcb)

        def build_cb(e_idx, cb):
            for nb in range(2):
                tcm = E(P, "tensor", c.s_pe,
                        lambda e, o=c.PS[0].ap()[:, :],
                        l=OH16.ap[0:16, e_idx * 128:(e_idx + 1) * 128],
                        r=WEF.ap[0:16, nb * 512:(nb + 1) * 512]:
                        e.matmul(o, l, r, start=True, stop=True),
                        waits=OH16.raw() + WEF.raw() + c.PB[0].war())
                c.PB[0].wrote(tcm)
                OH16.read_by(tcm)
                WEF.read_by(tcm)
                tce = E(P, "scalar", c.s_act,
                        lambda e, o=cb.ap[:, nb * 512:(nb + 1) * 512],
                        i=c.PS[0].ap()[:, :]:
                        e.activation(o, i, AF.Identity),
                        waits=c.PB[0].raw() + (cb.war() if nb == 0 else ()))
                (cb.wrote if nb == 0 else cb.add_write)(tce)
                c.PB[0].read_by(tce)

        for pr in range(9):
            if pr < 8:
                members = []
                for m in range(2):
                    e_idx = pr * 2 + m
                    cb = CBp[m]
                    build_cb(e_idx, cb)
                    gbuf = GA if m == 0 else GB
                    gu_compute(gbuf, c.scr["w_1w1e"][e_idx],
                               c.scr["w_1w3e"][e_idx],
                               c.prol["w_1w1e"], c.prol["w_1w3e"], cb, 512)
                    members.append((gbuf, c.scr["w_1w2e"][e_idx],
                                    c.prol["w_1w2e"]))
            else:
                gu_compute(GA, c.scr["w_1w1s"][:, 0:512],
                           c.scr["w_1w3s"][:, 0:512],
                           c.prol["w_1w1s"], c.prol["w_1w3s"], None, 512)
                gu_compute(GB, c.scr["w_1w1s"][:, 512:1024],
                           c.scr["w_1w3s"][:, 512:1024],
                           c.prol["w_1w1s"], c.prol["w_1w3s"], None, 512)
                members = [(GA, c.scr["w_1w2s"][0:512, :], c.prol["w_1w2s"]),
                           (GB, c.scr["w_1w2s"][512:1024, :],
                            c.prol["w_1w2s"])]
            for sup in range(4):
                wbs = []
                for m, (gbuf, w2s, p2) in enumerate(members):
                    wbs.append(c.WRA.load(
                        w2s[:, sup * 256:(sup + 1) * 256]
                        .rearrange("(kc p) m -> p kc m", p=128),
                        cols=1024, extra=(p2,)))
                for mt in range(2):
                    mc = sup * 2 + mt
                    for nb in range(2):
                        psx, pbx = ((c.PS[6], c.PB[6]) if nb == 0
                                    else (c.PS[7], c.PB[7]))
                        tokm = None
                        nmm = 0
                        for m, (gbuf, w2s, p2) in enumerate(members):
                            wb2 = wbs[m]
                            for kc in range(4):
                                tokm = E(P, "tensor", c.s_pe,
                                         lambda e, o=psx.ap()[:, :],
                                         l=wb2.ap[:, kc * 256 + mt * 128:
                                                  kc * 256 + mt * 128 + 128],
                                         r=gbuf.ap[:, kc * 1024 + nb * 512:
                                                   kc * 1024 + (nb + 1) * 512],
                                         s=(nmm == 0), p=(nmm == 7):
                                         e.matmul(o, l, r, start=s, stop=p),
                                         waits=wb2.raw() + gbuf.raw()
                                         + (pbx.war() if nmm == 0 else ()))
                                wb2.read_by(tokm)
                                gbuf.read_by(tokm)
                                nmm += 1
                        pbx.wrote(tokm)
                        xap = c.XRES.ap()[:, mc * T_SEQ + t0 + nb * 512:
                                          mc * T_SEQ + t0 + (nb + 1) * 512]
                        tev = E(P, "vector", c.s_dve,
                                lambda e, o=xap, a=psx.ap()[:, :], b=xap:
                                e.scalar_tensor_tensor(o, a, 0.0, b,
                                                       OP.add, OP.add),
                                waits=pbx.raw() + c.xres[mc].war())
                        c.xres[mc].add_write(tev)
                        pbx.read_by(tev)
    c.AF.reset(f0)
    c.AB.reset(b0)
    set_fence(c)


# version 12
# speedup vs baseline: 1.3208x; 1.0221x over previous
import numpy as np
import ml_dtypes
import concourse.bass as bass
import concourse.mybir as mybir
from concourse.bass_utils import run_bass_kernel_spmd

F32 = mybir.dt.float32
BF16 = mybir.dt.bfloat16
AF = mybir.ActivationFunctionType
OP = mybir.AluOpType
AX = mybir.AxisListType.X

ENGINES = ("sync", "tensor", "vector", "scalar", "gpsimd")
T_SEQ = 2048
NCC = 8
NB = 2
TT0 = 512
CH = 128
NCH0 = TT0 // CH
NT0 = T_SEQ // TT0
LN_EPS = 1e-5
GN_EPS = 6.4e-4


class Sem:
    __slots__ = ("name", "val", "h", "owner")

    def __init__(self, name, owner=None):
        self.name, self.val, self.h, self.owner = name, 0, None, owner


class Prog:
    def __init__(self, nc):
        self.nc = nc
        self.stream = {e: [] for e in ENGINES}
        self.sems = []
        self.last_wait = {e: {} for e in ENGINES}

    def sem(self, name, owner=None):
        s = Sem(name, owner)
        self.sems.append(s)
        return s

    def on(self, eng, fn):
        self.stream[eng].append(fn)

    def wait(self, eng, *tokens, hard=False, force=False):
        need = {}
        for t in tokens:
            if not t:
                continue
            sem, val = t
            if sem.owner == eng and not hard:
                continue
            if (not force and not globals().get("NO_DEDUP")
                    and val <= self.last_wait[eng].get(sem, 0)):
                continue
            need[sem] = max(need.get(sem, 0), val)
        for sem, val in need.items():
            self.last_wait[eng][sem] = val
            self.on(eng, lambda e, s=sem, v=val: e.wait_ge(s.h, v))

    def finalize(self):
        nc = self.nc
        for s in self.sems:
            s.h = nc.alloc_semaphore(s.name)
        with nc.Block() as block:
            for e in ENGINES:
                ops = self.stream[e]

                def runner(engine, ops=ops):
                    for fn in ops:
                        fn(engine)

                getattr(block, e)(runner)


class Buf:
    __slots__ = ("ap", "ready", "readers", "aliases")

    def __init__(self, ap=None):
        self.ap = ap
        self.ready = []
        self.readers = []
        self.aliases = []

    def war(self):
        t = tuple(self.readers) + tuple(self.ready)
        for a in self.aliases:
            t += tuple(a.readers) + tuple(a.ready)
        return t

    def wrote(self, tok):
        self.ready = [tok] if tok else []
        self.readers = []

    def add_write(self, tok):
        if tok:
            self.ready.append(tok)

    def raw(self):
        return tuple(self.ready)

    def read_by(self, tok):
        if tok:
            self.readers.append(tok)


def alias(*bufs):
    for b in bufs:
        b.aliases = [x for x in bufs if x is not b]


class Carver:
    def __init__(self, tensor, ncols, name):
        self.t, self.n, self.off, self.name = tensor, ncols, 0, name

    def take(self, cols, rows=128):
        assert self.off + cols <= self.n, (
            f"{self.name}: overflow {self.off}+{cols}>{self.n}")
        ap = self.t.ap()[0:rows, self.off:self.off + cols]
        self.off += cols
        return ap

    def buf(self, cols, rows=128):
        b = Buf(self.take(cols, rows))
        b.ready = list(getattr(self, "fence", ()))
        return b

    def mark(self):
        return self.off

    def reset(self, off):
        self.off = off


def _inc(P, sem, k):
    sem.val += k
    return (sem, sem.val)


SIM_SAFE = True
_SELFWAIT_ENGINES = ("vector", "scalar", "gpsimd")


def E(P, eng, sem, build, waits=(), k=1, hard=False, force=False):
    if SIM_SAFE and eng in _SELFWAIT_ENGINES and k == 1:
        waits = tuple(waits) + ((sem, sem.val),)
        hard = True
    P.wait(eng, *waits, hard=hard, force=force)
    tok = _inc(P, sem, k)
    P.on(eng, lambda e, b=build, s=sem, kk=k: b(e).then_inc(s.h, kk))
    return tok


def _pc(v):
    return np.ascontiguousarray(np.asarray(v, np.float32).reshape(8, 128).T)


def _stage_host(inp):
    g = {}
    g["c0_maa_x"] = _pc(inp["L0_attn_time_maa_x"])
    g["c0_maa_w"] = _pc(inp["L0_attn_time_maa_w"])
    g["c0_maa_k"] = _pc(inp["L0_attn_time_maa_k"])
    g["c0_maa_v"] = _pc(inp["L0_attn_time_maa_v"])
    g["c0_maa_r"] = _pc(inp["L0_attn_time_maa_r"])
    g["c0_maa_g"] = _pc(inp["L0_attn_time_maa_g"])
    g["c0_decay"] = _pc(inp["L0_attn_time_decay"])
    g["c0_ln_w"] = _pc(inp["L0_attn_ln_w"])
    g["c0_ln_b"] = _pc(inp["L0_attn_ln_b"])
    g["c0_u"] = _pc(np.asarray(inp["L0_attn_time_faaaa"], np.float32).reshape(1024))
    w1 = np.asarray(inp["L0_attn_time_maa_w1"], np.float32)
    g["w_maa1"] = np.ascontiguousarray(
        w1.reshape(8, 128, 160).transpose(1, 0, 2).reshape(128, 8 * 160))
    w2r = np.asarray(inp["L0_attn_time_maa_w2"], np.float32)
    w2p = np.zeros((5, 128, 1024), np.float32)
    for s in range(5):
        r0 = 32 * s if s < 4 else 0
        w2p[s, r0:r0 + 32] = w2r[s]
    g["w_maa2"] = np.ascontiguousarray(w2p.reshape(640, 1024))
    d1 = np.asarray(inp["L0_attn_time_decay_w1"], np.float32)
    g["w_dw1"] = np.ascontiguousarray(
        d1.reshape(8, 128, 64).transpose(1, 0, 2).reshape(128, 8 * 64))
    g["w_dw2"] = np.asarray(inp["L0_attn_time_decay_w2"], np.float32)
    for nm in ("Wr", "Wk", "Wv", "Wg", "Wo"):
        g["w_" + nm] = np.asarray(inp["L0_attn_" + nm], np.float32)
    gw = np.asarray(inp["L0_ffn_gate_w"], np.float32)
    g["w_gate0"] = np.ascontiguousarray(
        gw.reshape(8, 128, 8).transpose(1, 0, 2).reshape(128, 64))
    g["c0_gate_b"] = np.asarray(inp["L0_ffn_gate_b"], np.float32).reshape(8, 1)
    g["c0_shb1"] = np.ascontiguousarray(
        np.asarray(inp["L0_ffn_sh_b1"], np.float32).reshape(32, 128).T)
    g["c0_shb2"] = _pc(inp["L0_ffn_sh_b2"])
    eb1 = np.asarray(inp["L0_ffn_eb1"], np.float32)
    g["c0_eb1"] = np.ascontiguousarray(
        eb1.reshape(8, 32, 128).transpose(2, 0, 1).reshape(128, 256))
    eb2 = np.asarray(inp["L0_ffn_eb2"], np.float32)
    g["c0_eb2"] = np.ascontiguousarray(
        eb2.reshape(8, 8, 128).transpose(2, 0, 1).reshape(128, 64))
    g["w_shw1"] = np.asarray(inp["L0_ffn_sh_w1"], np.float32)
    g["w_shw2"] = np.asarray(inp["L0_ffn_sh_w2"], np.float32)
    g["w_ew1"] = np.asarray(inp["L0_ffn_ew1"], np.float32)
    g["w_ew2"] = np.asarray(inp["L0_ffn_ew2"], np.float32)
    g["w_1Wq"] = np.ascontiguousarray(np.asarray(inp["L1_attn_Wq"], np.float32).T)
    g["w_1Wk"] = np.ascontiguousarray(np.asarray(inp["L1_attn_Wk"], np.float32).T)
    g["w_1Wv"] = np.ascontiguousarray(np.asarray(inp["L1_attn_Wv"], np.float32).T)
    g["c1_bq"] = _pc(inp["L1_attn_bq"])
    g["c1_bk"] = _pc(inp["L1_attn_bk"])
    g["c1_bv_row"] = np.asarray(inp["L1_attn_bv"], np.float32).reshape(1, 1024)
    wg1 = np.asarray(inp["L1_ffn_w_gate"], np.float32)
    g["w_gate1"] = np.ascontiguousarray(
        wg1.T.reshape(8, 128, 16).transpose(1, 0, 2).reshape(128, 128))
    g["w_1w1e"] = np.ascontiguousarray(
        np.asarray(inp["L1_ffn_w1_e"], np.float32).transpose(0, 2, 1))
    g["w_1w3e"] = np.ascontiguousarray(
        np.asarray(inp["L1_ffn_w3_e"], np.float32).transpose(0, 2, 1))
    g["w_1w2e"] = np.ascontiguousarray(
        np.asarray(inp["L1_ffn_w2_e"], np.float32).transpose(0, 2, 1))
    g["w_1w1s"] = np.ascontiguousarray(np.asarray(inp["L1_ffn_w1_s"], np.float32).T)
    g["w_1w3s"] = np.ascontiguousarray(np.asarray(inp["L1_ffn_w3_s"], np.float32).T)
    g["w_1w2s"] = np.ascontiguousarray(np.asarray(inp["L1_ffn_w2_s"], np.float32).T)
    g["c_ident"] = np.eye(128, dtype=np.float32)
    lt = (np.arange(128)[:, None] < np.arange(128)[None, :]).astype(np.float32)
    g["c_ltm"] = np.ascontiguousarray(np.tile(lt, (1, 4)))
    g["c_caus"] = np.where(np.arange(128)[None, :] <= np.arange(128)[:, None],
                           0.0, -1e30).astype(np.float32)
    oh8 = np.zeros((8, 1024), np.float32)
    for e in range(8):
        oh8[e, e * 128:(e + 1) * 128] = 1.0
    g["c_oh8"] = oh8
    oh16 = np.zeros((16, 2048), np.float32)
    for e in range(16):
        oh16[e, e * 128:(e + 1) * 128] = 1.0
    g["c_oh16"] = oh16
    ind = np.zeros((128, 128), np.float32)
    for cc in range(8):
        for p in range(128):
            ind[p, cc * 16 + 2 * cc + p // 64] = 1.0
    g["c_ind16"] = ind
    indt = np.zeros((16, 1024), np.float32)
    for cc in range(8):
        for p in range(128):
            indt[2 * cc + p // 64, cc * 128 + p] = 1.0
    g["c_ind16t"] = indt
    g["c_iotaeps"] = np.tile((-1e-7 * np.arange(16, dtype=np.float32))[None, :],
                             (128, 1))
    g["c_identf"] = np.eye(128, dtype=np.float32)
    f32names = ("c0_maa_x", "c0_maa_w", "c0_maa_k", "c0_maa_v", "c0_maa_r",
                "c0_maa_g", "c0_decay", "c0_ln_w", "c0_ln_b", "c0_u",
                "c0_shb1", "c0_shb2", "c0_eb1", "c0_eb2", "c1_bq", "c1_bk",
                "c0_gate_b", "c1_bv_row", "c_iotaeps", "c_identf")
    for nm in ("c_ind16", "c_ind16t", "c_oh8", "c_oh16"):
        g[nm] = g[nm].astype(ml_dtypes.bfloat16)
    bfnames = ("c_ident", "c_ltm", "c_caus", "w_maa1", "w_dw1", "w_dw2",
               "w_gate0", "w_gate1")
    def cat(names):
        parts, offs, off = [], {}, 0
        for nm in names:
            a = np.asarray(g[nm], np.float32)
            pad = np.zeros((128, a.shape[1]), np.float32)
            pad[0:a.shape[0]] = a
            parts.append(pad)
            offs[nm] = (off, a.shape[1], a.shape[0])
            off += a.shape[1]
        return np.ascontiguousarray(np.concatenate(parts, 1)), offs
    g["c_init32"], g["_offs32"] = cat(f32names)
    g["c_initbf"], g["_offsbf"] = cat(bfnames)
    return g


INIT_OFFS = {}


def stage_inputs_per_core(inputs):
    shared = _stage_host(inputs)
    INIT_OFFS["f32"] = shared.pop("_offs32")
    INIT_OFFS["bf"] = shared.pop("_offsbf")
    x = np.asarray(inputs["L0_attn_x"], np.float32)
    percore = []
    for i in range(8):
        d = dict(shared)
        d["x"] = np.ascontiguousarray(x[2 * i:2 * i + 2].transpose(0, 2, 1))
        percore.append(d)
    return percore


class Ctx:
    pass


PROLOGUE_ORDER = (
    "w_maa2", "w_Wr", "w_Wk", "w_Wv", "w_Wg", "w_Wo",
    "w_shw1", "w_shw2", "w_ew1", "w_ew2",
    "w_1Wq", "w_1Wk", "w_1Wv",
    "w_1w1s", "w_1w3s", "w_1w2s", "w_1w1e", "w_1w3e", "w_1w2e",
)


def make_ctx(nc, stage_shapes, n_blocks=4):
    c = Ctx()
    c.nc = nc
    c.n_blocks = n_blocks
    P = c.P = Prog(nc)
    c.inp = {}
    for name, shp in stage_shapes.items():
        dt = BF16 if (hasattr(shp, "dtype") or name in
                      ("c_ind16", "c_ind16t", "c_oh8", "c_oh16")) else F32
        c.inp[name] = nc.dram_tensor(name, list(shp), dt,
                                     kind="ExternalInput").ap()
    c.out = nc.dram_tensor("y", [NB, 1024, T_SEQ], F32, kind="ExternalOutput").ap()
    c.scr = {n: nc.dram_tensor("s" + n, list(s), BF16, kind="Internal").ap()
             for n, s in stage_shapes.items() if n.startswith("w_")}
    c.KS = nc.dram_tensor("KS", [1024, T_SEQ], BF16, kind="Internal").ap()
    c.VS = nc.dram_tensor("VS", [T_SEQ, 1024], BF16, kind="Internal").ap()
    c.QS = nc.dram_tensor("QS", [1024, T_SEQ], BF16, kind="Internal").ap()

    c.s_pe = P.sem("s_pe", "tensor")
    c.s_dve = P.sem("s_dve", "vector")
    c.s_act = P.sem("s_act", "scalar")
    c.s_pool = P.sem("s_pool", "gpsimd")
    c.s_gd = P.sem("s_gd")
    c.s_ld = P.sem("s_ld")
    c.s_out = P.sem("s_out")
    c.s_ws = [P.sem(f"s_ws{i}") for i in range(6)]
    c.s_x = P.sem("s_x")

    class APView:
        def __init__(self, ap_):
            self._ap = ap_

        def ap(self):
            return self._ap

    c.APView = APView
    c.XRES = nc.alloc_sbuf_tensor("XRES", [128, NCC * T_SEQ], F32)
    c.S = nc.alloc_sbuf_tensor("Swkv", [128, 512], F32)
    c.ONESC = nc.alloc_sbuf_tensor("ONESC", [128, 1], F32)
    c.ONESCB = nc.alloc_sbuf_tensor("ONESCB", [128, 1], BF16)
    c.ONESR = nc.alloc_sbuf_tensor("ONESR", [1, 128], F32)
    c.ZERO = nc.alloc_sbuf_tensor("ZEROt", [128, 128], F32)
    c.EPSLN = nc.alloc_sbuf_tensor("EPSLN", [128, 1], F32)
    c.DUMMY = nc.alloc_sbuf_tensor("DUMMYT", [128, 1], F32)
    c.EPSGN = nc.alloc_sbuf_tensor("EPSGN", [128, 1], F32)
    o32, obf = INIT_OFFS["f32"], INIT_OFFS["bf"]
    c.INIT32 = nc.alloc_sbuf_tensor("INIT32", [128, stage_shapes["c_init32"][1]],
                                    F32)
    c.INITBF = nc.alloc_sbuf_tensor("INITBF", [128, stage_shapes["c_initbf"][1]],
                                    BF16)

    def v32(nm):
        off, w, rows = o32[nm]
        return APView(c.INIT32.ap()[0:rows, off:off + w])

    def vbf(nm):
        off, w, rows = obf[nm]
        return APView(c.INITBF.ap()[0:rows, off:off + w])

    c.cols = {nm: v32(nm) for nm in
              ("c0_maa_x", "c0_maa_w", "c0_maa_k", "c0_maa_v", "c0_maa_r",
               "c0_maa_g", "c0_decay", "c0_ln_w", "c0_ln_b", "c0_u",
               "c0_shb1", "c0_shb2", "c0_eb1", "c0_eb2", "c1_bq", "c1_bk")}
    c.GATEB0 = v32("c0_gate_b")
    c.BVROW = v32("c1_bv_row")
    c.IOTAEPS = v32("c_iotaeps")
    c.IDENTF = v32("c_identf")
    c.IDENT = vbf("c_ident")
    c.LTM = vbf("c_ltm")
    c.CAUS = vbf("c_caus")
    c.MAA1 = vbf("w_maa1")
    c.DW1 = vbf("w_dw1")
    c.DW2 = vbf("w_dw2")
    c.GATE0 = vbf("w_gate0")
    c.GATE1 = vbf("w_gate1")


    c.AF = Carver(nc.alloc_sbuf_tensor("AF32", [128, 8448], F32), 8448, "AF32")
    c.AB = Carver(nc.alloc_sbuf_tensor("ABF", [128, 39808], BF16), 39808, "ABF")
    c.ROWS = [Buf(c.AF.take(512, rows=1)) for _ in range(4)]
    c._row_i = 0

    c.PS = [nc.alloc_psum_tensor(f"pb{i}", [128, 512], F32) for i in range(8)]
    c.PB = [Buf(t.ap()) for t in c.PS]
    c.xres = [Buf(c.XRES.ap()[:, cc * T_SEQ:(cc + 1) * T_SEQ]) for cc in range(NCC)]
    c.sbuf = Buf(c.S.ap())

    _init_consts(c)
    return c


def row(c):
    b = c.ROWS[c._row_i % len(c.ROWS)]
    c._row_i += 1
    return b


def _init_consts(c):
    P = c.P
    for t, v in ((c.ONESC, 1.0), (c.ONESCB, 1.0), (c.ONESR, 1.0), (c.ZERO, 0.0),
                 (c.EPSLN, LN_EPS), (c.EPSGN, GN_EPS)):
        E(P, "gpsimd", c.s_pool, lambda e, tt=t, vv=v: e.memset(tt.ap()[:, :], vv))

    E(P, "sync", c.s_ld, lambda e, d=c.INIT32.ap()[:, :],
      s=c.inp["c_init32"]: e.dma_start(d, s), k=16)
    E(P, "gpsimd", c.s_gd, lambda e, d=c.INITBF.ap()[:, :],
      s=c.inp["c_initbf"]: e.dma_start(d, s), k=16)
    c.tok_init = ((c.s_ld, c.s_ld.val), (c.s_gd, c.s_gd.val),
                  (c.s_pool, c.s_pool.val))


def build_prologue(c):
    P = c.P
    c.prol = {}
    for i, name in enumerate(PROLOGUE_ORDER):
        s_w = P.sem("s_prol_" + name)
        w = ((c.s_act, 1), (c.s_x, 16)) if i >= 6 else ()
        tok = E(P, "gpsimd", s_w,
                lambda e, s=c.inp[name].flatten().unsqueeze(0),
                d=c.scr[name].flatten().unsqueeze(0):
                e.dma_start(d, s), waits=w, k=16)
        c.prol[name] = tok


class WRing:

    def __init__(self, c, carver, cols, nbuf=2, slot0=0):
        self.c = c
        self.bufs = [carver.buf(cols) for _ in range(nbuf)]
        self.sems = [c.s_ws[slot0 + i] for i in range(nbuf)]
        self.i = 0

    def load(self, src_ap, rows=128, cols=None, extra=(), row0=0):
        c = self.c
        b = self.bufs[self.i % len(self.bufs)]
        sem = self.sems[self.i % len(self.bufs)]
        self.i += 1
        dap = b.ap if cols is None else b.ap[row0:row0 + rows, 0:cols]
        if len(src_ap.shape) == 3:
            dap = dap.rearrange("p (a b) -> p a b", a=src_ap.shape[1])
        tok = E(c.P, "sync", sem, lambda e, d=dap, s=src_ap: e.dma_start(d, s),
                waits=b.war() + tuple(extra) + ((sem, sem.val),), k=16)
        b.wrote(tok)
        return b


def emit_preln(c, t0, width, out_tiles, sq_buf, tmp_buf):
    P = c.P
    for blk in range(width // 512):
        tb = t0 + blk * 512
        pbs, pbq = c.PB[2], c.PB[3]
        tok = tok2 = None
        for cc in range(NCC):
            xap = c.XRES.ap()[:, cc * T_SEQ + tb: cc * T_SEQ + tb + 512]
            tsq = E(P, "scalar", c.s_act,
                    lambda e, o=sq_buf.ap, i=xap: e.activation(o, i, AF.Square),
                    waits=c.xres[cc].raw() + sq_buf.war())
            sq_buf.wrote(tsq)
            tok = E(P, "tensor", c.s_pe,
                    lambda e, o=c.PS[2].ap()[0:1, :], l=c.ONESC.ap()[:, :],
                    r=xap, s=(cc == 0), p=(cc == NCC - 1):
                    e.matmul(o, l, r, start=s, stop=p),
                    waits=c.xres[cc].raw() + (pbs.war() if cc == 0 else ()))
            c.xres[cc].read_by(tok)
            tok2 = E(P, "tensor", c.s_pe,
                     lambda e, o=c.PS[3].ap()[0:1, :], l=c.ONESCB.ap()[:, :],
                     r=sq_buf.ap, s=(cc == 0), p=(cc == NCC - 1):
                     e.matmul(o, l, r, start=s, stop=p),
                     waits=sq_buf.raw() + (pbq.war() if cc == 0 else ()))
            sq_buf.read_by(tok2)
        pbs.wrote(tok)
        pbq.wrote(tok2)
        mr, msq, sd, rst = row(c), row(c), row(c), row(c)
        t1 = E(P, "vector", c.s_dve, lambda e, o=mr.ap, i=c.PS[2].ap()[0:1, :]:
               e.tensor_scalar(o, i, 1.0 / 1024, None, OP.mult),
               waits=pbs.raw() + mr.war())
        pbs.read_by(t1)
        mr.wrote(t1)
        t2 = E(P, "vector", c.s_dve, lambda e, o=msq.ap, i=c.PS[3].ap()[0:1, :]:
               e.tensor_scalar(o, i, 1.0 / 1024, None, OP.mult),
               waits=pbq.raw() + msq.war())
        pbq.read_by(t2)
        msq.wrote(t2)
        t3 = E(P, "vector", c.s_dve, lambda e, o=sd.ap, a=mr.ap, b=mr.ap:
               e.tensor_tensor(o, a, b, OP.mult), waits=sd.war())
        t4 = E(P, "vector", c.s_dve, lambda e, o=msq.ap, a=msq.ap, b=sd.ap:
               e.tensor_tensor(o, a, b, OP.subtract))
        t4 = E(P, "vector", c.s_dve, lambda e, o=msq.ap, i=msq.ap:
               e.tensor_scalar(o, i, 0.0, None, OP.max))
        msq.wrote(t4)
        t5 = E(P, "scalar", c.s_act, lambda e, o=sd.ap, i=msq.ap:
               e.activation(o, i, AF.Sqrt, bias=c.EPSLN.ap()[0:1, :]),
               waits=(t4,) + c.tok_init)
        sd.wrote(t5)
        msq.read_by(t5)
        t6 = E(P, "vector", c.s_dve, lambda e, o=rst.ap, i=sd.ap:
               e.reciprocal(o, i), waits=(t5,) + rst.war())
        rst.wrote(t6)
        sd.read_by(t6)
        bm = E(P, "tensor", c.s_pe,
               lambda e, o=c.PS[2].ap()[:, :], l=c.ONESR.ap()[:, :], r=mr.ap:
               e.matmul(o, l, r, start=True, stop=True),
               waits=(t1,) + pbs.war())
        pbs.wrote(bm)
        mr.read_by(bm)
        br = E(P, "tensor", c.s_pe,
               lambda e, o=c.PS[3].ap()[:, :], l=c.ONESR.ap()[:, :], r=rst.ap:
               e.matmul(o, l, r, start=True, stop=True),
               waits=(t6,) + pbq.war())
        pbq.wrote(br)
        rst.read_by(br)
        for cc in range(NCC):
            xap = c.XRES.ap()[:, cc * T_SEQ + tb: cc * T_SEQ + tb + 512]
            ta = E(P, "vector", c.s_dve,
                   lambda e, o=tmp_buf.ap, a=xap, b=c.PS[2].ap()[:, :]:
                   e.tensor_tensor(o, a, b, OP.subtract),
                   waits=pbs.raw() + c.xres[cc].raw() + tmp_buf.war())
            pbs.read_by(ta)
            c.xres[cc].read_by(ta)
            tmp_buf.wrote(ta)
            oap = out_tiles[cc].ap[:, blk * 512:blk * 512 + 512]
            tz = E(P, "vector", c.s_dve,
                   lambda e, o=oap, a=tmp_buf.ap, b=c.PS[3].ap()[:, :]:
                   e.tensor_tensor(o, a, b, OP.mult),
                   waits=pbq.raw() + out_tiles[cc].war())
            pbq.read_by(tz)
            tmp_buf.read_by(tz)
            if blk == 0:
                out_tiles[cc].wrote(tz)
            else:
                out_tiles[cc].add_write(tz)


def pib(PIBUF, mc, ch, off, w=1):
    base = (mc * NCH0 + ch) * 130
    return PIBUF.ap[:, base + off: base + off + w]


def pib_tab(PIBUF, ch, off):
    v = PIBUF.ap.rearrange("p (mc ch w) -> p mc ch w", mc=NCC, w=130)
    return v[:, :, ch:ch + 1, off:off + 1].rearrange("p mc ch w -> p (mc ch w)")


def wsup(c, wname, half, mwidth=256):
    return (c.scr[wname].rearrange("(kc p) m -> p kc m", p=128)
            [:, :, half * mwidth:(half + 1) * mwidth])


def phase_l0attn(c):
    P = c.P
    f0, b0 = c.AF.mark(), c.AB.mark()

    STRV = [c.AB.buf(512) for _ in range(NCC)]
    STRG = [c.AB.buf(512) for _ in range(NCC)]
    RTA = c.AB.take(NCC * 512)
    RT = [Buf(RTA[:, mc * 512:(mc + 1) * 512]) for mc in range(NCC)]
    KTEA = c.AB.take(NCC * 512)
    KTE = [Buf(KTEA[:, mc * 512:(mc + 1) * 512]) for mc in range(NCC)]
    KTOA = c.AB.take(NCC * 512)
    KTO = [Buf(KTOA[:, mc * 512:(mc + 1) * 512]) for mc in range(NCC)]
    PIBUF = Buf(c.AF.take(NCC * NCH0 * 130))
    CARRY = [Buf(c.AF.take(8)) for _ in range(2)]
    EMID = Buf(c.AF.take(32))
    CD = Buf(c.AF.take(32))
    CD2 = Buf(c.AF.take(32))
    NMID = Buf(c.AF.take(32))
    RCD = Buf(c.AF.take(32))
    IND16 = c.AB.buf(128)
    IND16T = Buf(c.AB.take(1024, rows=16))
    for _b, _nm in ((IND16, "c_ind16"), (IND16T, "c_ind16t")):
        _t = E(P, "sync", c.s_ld, lambda e, d=_b.ap, s=c.inp[_nm]:
               e.dma_start(d, s),
               waits=_b.war() + tuple(c.AB.fence)
               + ((c.s_ld, c.s_ld.val),), k=16)
        _b.wrote(_t)

    fi, bi = c.AF.mark(), c.AB.mark()
    ZT = [c.AB.buf(512) for _ in range(NCC)]
    XXT = [c.AB.buf(512) for _ in range(NCC)]
    XXX = [c.AB.buf(512) for _ in range(2)]
    STRR = [c.AB.buf(512) for _ in range(NCC)]
    SQB = XXX[0]
    TMPB = XXX[1]
    MXALL = c.AB.buf(512)
    MXALL2 = c.AB.buf(512)
    TH = c.AB.buf(512, rows=64)
    TMP2 = [c.AB.buf(512) for _ in range(2)]
    EXPWR = [Buf(c.AF.take(512)) for _ in range(2)]
    EXPX = [Buf(c.AF.take(512)) for _ in range(2)]
    fi2, bi2 = c.AF.mark(), c.AB.mark()

    c.AF.reset(fi)
    c.AB.reset(bi)
    VTM = [c.AB.buf(1024) for _ in range(NCH0)]
    yb_base = c.AB.mark()
    YBALL = c.AB.take(NCC * 512)
    YB = [Buf(c.AB.t.ap()[:, yb_base + cc * 512: yb_base + (cc + 1) * 512])
          for cc in range(NCC)]
    abuf_base = c.AB.mark()
    ABUF = c.AB.buf(512)
    TMPA = c.AB.buf(512)
    D4 = c.AB.buf(512)
    TMPA2 = D4
    KHT = [c.AB.buf(128) for _ in range(2)]
    RKA = c.AB.buf(1024)
    KHA = RKA
    YSQ = TMPA
    SSC2 = c.AB.buf(1024)
    ENWS = Buf(c.AF.t.ap()[:, fi:fi + 1024])
    ENWB = Buf(c.AB.t.ap()[:, abuf_base + 512:abuf_base + 1536])
    GMB = c.AB.buf(512, rows=16)
    GRB = c.AB.buf(512, rows=16)
    YNG = [c.AB.buf(512) for _ in range(NCC)]
    GSIL = [c.AB.buf(512)]
    RLW = ABUF
    GN = [Buf(c.AF.take(512, rows=16)) for _ in range(3)]
    DGT = Buf(c.AF.take(16))
    fi3, bi3 = c.AF.mark(), c.AB.mark()

    inner = (ZT + XXT + XXX + STRR + [MXALL, MXALL2, TH]
             + TMP2 + EXPWR + EXPX)
    post = (VTM + YB + YNG + GSIL + [ABUF, TMPA, D4, SSC2, ENWS, ENWB,
                                     GMB, GRB, DGT, RKA] + KHT + GN)
    for b in inner:
        b.aliases = post
        b.ready = list(c.AB.fence)
    for b in post:
        b.aliases = inner
        b.ready = list(c.AB.fence)
    for b in (STRV + STRG + RT + KTE + KTO + CARRY
              + [EMID, CD, CD2, NMID, RCD, PIBUF]):
        b.ready = list(c.AB.fence)
    ENWB.aliases = ENWB.aliases + [TMPA, D4]
    TMPA.aliases = TMPA.aliases + [ENWB]
    D4.aliases = D4.aliases + [ENWB]
    c.AF.reset(max(fi2, fi3))
    c.AB.reset(max(bi2, bi3))

    tS = E(P, "gpsimd", c.s_pool, lambda e: e.memset(c.S.ap()[:, :], 0.0),
           waits=c.sbuf.war())
    c.sbuf.wrote(tS)
    for i in range(2):
        tC = E(P, "gpsimd", c.s_pool, lambda e, a=CARRY[i].ap: e.memset(a, 0.0),
               waits=CARRY[i].war())
        CARRY[i].wrote(tC)
    for mc in range(NCC):
        tz1 = E(P, "gpsimd", c.s_pool,
                lambda e, a=KTE[mc].ap: e.memset(a, 0.0),
                waits=KTE[mc].war())
        KTE[mc].add_write(tz1)
        tz2 = E(P, "gpsimd", c.s_pool,
                lambda e, a=KTO[mc].ap: e.memset(a, 0.0),
                waits=KTO[mc].war())
        KTO[mc].add_write(tz2)
    tz3 = E(P, "gpsimd", c.s_pool, lambda e, a=SSC2.ap: e.memset(a, 0.0),
            waits=SSC2.war())
    SSC2.add_write(tz3)
    tz4 = E(P, "gpsimd", c.s_pool, lambda e, a=MXALL2.ap:
            e.memset(a, 0.0), waits=MXALL2.war())
    MXALL2.add_write(tz4)

    maa_s_col = [c.cols["c0_maa_w"], c.cols["c0_maa_k"], c.cols["c0_maa_v"],
                 c.cols["c0_maa_r"], c.cols["c0_maa_g"]]
    trunc = globals().get("L0_TRUNC", "full")

    def stream_tile(s_idx, dest_tiles):
        mxb = MXALL if s_idx < 4 else MXALL2
        lb = c.WRB.load(c.scr["w_maa2"][s_idx * 128:(s_idx + 1) * 128, :],
                        rows=128, cols=1024,
                        extra=(c.prol["w_maa2"],))
        for mc in range(NCC):
            psx, pbx = (c.PS[4], c.PB[4]) if mc % 2 == 0 else (c.PS[5], c.PB[5])
            tmm = E(P, "tensor", c.s_pe,
                    lambda e, o=psx.ap()[:, :],
                    l=lb.ap[0:128, mc * 128:(mc + 1) * 128],
                    r=mxb.ap:
                    e.matmul(o, l, r, start=True, stop=True),
                    waits=lb.raw() + mxb.raw() + pbx.war())
            lb.read_by(tmm)
            mxb.read_by(tmm)
            pbx.wrote(tmm)
            t2b = TMP2[mc % 2]
            tm2 = E(P, "vector", c.s_dve,
                    lambda e, o=t2b.ap, a=psx.ap()[:, :],
                    s=maa_s_col[s_idx].ap()[:, mc:mc + 1], b=XXT[mc].ap:
                    e.scalar_tensor_tensor(o, a, s, b, OP.add, OP.mult),
                    waits=pbx.raw() + XXT[mc].raw() + t2b.war() + c.tok_init)
            t2b.wrote(tm2)
            pbx.read_by(tm2)
            XXT[mc].read_by(tm2)
            eng3, sem3 = (("gpsimd", c.s_pool) if mc % 2 == 0
                          else ("vector", c.s_dve))
            ts3 = E(P, eng3, sem3,
                    lambda e, o=dest_tiles[mc].ap, a=t2b.ap, b=ZT[mc].ap:
                    e.tensor_tensor(o, a, b, OP.add),
                    waits=t2b.raw() + ZT[mc].raw() + dest_tiles[mc].war())
            dest_tiles[mc].wrote(ts3)
            t2b.read_by(ts3)
            ZT[mc].read_by(ts3)

    def project_scaled(s_idx, wname, out_tiles, kind):
        stream_tile(s_idx, STRR)
        for half in range(4):
            wb = c.WRA.load(wsup(c, wname, half), cols=2048,
                            extra=(c.prol[wname],))
            for sub in range(2):
                mc = half * 2 + sub
                psx, pbx = (c.PS[6], c.PB[6]) if mc % 2 == 0 else (c.PS[7], c.PB[7])
                tokm = None
                for kc in range(NCC):
                    tokm = E(P, "tensor", c.s_pe,
                             lambda e, o=psx.ap()[:, :],
                             l=wb.ap[:, kc * 256 + sub * 128:
                                     kc * 256 + sub * 128 + 128],
                             r=STRR[kc].ap, s=(kc == 0), p=(kc == NCC - 1):
                             e.matmul(o, l, r, start=s, stop=p),
                             waits=wb.raw() + STRR[kc].raw()
                             + (pbx.war() if kc == 0 else ()))
                    wb.read_by(tokm)
                    STRR[kc].read_by(tokm)
                pbx.wrote(tokm)
                ex = EXPX[mc % 2]
                for ch in range(NCH0):
                    if kind == "r":
                        iap, sc, bap = pib(PIBUF, mc, ch, 0, 128), -1.0, \
                            pib(PIBUF, mc, ch, 64)
                    else:
                        iap, sc, bap = pib(PIBUF, mc, ch, 1, 128), 1.0, \
                            NMID.ap[:, ch * 8 + mc:ch * 8 + mc + 1]
                    tex = E(P, "scalar", c.s_act,
                            lambda e, o=ex.ap[:, ch * 128:(ch + 1) * 128],
                            i=iap, s=sc, b=bap:
                            e.activation(o, i, AF.Exp, scale=s, bias=b),
                            waits=PIBUF.raw() + NMID.raw()
                            + (ex.war() if ch == 0 else ()))
                    PIBUF.read_by(tex)
                    NMID.read_by(tex)
                    (ex.wrote if ch == 0 else ex.add_write)(tex)
                tcl = E(P, "vector", c.s_dve,
                        lambda e, o=ex.ap, i=ex.ap:
                        e.tensor_scalar(o, i, 1e17, None, OP.min),
                        waits=ex.raw())
                ex.add_write(tcl)
                if kind == "r":
                    tev = E(P, "vector", c.s_dve,
                            lambda e, o=out_tiles[mc].ap, a=psx.ap()[:, :],
                            b=ex.ap:
                            e.tensor_tensor(o, a, b, OP.mult),
                            waits=pbx.raw() + ex.raw() + out_tiles[mc].war())
                    out_tiles[mc].wrote(tev)
                    pbx.read_by(tev)
                    ex.read_by(tev)
                else:
                    be, bo = KTE[mc], KTO[mc]
                    t1 = E(P, "vector", c.s_dve,
                           lambda e, o=be.ap[0:64, :], a=psx.ap()[0:64, :],
                           b=ex.ap[0:64, :]:
                           e.tensor_tensor(o, a, b, OP.mult),
                           waits=pbx.raw() + ex.raw() + be.war())
                    be.add_write(t1)
                    pbx.read_by(t1)
                    ex.read_by(t1)
                    t2 = E(P, "vector", c.s_dve,
                           lambda e, o=bo.ap[64:128, :], a=psx.ap()[64:128, :],
                           b=ex.ap[64:128, :]:
                           e.tensor_tensor(o, a, b, OP.mult),
                           waits=bo.war())
                    bo.add_write(t2)
                    pbx.read_by(t2)
                    ex.read_by(t2)

    for tile in range(NT0):
        t0 = tile * TT0
        cin, cout = CARRY[tile % 2], CARRY[(tile + 1) % 2]
        if trunc == "none":
            continue
        emit_preln(c, t0, TT0, ZT, SQB, TMPB)
        if trunc == "preln":
            continue
        for cc in range(NCC):
            tcs = E(P, "vector", c.s_dve,
                    lambda e, o=cout.ap[:, cc:cc + 1], i=ZT[cc].ap[:, 511:512]:
                    e.tensor_copy(o, i), waits=ZT[cc].raw() + cout.war())
            (cout.wrote if cc == 0 else cout.add_write)(tcs)
            ZT[cc].read_by(tcs)
            tx = E(P, "vector", c.s_dve,
                   lambda e, o=XXT[cc].ap[:, 1:512], a=ZT[cc].ap[:, 0:511],
                   b=ZT[cc].ap[:, 1:512]:
                   e.tensor_tensor(o, a, b, OP.subtract),
                   waits=ZT[cc].raw() + XXT[cc].war() + cin.raw())
            tx0 = E(P, "vector", c.s_dve,
                    lambda e, o=XXT[cc].ap[:, 0:1], a=cin.ap[:, cc:cc + 1],
                    b=ZT[cc].ap[:, 0:1]:
                    e.tensor_tensor(o, a, b, OP.subtract))
            XXT[cc].wrote(tx)
            XXT[cc].add_write(tx0)
            ZT[cc].read_by(tx0)
            cin.read_by(tx0)
            xb = XXX[cc % 2]
            txx = E(P, "vector", c.s_dve,
                    lambda e, o=xb.ap, a=XXT[cc].ap,
                    s=c.cols["c0_maa_x"].ap()[:, cc:cc + 1], b=ZT[cc].ap:
                    e.scalar_tensor_tensor(o, a, s, b, OP.mult, OP.add),
                    waits=xb.war() + c.tok_init)
            xb.wrote(txx)
            XXT[cc].read_by(txx)
            ZT[cc].read_by(txx)
            pA, pB = c.PB[0], c.PB[1]
            tokA = E(P, "tensor", c.s_pe,
                     lambda e, o=c.PS[0].ap()[:, :],
                     l=c.MAA1.ap()[:, cc * 160:cc * 160 + 128],
                     r=xb.ap, s=(cc == 0), p=(cc == NCC - 1):
                     e.matmul(o, l, r, start=s, stop=p),
                     waits=xb.raw() + (pA.war() if cc == 0 else ())
                     + c.tok_init)
            xb.read_by(tokA)
            tokB = E(P, "tensor", c.s_pe,
                     lambda e, o=c.PS[1].ap()[0:32, :],
                     l=c.MAA1.ap()[:, cc * 160 + 128:cc * 160 + 160],
                     r=xb.ap, s=(cc == 0), p=(cc == NCC - 1):
                     e.matmul(o, l, r, start=s, stop=p),
                     waits=(pB.war() if cc == 0 else ()))
            xb.read_by(tokB)
        pA.wrote(tokA)
        pB.wrote(tokB)
        for dst, rows, src_ap, pb_ in (
                (MXALL, (0, 64), c.PS[0].ap()[0:64, :], pA),
                (MXALL, (64, 128), c.PS[0].ap()[64:128, :], pA),
                (MXALL2, (0, 32), c.PS[1].ap()[0:32, :], pB)):
            tm = E(P, "scalar", c.s_act,
                   lambda e, o=dst.ap[rows[0]:rows[1], :], i=src_ap:
                   e.activation(o, i, AF.Tanh),
                   waits=pb_.raw() + (dst.war() if (rows[0] == 0 or dst is MXALL2)
                                      else ()))
            (dst.wrote if (dst is MXALL and rows[0] == 0) else dst.add_write)(tm)
            pb_.read_by(tm)

        tz = E(P, "gpsimd", c.s_pool,
               lambda e, a=PIBUF.ap.rearrange("p (q w) -> p q w", w=130)
               [:, :, 0:1].rearrange("p q w -> p (q w)"):
               e.memset(a, 0.0), waits=PIBUF.war())
        PIBUF.wrote(tz)
        stream_tile(0, STRR)
        tth = None
        for cc in range(NCC):
            tth = E(P, "tensor", c.s_pe,
                    lambda e, o=c.PS[5].ap()[0:64, :],
                    l=c.DW1.ap()[:, cc * 64:(cc + 1) * 64], r=STRR[cc].ap,
                    s=(cc == 0), p=(cc == NCC - 1):
                    e.matmul(o, l, r, start=s, stop=p),
                    waits=STRR[cc].raw() + (c.PB[5].war() if cc == 0 else ())
                    + c.tok_init)
            STRR[cc].read_by(tth)
        c.PB[5].wrote(tth)
        tte = E(P, "scalar", c.s_act,
                lambda e, o=TH.ap, i=c.PS[5].ap()[0:64, :]:
                e.activation(o, i, AF.Tanh), waits=c.PB[5].raw() + TH.war())
        TH.wrote(tte)
        c.PB[5].read_by(tte)
        for mc in range(NCC):
            psx, pbx = (c.PS[4], c.PB[4]) if mc % 2 == 0 else (c.PS[5], c.PB[5])
            tw = E(P, "tensor", c.s_pe,
                   lambda e, o=psx.ap()[:, :],
                   l=c.DW2.ap()[:, mc * 128:(mc + 1) * 128], r=TH.ap:
                   e.matmul(o, l, r, start=True, stop=True),
                   waits=TH.raw() + pbx.war() + c.tok_init)
            TH.read_by(tw)
            pbx.wrote(tw)
            ew = EXPWR[mc % 2]
            te = E(P, "scalar", c.s_act,
                   lambda e, o=ew.ap, i=psx.ap()[:, :],
                   b=c.cols["c0_decay"].ap()[:, mc:mc + 1]:
                   e.activation(o, i, AF.Exp, bias=b),
                   waits=pbx.raw() + ew.war() + c.tok_init)
            ew.wrote(te)
            pbx.read_by(te)
            for ch in range(NCH0):
                base = (mc * NCH0 + ch) * 130
                tsc = E(P, "vector", c.s_dve,
                        lambda e, o=PIBUF.ap[:, base + 1:base + 129],
                        d0=ew.ap[:, ch * 128:(ch + 1) * 128],
                        d1=c.ZERO.ap()[:, 0:128]:
                        e.tensor_tensor_scan(o, d0, d1, 0.0, OP.add, OP.add),
                        waits=ew.raw() + PIBUF.raw() + c.tok_init)
                PIBUF.add_write(tsc)
                ew.read_by(tsc)
        for ch in range(NCH0):
            te1 = E(P, "scalar", c.s_act,
                    lambda e, o=EMID.ap[:, ch * 8:ch * 8 + 8],
                    i=pib_tab(PIBUF, ch, 64):
                    e.activation(o, i, AF.Exp, scale=-1.0),
                    waits=PIBUF.raw() + EMID.war())
            (EMID.wrote if ch == 0 else EMID.add_write)(te1)
            PIBUF.read_by(te1)
            te2 = E(P, "scalar", c.s_act,
                    lambda e, o=CD.ap[:, ch * 8:ch * 8 + 8],
                    i=pib_tab(PIBUF, ch, 128):
                    e.activation(o, i, AF.Exp, scale=-1.0),
                    waits=PIBUF.raw() + CD.war())
            (CD.wrote if ch == 0 else CD.add_write)(te2)
            PIBUF.read_by(te2)
            tnm = E(P, "vector", c.s_dve,
                    lambda e, o=NMID.ap[:, ch * 8:ch * 8 + 8],
                    i=pib_tab(PIBUF, ch, 64):
                    e.tensor_scalar(o, i, -1.0, None, OP.mult),
                    waits=PIBUF.raw() + NMID.war())
            (NMID.wrote if ch == 0 else NMID.add_write)(tnm)
            PIBUF.read_by(tnm)
        tr1 = E(P, "vector", c.s_dve, lambda e, o=RCD.ap, i=EMID.ap:
                e.reciprocal(o, i), waits=EMID.raw() + RCD.war())
        RCD.wrote(tr1)
        EMID.read_by(tr1)
        tr2 = E(P, "vector", c.s_dve, lambda e, o=CD2.ap, a=CD.ap, b=RCD.ap:
                e.tensor_tensor(o, a, b, OP.mult),
                waits=CD.raw() + CD2.war())
        CD2.wrote(tr2)
        RCD.read_by(tr2)
        CD.read_by(tr2)

        if trunc == "wpath":
            continue
        project_scaled(3, "w_Wr", RT, "r")
        project_scaled(1, "w_Wk", None, "k")
        stream_tile(2, STRV)
        stream_tile(4, STRG)

        for half in range(4):
            wb = c.WRA.load(wsup(c, "w_Wv", half), cols=2048,
                            extra=(c.prol["w_Wv"],))
            for tb in range(NCH0):
                psx, pbx = (c.PS[6], c.PB[6]) if tb % 2 == 0 else (c.PS[7], c.PB[7])
                tokm = None
                for kc in range(NCC):
                    tokm = E(P, "tensor", c.s_pe,
                             lambda e, o=psx.ap()[:, 0:256],
                             l=STRV[kc].ap[:, tb * 128:(tb + 1) * 128],
                             r=wb.ap[:, kc * 256:(kc + 1) * 256],
                             s=(kc == 0), p=(kc == NCC - 1):
                             e.matmul(o, l, r, start=s, stop=p),
                             waits=wb.raw() + STRV[kc].raw()
                             + (pbx.war() if kc == 0 else ()))
                    wb.read_by(tokm)
                    STRV[kc].read_by(tokm)
                pbx.wrote(tokm)
                tv = E(P, "scalar", c.s_act,
                       lambda e, o=VTM[tb].ap[:, half * 256:(half + 1) * 256],
                       i=psx.ap()[:, 0:256]: e.activation(o, i, AF.Copy),
                       waits=pbx.raw() + (VTM[tb].war() if half == 0 else ()))
                (VTM[tb].wrote if half == 0 else VTM[tb].add_write)(tv)
                pbx.read_by(tv)

        if trunc == "streams":
            continue
        tzs = E(P, "gpsimd", c.s_pool, lambda e, a=SSC2.ap: e.memset(a, 0.0),
                waits=SSC2.war(), force=True)
        SSC2.wrote(tzs)
        for ch in range(NCH0):
            wtr = globals().get("WKV_TRUNC", "full")
            for par in range(2):
                rr = 64 * par
                o3 = (SSC2.ap[rr:rr + 64, :]
                      .rearrange("p (m t) -> p m t", t=128)
                      [:, :, par * 64:par * 64 + 64])
                a3 = (c.S.ap()[rr:rr + 64, :]
                      .rearrange("p (m t) -> p m t", t=64))
                b3 = (EMID.ap[rr:rr + 64, ch * 8:ch * 8 + 8]
                      .rearrange("p (m o) -> p m o", o=1)
                      .broadcast_to([64, 8, 64]))
                engs, sems = (("gpsimd", c.s_pool) if par == 0
                              else ("vector", c.s_dve))
                tsnap = E(P, engs, sems,
                          lambda e, o=o3, a=a3, b=b3:
                          e.tensor_tensor(o, a, b, OP.mult),
                          waits=c.sbuf.raw() + EMID.raw() + SSC2.war(),
                          force=True)
                c.sbuf.read_by(tsnap)
                EMID.read_by(tsnap)
                SSC2.add_write(tsnap)
            if wtr == "ssc":
                continue
            pall = PIBUF.ap.rearrange("p (m t) -> p m t", t=NCH0 * 130)
            ten1 = E(P, "vector", c.s_dve,
                     lambda e, o=ENWS.ap.rearrange("p (m t) -> p m t", t=128),
                     a=pall[:, :, ch * 130:ch * 130 + 128],
                     b=pall[:, :, ch * 130 + 1:ch * 130 + 129]:
                     e.tensor_tensor(o, a, b, OP.subtract),
                     waits=PIBUF.raw() + ENWS.war())
            ENWS.wrote(ten1)
            PIBUF.read_by(ten1)
            ten2 = E(P, "scalar", c.s_act,
                     lambda e, o=ENWB.ap, i=ENWS.ap:
                     e.activation(o, i, AF.Exp),
                     waits=ENWS.raw() + ENWB.war())
            ENWB.wrote(ten2)
            ENWS.read_by(ten2)
            for par, arr, kbl in ((0, KTEA, KTE), (1, KTOA, KTO)):
                rr = 64 * par
                o3 = (RKA.ap[rr:rr + 64, :]
                      .rearrange("p (m t) -> p m t", t=128))
                a3 = (RTA[rr:rr + 64, :]
                      .rearrange("p (m t) -> p m t", t=512)
                      [:, :, ch * 128:(ch + 1) * 128])
                b3 = (arr[rr:rr + 64, :]
                      .rearrange("p (m t) -> p m t", t=512)
                      [:, :, ch * 128:(ch + 1) * 128])
                tr = E(P, "vector", c.s_dve,
                       lambda e, o=o3, a=a3, b=b3:
                       e.tensor_tensor(o, a, b, OP.mult),
                       waits=(RKA.war() if par == 0 else ())
                       + tuple(t for b_ in RT for t in b_.raw())
                       + tuple(t for b_ in kbl for t in b_.raw()))
                (RKA.wrote if par == 0 else RKA.add_write)(tr)
                for b_ in RT:
                    b_.read_by(tr)
                for b_ in kbl:
                    b_.read_by(tr)
            tu = E(P, "vector", c.s_dve,
                   lambda e, o=RKA.ap.rearrange("p (m t) -> p m t", t=128),
                   a=RKA.ap.rearrange("p (m t) -> p m t", t=128),
                   b=c.cols["c0_u"].ap()[:, 0:8]
                   .rearrange("p (m o) -> p m o", o=1)
                   .broadcast_to([128, 8, 128]):
                   e.tensor_tensor(o, a, b, OP.mult),
                   waits=c.tok_init)
            RKA.add_write(tu)
            te2 = E(P, "vector", c.s_dve,
                    lambda e, o=RKA.ap, a=RKA.ap, b=ENWB.ap:
                    e.tensor_tensor(o, a, b, OP.mult),
                    waits=ENWB.raw())
            RKA.add_write(te2)
            ENWB.read_by(te2)
            tdm = None
            for mc in range(NCC):
                tdm = E(P, "tensor", c.s_pe,
                        lambda e, o=c.PS[1].ap()[:, 0:16],
                        l=RKA.ap[:, mc * 128:(mc + 1) * 128],
                        r=IND16.ap[:, mc * 16:(mc + 1) * 16],
                        s=(mc == 0), p=(mc == NCC - 1):
                        e.matmul(o, l, r, start=s, stop=p),
                        waits=RKA.raw() + IND16.raw()
                        + (c.PB[1].war() if mc == 0 else ()))
                RKA.read_by(tdm)
                IND16.read_by(tdm)
            c.PB[1].wrote(tdm)
            tdg = E(P, "scalar", c.s_act,
                    lambda e, o=DGT.ap, i=c.PS[1].ap()[:, 0:16]:
                    e.activation(o, i, AF.Identity),
                    waits=c.PB[1].raw() + DGT.war())
            DGT.wrote(tdg)
            c.PB[1].read_by(tdg)

            if wtr == "diag":
                continue
            for hg in range(4):
                psa, pba = (c.PS[4], c.PB[4]) if hg % 2 == 0 else (c.PS[2], c.PB[2])
                tokA = None
                for h4 in range(4):
                    h = hg * 4 + h4
                    mc = h // 2
                    kb = KTE[mc] if h % 2 == 0 else KTO[mc]
                    tokA = E(P, "tensor", c.s_pe,
                             lambda e, o=psa.ap()[:, h4 * 128:(h4 + 1) * 128],
                             l=kb.ap[:, ch * 128:(ch + 1) * 128],
                             r=RT[mc].ap[:, ch * 128:(ch + 1) * 128]:
                             e.matmul(o, l, r, start=True, stop=True),
                             waits=kb.raw() + RT[mc].raw()
                             + (pba.war() if h4 == 0 else ()))
                    kb.read_by(tokA)
                    RT[mc].read_by(tokA)
                pba.wrote(tokA)
                if wtr == "atmm":
                    continue
                tD = E(P, "gpsimd", c.s_pool,
                       lambda e, o=D4.ap.rearrange("p (m t) -> p m t", t=128),
                       a=c.IDENT.ap()[:, :]
                       .rearrange("p (o t) -> p o t", o=1)
                       .broadcast_to([128, 4, 128]),
                       b=DGT.ap[:, hg * 4:hg * 4 + 4]
                       .rearrange("p (m o) -> p m o", o=1)
                       .broadcast_to([128, 4, 128]):
                       e.tensor_tensor(o, a, b, OP.mult),
                       waits=DGT.raw() + D4.war() + c.tok_init)
                DGT.read_by(tD)
                D4.wrote(tD)
                tA1 = E(P, "vector", c.s_dve,
                        lambda e, o=TMPA.ap, a=psa.ap()[:, :], b=c.LTM.ap()[:, :]:
                        e.tensor_tensor(o, a, b, OP.mult),
                        waits=pba.raw() + TMPA.war() + c.tok_init)
                TMPA.wrote(tA1)
                pba.read_by(tA1)
                enga, sema = (("gpsimd", c.s_pool) if hg % 2 == 0
                              else ("vector", c.s_dve))
                tA2 = E(P, enga, sema,
                        lambda e, o=ABUF.ap, a=TMPA.ap, b=D4.ap:
                        e.tensor_tensor(o, a, b, OP.add),
                        waits=TMPA.raw() + D4.raw() + ABUF.war())
                ABUF.wrote(tA2)
                TMPA.read_by(tA2)
                D4.read_by(tA2)
                if wtr in ("at", "atmm"):
                    continue
                psy, pby = (c.PS[6], c.PB[6]) if hg < 2 else (c.PS[7], c.PB[7])
                ycol = (hg % 2) * 256
                for h4 in range(4):
                    h = hg * 4 + h4
                    mc, rr = h // 2, 64 * (h % 2)
                    dst = psy.ap()[rr:rr + 64,
                                   ycol + (h4 // 2) * 128:ycol + (h4 // 2) * 128 + 128]
                    t1 = E(P, "tensor", c.s_pe,
                           lambda e, o=dst,
                           l=VTM[ch].ap[:, mc * 128 + rr:mc * 128 + rr + 64],
                           r=ABUF.ap[:, h4 * 128:(h4 + 1) * 128]:
                           e.matmul(o, l, r, start=True, stop=False),
                           waits=VTM[ch].raw() + ABUF.raw()
                           + (pby.war() if (hg % 2 == 0 and h4 == 0) else ()))
                    VTM[ch].read_by(t1)
                    ABUF.read_by(t1)
                    toky = E(P, "tensor", c.s_pe,
                             lambda e, o=dst, l=SSC2.ap[:, h * 64:(h + 1) * 64],
                             r=RT[mc].ap[:, ch * 128:(ch + 1) * 128]:
                             e.matmul(o, l, r, start=False, stop=True),
                             waits=SSC2.raw() + RT[mc].raw())
                    SSC2.read_by(toky)
                    RT[mc].read_by(toky)
                    pby.add_write(toky)
            if wtr in ("y", "atmm"):
                continue
            for half in range(2):
                psy, pby = (c.PS[6], c.PB[6]) if half == 0 else (c.PS[7], c.PB[7])
                base = yb_base + half * 4 * 512
                oview = (c.AB.t.ap()[:, base:base + 2048]
                         .rearrange("p (g f) -> p g f", f=512)
                         [:, :, ch * 128:(ch + 1) * 128])
                ty = E(P, "vector", c.s_dve,
                       lambda e, o=oview,
                       i=psy.ap()[:, :].rearrange("p (g f) -> p g f", f=128):
                       e.tensor_copy(o, i),
                       waits=pby.raw() + tuple(t for b in YB[half * 4:half * 4 + 4]
                                               for t in b.war()))
                for b in YB[half * 4:half * 4 + 4]:
                    (b.wrote if ch == 0 else b.add_write)(ty)
                pby.read_by(ty)
            if wtr in ("yevac", "atmm2"):
                continue
            for par, arr, kbl in ((0, KTEA, KTE), (1, KTOA, KTO)):
                rr = 64 * par
                o3 = (KHA.ap[rr:rr + 64, :]
                      .rearrange("p (m t) -> p m t", t=128))
                a3 = (arr[rr:rr + 64, :]
                      .rearrange("p (m t) -> p m t", t=512)
                      [:, :, ch * 128:(ch + 1) * 128])
                b3 = (CD2.ap[rr:rr + 64, ch * 8:ch * 8 + 8]
                      .rearrange("p (m o) -> p m o", o=1)
                      .broadcast_to([64, 8, 128]))
                engk, semk = (("gpsimd", c.s_pool) if par == 0
                              else ("vector", c.s_dve))
                tk = E(P, engk, semk,
                       lambda e, o=o3, a=a3, b=b3:
                       e.tensor_tensor(o, a, b, OP.mult),
                       waits=KHA.war() + CD2.raw()
                       + tuple(t for b_ in kbl for t in b_.raw()),
                       force=True)
                (KHA.wrote if par == 0 else KHA.add_write)(tk)
                for b_ in kbl:
                    b_.read_by(tk)
                CD2.read_by(tk)
            tlast = None
            for mc in range(NCC):
                pst = c.PS[1].ap().bitcast(BF16)[:, 0:128]
                tk2 = E(P, "tensor", c.s_pe,
                        lambda e, o=pst, i=KHA.ap[:, mc * 128:(mc + 1) * 128],
                        ident=c.IDENT.ap()[:, :]:
                        e.transpose(o, i, ident),
                        waits=KHA.raw() + c.PB[1].war() + c.tok_init)
                c.PB[1].wrote(tk2)
                KHA.read_by(tk2)
                kht = KHT[mc % 2]
                tk3 = E(P, "scalar", c.s_act,
                        lambda e, o=kht.ap, i=pst:
                        e.activation(o, i, AF.Identity),
                        waits=c.PB[1].raw() + kht.war())
                kht.wrote(tk3)
                c.PB[1].read_by(tk3)
                for hh in range(2):
                    h = mc * 2 + hh
                    rr = 64 * hh
                    dst = c.PS[5].ap()[rr:rr + 64, mc * 64:(mc + 1) * 64]
                    tlast = E(P, "tensor", c.s_pe,
                              lambda e, o=dst, l=kht.ap[:, rr:rr + 64],
                              r=VTM[ch].ap[:, mc * 128 + rr:mc * 128 + rr + 64]:
                              e.matmul(o, l, r, start=True, stop=True),
                              waits=kht.raw() + VTM[ch].raw()
                              + (c.PB[5].war() if (mc == 0 and hh == 0) else ()))
                    kht.read_by(tlast)
                    VTM[ch].read_by(tlast)
                    c.PB[5].add_write(tlast)
            for par in range(2):
                rr = 64 * par
                o3 = (c.S.ap()[rr:rr + 64, :]
                      .rearrange("p (m t) -> p m t", t=64))
                b3 = (CD.ap[rr:rr + 64, ch * 8:ch * 8 + 8]
                      .rearrange("p (m o) -> p m o", o=1)
                      .broadcast_to([64, 8, 64]))
                engp, semp = (("gpsimd", c.s_pool) if par == 0
                              else ("vector", c.s_dve))
                tS1 = E(P, engp, semp,
                        lambda e, o=o3, a=o3, b=b3:
                        e.tensor_tensor(o, a, b, OP.mult),
                        waits=c.sbuf.war() + CD.raw(), force=True)
                c.sbuf.add_write(tS1)
                CD.read_by(tS1)
            tS2 = E(P, "vector", c.s_dve,
                    lambda e, o=c.S.ap()[:, :], a=c.S.ap()[:, :],
                    b=c.PS[5].ap()[:, :]:
                    e.tensor_tensor(o, a, b, OP.add),
                    waits=c.PB[5].raw() + c.sbuf.war() + c.sbuf.raw())
            c.sbuf.add_write(tS2)
            c.PB[5].read_by(tS2)

        if trunc == "wkv":
            continue
        wgb = None
        for cc in range(NCC):
            if cc % 2 == 0:
                wgb = c.WRA.load(wsup(c, "w_Wg", cc // 2), cols=2048,
                                 extra=(c.prol["w_Wg"],))
            tokg = None
            for kc in range(NCC):
                tokg = E(P, "tensor", c.s_pe,
                         lambda e, o=c.PS[0].ap()[:, :],
                         l=wgb.ap[:, kc * 256 + (cc % 2) * 128:
                                  kc * 256 + (cc % 2) * 128 + 128],
                         r=STRG[kc].ap, s=(kc == 0), p=(kc == NCC - 1):
                         e.matmul(o, l, r, start=s, stop=p),
                         waits=wgb.raw() + STRG[kc].raw()
                         + (c.PB[0].war() if kc == 0 else ()))
                wgb.read_by(tokg)
                STRG[kc].read_by(tokg)
            c.PB[0].wrote(tokg)
            gs = GSIL[0]
            tg = E(P, "scalar", c.s_act,
                   lambda e, o=gs.ap, i=c.PS[0].ap()[:, :]:
                   e.activation(o, i, AF.Silu), waits=c.PB[0].raw() + gs.war())
            gs.wrote(tg)
            c.PB[0].read_by(tg)
            tq = E(P, "scalar", c.s_act,
                   lambda e, o=YSQ.ap, i=YB[cc].ap: e.activation(o, i, AF.Square),
                   waits=YB[cc].raw() + YSQ.war())
            YSQ.wrote(tq)
            YB[cc].read_by(tq)
            tm1 = E(P, "tensor", c.s_pe,
                    lambda e, o=c.PS[2].ap()[0:16, :],
                    l=IND16.ap[:, cc * 16:(cc + 1) * 16], r=YB[cc].ap:
                    e.matmul(o, l, r, start=True, stop=True),
                    waits=YB[cc].raw() + IND16.raw() + c.PB[2].war())
            c.PB[2].wrote(tm1)
            YB[cc].read_by(tm1)
            IND16.read_by(tm1)
            tm2 = E(P, "tensor", c.s_pe,
                    lambda e, o=c.PS[3].ap()[0:16, :],
                    l=IND16.ap[:, cc * 16:(cc + 1) * 16], r=YSQ.ap:
                    e.matmul(o, l, r, start=True, stop=True),
                    waits=YSQ.raw() + c.PB[3].war())
            c.PB[3].wrote(tm2)
            YSQ.read_by(tm2)
            IND16.read_by(tm2)
            gm, gv, gt = GN[0], GN[1], GN[2]
            ts1 = E(P, "vector", c.s_dve,
                    lambda e, o=gm.ap, i=c.PS[2].ap()[0:16, :]:
                    e.tensor_scalar(o, i, 1.0 / 64, None, OP.mult),
                    waits=c.PB[2].raw() + gm.war())
            gm.wrote(ts1)
            c.PB[2].read_by(ts1)
            ts2 = E(P, "vector", c.s_dve,
                    lambda e, o=gv.ap, i=c.PS[3].ap()[0:16, :]:
                    e.tensor_scalar(o, i, 1.0 / 64, None, OP.mult),
                    waits=c.PB[3].raw() + gv.war())
            gv.wrote(ts2)
            c.PB[3].read_by(ts2)
            ts3 = E(P, "vector", c.s_dve,
                    lambda e, o=gt.ap, a=gm.ap, b=gm.ap:
                    e.tensor_tensor(o, a, b, OP.mult), waits=gt.war())
            gt.wrote(ts3)
            gm.read_by(ts3)
            ts4 = E(P, "vector", c.s_dve,
                    lambda e, o=gv.ap, a=gv.ap, b=gt.ap:
                    e.tensor_tensor(o, a, b, OP.subtract))
            ts4 = E(P, "vector", c.s_dve,
                    lambda e, o=gv.ap, i=gv.ap:
                    e.tensor_scalar(o, i, 0.0, None, OP.max))
            gv.wrote(ts4)
            gt.read_by(ts4)
            ts5 = E(P, "scalar", c.s_act,
                    lambda e, o=gv.ap, i=gv.ap:
                    e.activation(o, i, AF.Sqrt, bias=c.EPSGN.ap()[0:16, :]),
                    waits=(ts4,) + c.tok_init)
            gv.wrote(ts5)
            ts6 = E(P, "vector", c.s_dve,
                    lambda e, o=gv.ap, i=gv.ap: e.reciprocal(o, i),
                    waits=(ts5,))
            gv.wrote(ts6)
            tb1 = E(P, "vector", c.s_dve,
                    lambda e, o=GMB.ap, i=gm.ap: e.tensor_copy(o, i),
                    waits=gm.raw() + GMB.war())
            GMB.wrote(tb1)
            gm.read_by(tb1)
            tb2 = E(P, "vector", c.s_dve,
                    lambda e, o=GRB.ap, i=gv.ap: e.tensor_copy(o, i),
                    waits=gv.raw() + GRB.war())
            GRB.wrote(tb2)
            gv.read_by(tb2)
            bm = E(P, "tensor", c.s_pe,
                   lambda e, o=c.PS[2].ap()[:, :],
                   l=IND16T.ap[:, cc * 128:(cc + 1) * 128], r=GMB.ap:
                   e.matmul(o, l, r, start=True, stop=True),
                   waits=GMB.raw() + IND16T.raw() + c.PB[2].war())
            c.PB[2].wrote(bm)
            GMB.read_by(bm)
            IND16T.read_by(bm)
            br = E(P, "tensor", c.s_pe,
                   lambda e, o=c.PS[3].ap()[:, :],
                   l=IND16T.ap[:, cc * 128:(cc + 1) * 128], r=GRB.ap:
                   e.matmul(o, l, r, start=True, stop=True),
                   waits=GRB.raw() + c.PB[3].war())
            c.PB[3].wrote(br)
            GRB.read_by(br)
            IND16T.read_by(br)
            tn1 = E(P, "vector", c.s_dve,
                    lambda e, o=TMPA.ap, a=YB[cc].ap, b=c.PS[2].ap()[:, :]:
                    e.tensor_tensor(o, a, b, OP.subtract),
                    waits=c.PB[2].raw() + YB[cc].raw() + TMPA.war())
            TMPA.wrote(tn1)
            c.PB[2].read_by(tn1)
            YB[cc].read_by(tn1)
            tn2 = E(P, "vector", c.s_dve,
                    lambda e, o=RLW.ap, i=c.PS[3].ap()[:, :],
                    s=c.cols["c0_ln_w"].ap()[:, cc:cc + 1]:
                    e.tensor_scalar(o, i, s, None, OP.mult),
                    waits=c.PB[3].raw() + RLW.war() + c.tok_init)
            RLW.wrote(tn2)
            c.PB[3].read_by(tn2)
            tn3 = E(P, "gpsimd", c.s_pool,
                    lambda e, o=TMPA2.ap, a=TMPA.ap, b=RLW.ap:
                    e.tensor_tensor(o, a, b, OP.mult),
                    waits=TMPA.raw() + RLW.raw() + TMPA2.war())
            TMPA2.wrote(tn3)
            TMPA.read_by(tn3)
            RLW.read_by(tn3)
            tn4 = E(P, "vector", c.s_dve,
                    lambda e, o=YNG[cc].ap, a=TMPA2.ap,
                    s=c.cols["c0_ln_b"].ap()[:, cc:cc + 1], b=gs.ap:
                    e.scalar_tensor_tensor(o, a, s, b, OP.add, OP.mult),
                    waits=TMPA2.raw() + gs.raw() + YNG[cc].war() + c.tok_init)
            YNG[cc].wrote(tn4)
            TMPA2.read_by(tn4)
            gs.read_by(tn4)
        for half in range(4):
            wb = c.WRA.load(wsup(c, "w_Wo", half), cols=2048,
                            extra=(c.prol["w_Wo"],))
            for sub in range(2):
                mc = half * 2 + sub
                psx, pbx = (c.PS[0], c.PB[0]) if mc % 2 == 0 else (c.PS[1], c.PB[1])
                tokm = None
                for kc in range(NCC):
                    tokm = E(P, "tensor", c.s_pe,
                             lambda e, o=psx.ap()[:, :],
                             l=wb.ap[:, kc * 256 + sub * 128:
                                     kc * 256 + sub * 128 + 128],
                             r=YNG[kc].ap, s=(kc == 0), p=(kc == NCC - 1):
                             e.matmul(o, l, r, start=s, stop=p),
                             waits=wb.raw() + YNG[kc].raw()
                             + (pbx.war() if kc == 0 else ()))
                    wb.read_by(tokm)
                    YNG[kc].read_by(tokm)
                pbx.wrote(tokm)
                xap = c.XRES.ap()[:, mc * T_SEQ + t0: mc * T_SEQ + t0 + 512]
                ta = E(P, "vector", c.s_dve,
                       lambda e, o=xap, a=xap, b=psx.ap()[:, :]:
                       e.tensor_tensor(o, a, b, OP.add),
                       waits=pbx.raw() + c.xres[mc].war())
                c.xres[mc].add_write(ta)
                pbx.read_by(ta)

    c.dbg = dict(ZT=ZT, XXT=XXT, XXX=XXX, STRR=STRR, RT=RT, KTE=KTE, KTO=KTO,
                 VTM=VTM, YB=YB, YNG=YNG, PIBUF=PIBUF, EMID=EMID, CD=CD,
                 CD2=CD2, NMID=NMID, STRV=STRV, STRG=STRG, TH=TH,
                 SSC2=SSC2, DGT=DGT, GSIL=GSIL, KHT=KHT,
                 IND16=IND16, IND16T=IND16T)
    c.AF.reset(f0)
    c.AB.reset(b0)
    set_fence(c)


def set_fence(c):
    c.AB.fence = c.AF.fence = (
        (c.s_pe, c.s_pe.val), (c.s_dve, c.s_dve.val),
        (c.s_act, c.s_act.val), (c.s_pool, c.s_pool.val))


Carver.fence = ()


def phase_load_x(c, seq):
    P = c.P
    waits = tuple(t for b in c.xres for t in b.war()) + ((c.s_x, c.s_x.val),)
    tok = E(P, "sync", c.s_x,
            lambda e,
            o=c.XRES.ap()[:, :].rearrange("p (cc t) -> p cc t", cc=NCC),
            i=c.inp["x"][seq].rearrange("(cc p) t -> p cc t", p=128):
            e.dma_start(o, i), waits=waits, k=16)
    for b in c.xres:
        b.wrote(tok)


def phase_store(c, seq):
    P = c.P
    waits = (tuple(t for b in c.xres for t in b.raw())
             + ((c.s_out, c.s_out.val),))
    tok = E(P, "sync", c.s_out,
            lambda e,
            o=c.out[seq].rearrange("(cc p) t -> p cc t", p=128),
            i=c.XRES.ap()[:, :].rearrange("p (cc t) -> p cc t", cc=NCC):
            e.dma_start(o, i), waits=waits, k=16)
    for b in c.xres:
        b.read_by(tok)


def build_program(stage_shapes, n_blocks=4):
    nc = bass.Bass("TRN2", target_bir_lowering=False)
    c = make_ctx(nc, stage_shapes, n_blocks)
    c.WRA = WRing(c, Carver(nc.alloc_sbuf_tensor("WRA", [128, 6144], BF16),
                            6144, "WRA"), 2048, 3, slot0=0)
    c.WRB = WRing(c, Carver(nc.alloc_sbuf_tensor("WRB", [128, 2048], BF16),
                            2048, "WRB"), 1024, 2, slot0=3)
    build_prologue(c)
    for _e in ENGINES:
        c.P.wait(_e, *c.tok_init)
    E(c.P, "scalar", c.s_act, lambda e: e.activation(
        c.DUMMY.ap()[:, :], c.ONESC.ap()[:, :], AF.Square))
    phases = [phase_l0attn]
    if n_blocks > 1:
        phases.append(phase_l0ffn)
    if n_blocks > 2:
        phases.append(phase_l1attn)
    if n_blocks > 3:
        phases.append(phase_l1ffn)
    for seq in range(NB):
        phase_load_x(c, seq)
        for ph in phases:
            ph(c)
        phase_store(c, seq)
    P = c.P
    P.on("sync", lambda e, s=c.s_out, v=c.s_out.val: e.wait_ge(s.h, v))
    P.finalize()
    return nc, c


_BUILT = {}


def kernel(**inputs):
    percore = stage_inputs_per_core(inputs)
    shapes = {k: v.shape for k, v in percore[0].items()}
    key = tuple(sorted((k, tuple(s)) for k, s in shapes.items()))
    nc, _ = build_program(shapes, n_blocks=N_BLOCKS)
    res = run_bass_kernel_spmd(nc, percore, core_ids=list(range(8)))
    out = np.empty((16, 2048, 1024), np.float32)
    for i in range(8):
        y = res.results[i]["y"]
        out[2 * i:2 * i + 2] = np.asarray(y, np.float32).transpose(0, 2, 1)
    return out


N_BLOCKS = 4


def phase_l0ffn(c):
    P = c.P
    f0, b0 = c.AF.mark(), c.AB.mark()
    XN = [c.AB.buf(512) for _ in range(NCC)]
    SQB = c.AB.buf(512)
    TMPB = c.AB.buf(512)
    H = [c.AB.buf(512) for _ in range(32)]
    CB = [c.AB.buf(512) for _ in range(2)]
    TMPF = c.AB.buf(512)
    OH8 = Buf(c.AB.take(1024, rows=8))
    GSR = Buf(c.AF.take(512, rows=32))
    GT = Buf(c.AF.take(32))
    WEF8 = Buf(c.AB.take(512, rows=8))
    PRB = Buf(c.AF.take(8))
    M8 = Buf(c.AF.take(8))
    MSK = Buf(c.AF.take(8))
    W8B = c.AB.buf(8)
    RM = Buf(c.AF.take(4))
    for b in (XN + H + CB + [SQB, TMPB, TMPF, OH8, GSR, GT, WEF8, PRB, M8,
                             MSK, W8B, RM]):
        b.ready = list(c.AB.fence)
    _t = E(P, "sync", c.s_ld, lambda e, d=OH8.ap, s=c.inp["c_oh8"]:
           e.dma_start(d, s), waits=OH8.war() + ((c.s_ld, c.s_ld.val),), k=16)
    OH8.wrote(_t)

    def expert_cfg(e):
        if e < 8:
            return (c.scr["w_ew1"][e], c.scr["w_ew2"][e],
                    c.prol["w_ew1"], c.prol["w_ew2"],
                    c.cols["c0_eb1"].ap()[:, e * 32:(e + 1) * 32],
                    c.cols["c0_eb2"].ap()[:, e * 8:(e + 1) * 8], False)
        return (c.scr["w_shw1"], c.scr["w_shw2"],
                c.prol["w_shw1"], c.prol["w_shw2"],
                c.cols["c0_shb1"].ap()[:, :],
                c.cols["c0_shb2"].ap()[:, :], True)

    for tt in range(T_SEQ // 512):
        t0 = tt * 512
        emit_preln(c, t0, 512, XN, SQB, TMPB)
        for mc in range(NCC):
            xap = c.XRES.ap()[:, mc * T_SEQ + t0: mc * T_SEQ + t0 + 512]
            ta = E(P, "vector", c.s_dve,
                   lambda e, o=xap, a=xap, b=XN[mc].ap:
                   e.tensor_tensor(o, a, b, OP.add),
                   waits=XN[mc].raw() + c.xres[mc].war())
            c.xres[mc].add_write(ta)
            XN[mc].read_by(ta)
        tok = None
        for kc in range(NCC):
            tok = E(P, "tensor", c.s_pe,
                    lambda e, o=c.PS[2].ap()[0:8, :],
                    l=c.GATE0.ap()[:, kc * 8:(kc + 1) * 8], r=XN[kc].ap,
                    s=(kc == 0), p=(kc == NCC - 1):
                    e.matmul(o, l, r, start=s, stop=p),
                    waits=XN[kc].raw() + (c.PB[2].war() if kc == 0 else ()))
            XN[kc].read_by(tok)
        c.PB[2].wrote(tok)
        tg = E(P, "scalar", c.s_act,
               lambda e, o=GSR.ap[0:8, :], i=c.PS[2].ap()[0:8, :],
               b=c.GATEB0.ap()[:, :]:
               e.activation(o, i, AF.Identity, bias=b),
               waits=c.PB[2].raw() + GSR.war() + c.tok_init)
        GSR.wrote(tg)
        c.PB[2].read_by(tg)
        for tb in range(4):
            ttr = E(P, "tensor", c.s_pe,
                    lambda e, o=c.PS[3].ap()[:, 0:32],
                    i=GSR.ap[0:32, tb * 128:(tb + 1) * 128],
                    ident=c.IDENTF.ap()[0:32, 0:32]:
                    e.transpose(o, i, ident),
                    waits=GSR.raw() + c.PB[3].war() + c.tok_init)
            c.PB[3].wrote(ttr)
            GSR.read_by(ttr)
            te = E(P, "vector", c.s_dve,
                   lambda e, o=GT.ap[:, tb * 8:(tb + 1) * 8],
                   i=c.PS[3].ap()[:, 0:8]:
                   e.tensor_copy(o, i),
                   waits=c.PB[3].raw() + (GT.war() if tb == 0 else ()))
            (GT.wrote if tb == 0 else GT.add_write)(te)
            c.PB[3].read_by(te)
        for tb in range(4):
            gtb = GT.ap[:, tb * 8:(tb + 1) * 8]
            t1 = E(P, "vector", c.s_dve,
                   lambda e, o=RM.ap[:, 0:1], i=gtb:
                   e.reduce_max(o, i, axis=AX), waits=GT.raw() + RM.war())
            t2 = E(P, "vector", c.s_dve,
                   lambda e, o=RM.ap[:, 1:2], i=RM.ap[:, 0:1]:
                   e.tensor_scalar(o, i, -1.0, None, OP.mult))
            RM.wrote(t2)
            GT.read_by(t1)
            t3 = E(P, "scalar", c.s_act,
                   lambda e, o=PRB.ap, i=gtb, b=RM.ap[:, 1:2],
                   a=RM.ap[:, 2:3]:
                   e.activation(o, i, AF.Exp, bias=b, accum_out=a),
                   waits=(t2,) + PRB.war())
            PRB.wrote(t3)
            RM.add_write(t3)
            t4 = E(P, "vector", c.s_dve,
                   lambda e, o=M8.ap, i=PRB.ap: e.max(o, i),
                   waits=PRB.raw() + M8.war())
            M8.wrote(t4)
            PRB.read_by(t4)
            t5 = E(P, "vector", c.s_dve,
                   lambda e, o=MSK.ap, i=PRB.ap, s=M8.ap[:, 1:2]:
                   e.tensor_single_scalar(o, i, s, OP.is_ge),
                   waits=M8.raw() + MSK.war(), hard=True)
            MSK.wrote(t5)
            M8.read_by(t5)
            t6 = E(P, "vector", c.s_dve,
                   lambda e, o=MSK.ap, a=PRB.ap, b=MSK.ap:
                   e.tensor_tensor(o, a, b, OP.mult))
            MSK.wrote(t6)
            PRB.read_by(t6)
            t7 = E(P, "vector", c.s_dve,
                   lambda e, o=RM.ap[:, 3:4], i=MSK.ap:
                   e.tensor_reduce(o, i, AX, OP.add))
            RM.add_write(t7)
            t8 = E(P, "vector", c.s_dve,
                   lambda e, o=RM.ap[:, 0:1], a=RM.ap[:, 2:3], b=RM.ap[:, 3:4]:
                   e.scalar_tensor_tensor(o, a, 1e-9, b, OP.mult, OP.add),
                   hard=True)
            t9 = E(P, "vector", c.s_dve,
                   lambda e, o=RM.ap[:, 1:2], i=RM.ap[:, 0:1]:
                   e.reciprocal(o, i))
            RM.wrote(t9)
            t10 = E(P, "vector", c.s_dve,
                    lambda e, o=W8B.ap, i=MSK.ap, s=RM.ap[:, 1:2]:
                    e.tensor_scalar(o, i, s, None, OP.mult),
                    waits=W8B.war(), hard=True)
            W8B.wrote(t10)
            MSK.read_by(t10)
            RM.read_by(t10)
            tw = E(P, "tensor", c.s_pe,
                   lambda e, o=c.PS[3].ap().bitcast(BF16)[0:8, 0:128],
                   i=W8B.ap, ident=c.IDENT.ap()[:, :]:
                   e.transpose(o, i, ident),
                   waits=W8B.raw() + c.PB[3].war())
            c.PB[3].wrote(tw)
            W8B.read_by(tw)
            twe = E(P, "vector", c.s_dve,
                    lambda e, o=WEF8.ap[0:8, tb * 128:(tb + 1) * 128],
                    i=c.PS[3].ap().bitcast(BF16)[0:8, 0:128]:
                    e.tensor_copy(o, i),
                    waits=c.PB[3].raw() + (WEF8.war() if tb == 0 else ()))
            (WEF8.wrote if tb == 0 else WEF8.add_write)(twe)
            c.PB[3].read_by(twe)
        for ex in range(9):
            w1s, w2s, p1, p2, b1ap, b2ap, shared = expert_cfg(ex)
            if not shared:
                tcb = E(P, "tensor", c.s_pe,
                        lambda e, o=c.PS[0].ap()[:, :],
                        l=OH8.ap[0:8, ex * 128:(ex + 1) * 128],
                        r=WEF8.ap[0:8, :]:
                        e.matmul(o, l, r, start=True, stop=True),
                        waits=OH8.raw() + WEF8.raw() + c.PB[0].war())
                c.PB[0].wrote(tcb)
                OH8.read_by(tcb)
                WEF8.read_by(tcb)
                cb = CB[ex % 2]
                tcv = E(P, "scalar", c.s_act,
                        lambda e, o=cb.ap, i=c.PS[0].ap()[:, :]:
                        e.activation(o, i, AF.Identity),
                        waits=c.PB[0].raw() + cb.war())
                cb.wrote(tcv)
                c.PB[0].read_by(tcv)
            for sup in range(16):
                wb = c.WRA.load(
                    w1s.rearrange("(kc p) m -> p kc m", p=128)
                    [:, :, sup * 256:(sup + 1) * 256], cols=2048, extra=(p1,))
                for mt in range(2):
                    it = sup * 2 + mt
                    psx, pbx = ((c.PS[4], c.PB[4]) if it % 2 == 0
                                else (c.PS[5], c.PB[5]))
                    tokm = None
                    for kc in range(NCC):
                        tokm = E(P, "tensor", c.s_pe,
                                 lambda e, o=psx.ap()[:, :],
                                 l=wb.ap[:, kc * 256 + mt * 128:
                                         kc * 256 + mt * 128 + 128],
                                 r=XN[kc].ap, s=(kc == 0), p=(kc == NCC - 1):
                                 e.matmul(o, l, r, start=s, stop=p),
                                 waits=wb.raw() + XN[kc].raw()
                                 + (pbx.war() if kc == 0 else ()))
                        wb.read_by(tokm)
                        XN[kc].read_by(tokm)
                    pbx.wrote(tokm)
                    th = E(P, "scalar", c.s_act,
                           lambda e, o=H[it].ap, i=psx.ap()[:, :],
                           b=b1ap[:, it:it + 1]:
                           e.activation(o, i, AF.Gelu_apprx_tanh, bias=b),
                           waits=pbx.raw() + H[it].war() + c.tok_init)
                    H[it].wrote(th)
                    pbx.read_by(th)
            for db in range(4):
                toks = [None, None]
                for kq in range(4):
                    wb2 = c.WRA.load(
                        w2s.rearrange("(kc p) m -> p kc m", p=128)
                        [:, kq * 8:(kq + 1) * 8, db * 256:(db + 1) * 256],
                        cols=2048, extra=(p2,))
                    for dt in range(2):
                        psx, pbx = ((c.PS[6], c.PB[6]) if dt == 0
                                    else (c.PS[7], c.PB[7]))
                        for k8 in range(NCC):
                            kc = kq * 8 + k8
                            tokm = E(P, "tensor", c.s_pe,
                                     lambda e, o=psx.ap()[:, :],
                                     l=wb2.ap[:, k8 * 256 + dt * 128:
                                              k8 * 256 + dt * 128 + 128],
                                     r=H[kc].ap,
                                     s=(kc == 0), p=(kc == 31):
                                     e.matmul(o, l, r, start=s, stop=p),
                                     waits=wb2.raw() + H[kc].raw()
                                     + (pbx.war() if kc == 0 else ()))
                            wb2.read_by(tokm)
                            H[kc].read_by(tokm)
                            toks[dt] = tokm
                for dt in range(2):
                    mc = db * 2 + dt
                    psx, pbx = ((c.PS[6], c.PB[6]) if dt == 0
                                else (c.PS[7], c.PB[7]))
                    pbx.wrote(toks[dt])
                    xap = c.XRES.ap()[:, mc * T_SEQ + t0: mc * T_SEQ + t0 + 512]
                    if shared:
                        tev = E(P, "vector", c.s_dve,
                                lambda e, o=xap, a=psx.ap()[:, :],
                                s=b2ap[:, mc:mc + 1], b=xap:
                                e.scalar_tensor_tensor(o, a, s, b,
                                                       OP.add, OP.add),
                                waits=pbx.raw() + c.xres[mc].war()
                                + c.tok_init)
                        c.xres[mc].add_write(tev)
                        pbx.read_by(tev)
                    else:
                        cb = CB[ex % 2]
                        tev = E(P, "vector", c.s_dve,
                                lambda e, o=TMPF.ap, a=psx.ap()[:, :],
                                s=b2ap[:, mc:mc + 1], b=cb.ap:
                                e.scalar_tensor_tensor(o, a, s, b,
                                                       OP.add, OP.mult),
                                waits=pbx.raw() + cb.raw() + TMPF.war()
                                + c.tok_init)
                        TMPF.wrote(tev)
                        pbx.read_by(tev)
                        cb.read_by(tev)
                        tad = E(P, "vector", c.s_dve,
                                lambda e, o=xap, a=xap, b=TMPF.ap:
                                e.tensor_tensor(o, a, b, OP.add),
                                waits=c.xres[mc].war())
                        c.xres[mc].add_write(tad)
                        TMPF.read_by(tad)
    c.dbgf = dict(XN=XN, H=H, CB=CB, GT=GT, WEF8=WEF8, GSR=GSR, OH8=OH8)
    c.AF.reset(f0)
    c.AB.reset(b0)
    set_fence(c)


def phase_l1attn(c):
    P = c.P
    f0, b0 = c.AF.mark(), c.AB.mark()
    SC = [Buf(c.AF.take(2048)) for _ in range(2)]
    RM = [Buf(c.AF.take(4)) for _ in range(2)]
    KFr = Buf(c.AB.take(16384))
    bA = c.AB.mark()
    XN = [c.AB.buf(512 * 4) for _ in range(NCC)]
    SQB = c.AB.buf(512)
    TMPB = c.AB.buf(512)
    BVT = c.AB.buf(1024)
    STG = [c.AB.buf(512) for _ in range(2)]
    bA2 = c.AB.mark()
    c.AB.reset(bA)
    VFr = Buf(c.AB.take(16384))
    QR = [c.AB.buf(1024) for _ in range(2)]
    PBUF = c.AB.buf(2048)
    PT = c.AB.buf(2048)
    bB2 = c.AB.mark()
    c.AB.reset(max(bA2, bB2))
    partA = XN + [SQB, TMPB, BVT] + STG
    partB = [VFr, PBUF, PT] + QR
    for b in partA:
        b.aliases = partB
    for b in partB:
        b.aliases = partA
    s_kv = [P.sem(f"s_kv{len(P.sems)}") for _ in range(10)]
    for b in (XN + STG + QR + SC + RM + [SQB, TMPB, BVT, PBUF, PT, KFr, VFr]):
        b.ready = list(c.AB.fence)
    qtoks = [[] for _ in range(4)]
    kstores = []
    vstores = []

    for half in range(2):
        tb0 = E(P, "tensor", c.s_pe,
                lambda e, o=c.PS[0].ap()[:, :], l=c.ONESR.ap()[:, :],
                r=c.BVROW.ap()[0:1, half * 512:(half + 1) * 512]:
                e.matmul(o, l, r, start=True, stop=True),
                waits=c.tok_init + c.PB[0].war())
        c.PB[0].wrote(tb0)
        tb1 = E(P, "vector", c.s_dve,
                lambda e, o=BVT.ap[:, half * 512:(half + 1) * 512],
                i=c.PS[0].ap()[:, :]: e.tensor_copy(o, i),
                waits=c.PB[0].raw() + (BVT.war() if half == 0 else ()))
        (BVT.wrote if half == 0 else BVT.add_write)(tb1)
        c.PB[0].read_by(tb1)

    emit_preln(c, 0, T_SEQ, XN, SQB, TMPB)

    def proj(wname, bias_col, out_cb, kstore):
        for sup in range(4):
            wb = c.WRA.load(wsup(c, wname, sup), cols=2048,
                            extra=(c.prol[wname],))
            for mt in range(2):
                mc = sup * 2 + mt
                for nb in range(4):
                    psx, pbx = ((c.PS[4], c.PB[4]) if nb % 2 == 0
                                else (c.PS[5], c.PB[5]))
                    tokm = None
                    for kc in range(NCC):
                        tokm = E(P, "tensor", c.s_pe,
                                 lambda e, o=psx.ap()[:, :],
                                 l=wb.ap[:, kc * 256 + mt * 128:
                                         kc * 256 + mt * 128 + 128],
                                 r=XN[kc].ap[:, nb * 512:(nb + 1) * 512],
                                 s=(kc == 0), p=(kc == NCC - 1):
                                 e.matmul(o, l, r, start=s, stop=p),
                                 waits=wb.raw() + XN[kc].raw()
                                 + (pbx.war() if kc == 0 else ()))
                        wb.read_by(tokm)
                        XN[kc].read_by(tokm)
                    pbx.wrote(tokm)
                    out_cb(mc, nb, psx, pbx)

    def q_out(mc, nb, psx, pbx):
        st = STG[nb % 2]
        tq = E(P, "scalar", c.s_act,
               lambda e, o=st.ap, i=psx.ap()[:, :],
               b=c.cols["c1_bq"].ap()[:, mc:mc + 1]:
               e.activation(o, i, AF.Identity, bias=b),
               waits=pbx.raw() + st.war() + c.tok_init)
        st.wrote(tq)
        pbx.read_by(tq)
        sem = s_kv[nb % 2]
        td = E(P, "sync", sem,
               lambda e, o=c.QS[mc * 128:(mc + 1) * 128,
                              nb * 512:(nb + 1) * 512], i=st.ap:
               e.dma_start(o, i),
               waits=st.raw() + ((sem, sem.val),), k=16)
        st.read_by(td)
        qtoks[nb].append(td)

    def k_out(mc, nb, psx, pbx):
        st = STG[nb % 2]
        tk = E(P, "scalar", c.s_act,
               lambda e, o=st.ap, i=psx.ap()[:, :],
               b=c.cols["c1_bk"].ap()[:, mc:mc + 1]:
               e.activation(o, i, AF.Identity, bias=b),
               waits=pbx.raw() + st.war() + c.tok_init)
        st.wrote(tk)
        pbx.read_by(tk)
        sem = s_kv[2 + nb % 2]
        td = E(P, "sync", sem,
               lambda e, o=c.KS[mc * 128:(mc + 1) * 128,
                              nb * 512:(nb + 1) * 512], i=st.ap:
               e.dma_start(o, i),
               waits=st.raw() + ((sem, sem.val),), k=16)
        st.read_by(td)
        kstores.append(td)

    proj("w_1Wq", None, q_out, False)
    proj("w_1Wk", None, k_out, False)
    tkf = E(P, "sync", s_kv[6],
            lambda e, o=KFr.ap.rearrange("p (kc t) -> p kc t", kc=8),
            i=c.KS.rearrange("(kc p) t -> p kc t", p=128):
            e.dma_start(o, i),
            waits=KFr.war() + tuple(kstores) + ((s_kv[6], 0),), k=16)
    KFr.wrote(tkf)

    for sup in range(4):
        wb = c.WRA.load(wsup(c, "w_1Wv", sup), cols=2048,
                        extra=(c.prol["w_1Wv"],))
        for tb in range(16):
            psx, pbx = ((c.PS[4], c.PB[4]) if tb % 2 == 0
                        else (c.PS[5], c.PB[5]))
            tokm = None
            for kc in range(NCC):
                tokm = E(P, "tensor", c.s_pe,
                         lambda e, o=psx.ap()[:, 0:256],
                         l=XN[kc].ap[:, tb * 128:(tb + 1) * 128],
                         r=wb.ap[:, kc * 256:(kc + 1) * 256],
                         s=(kc == 0), p=(kc == NCC - 1):
                         e.matmul(o, l, r, start=s, stop=p),
                         waits=wb.raw() + XN[kc].raw()
                         + (pbx.war() if kc == 0 else ()))
                wb.read_by(tokm)
                XN[kc].read_by(tokm)
            pbx.wrote(tokm)
            st = STG[tb % 2]
            tv = E(P, "vector", c.s_dve,
                   lambda e, o=st.ap[:, 0:256], a=psx.ap()[:, 0:256],
                   b=BVT.ap[:, sup * 256:(sup + 1) * 256]:
                   e.tensor_tensor(o, a, b, OP.add),
                   waits=pbx.raw() + BVT.raw() + st.war())
            st.wrote(tv)
            pbx.read_by(tv)
            BVT.read_by(tv)
            sem = s_kv[4 + tb % 2]
            td = E(P, "sync", sem,
                   lambda e, o=c.VS[tb * 128:(tb + 1) * 128,
                                    sup * 256:(sup + 1) * 256],
                   i=st.ap[:, 0:256]:
                   e.dma_start(o, i),
                   waits=st.raw() + ((sem, sem.val),), k=16)
            st.read_by(td)
            vstores.append(td)

    vftoks = []
    for ck in range(4):
        tvf = E(P, "sync", s_kv[7],
                lambda e,
                o=VFr.ap[:, ck * 4096:(ck + 1) * 4096].rearrange(
                    "p (tb e) -> p tb e", e=1024),
                i=c.VS.rearrange("(tb p) e -> p tb e", p=128)
                [:, ck * 4:(ck + 1) * 4, :]:
                e.dma_start(o, i),
                waits=(VFr.war() + tuple(vstores) if ck == 0
                       else ()) + ((s_kv[7], s_kv[7].val),), k=16)
        (VFr.wrote if ck == 0 else VFr.add_write)(tvf)
        vftoks.append(tvf)

    def load_q(qc):
        qb = QR[qc % 2]
        sem = s_kv[8 + qc % 2]
        tql = E(P, "sync", sem,
                lambda e, o=qb.ap.rearrange("p (mc t) -> p mc t", t=128),
                i=c.QS.rearrange("(mc p) t -> p mc t", p=128)
                [:, :, qc * 128:(qc + 1) * 128]:
                e.dma_start(o, i),
                waits=qb.war() + tuple(qtoks[qc // 4])
                + ((sem, sem.val),), k=16)
        qb.wrote(tql)
    load_q(0)
    load_q(1)

    for qc in range(16):
        kr = (qc + 1) * 128
        nkb = (kr + 511) // 512
        qb = QR[qc % 2]
        SCq, RMq = SC[qc % 2], RM[qc % 2]
        for nb in range(nkb):
            w = min(512, kr - nb * 512)
            psx, pbx = ((c.PS[4], c.PB[4]) if nb % 2 == 0
                        else (c.PS[5], c.PB[5]))
            tokm = None
            for kc in range(NCC):
                tokm = E(P, "tensor", c.s_pe,
                         lambda e, o=psx.ap()[:, 0:w],
                         l=qb.ap[:, kc * 128:(kc + 1) * 128],
                         r=KFr.ap[:, kc * 2048 + nb * 512:
                                  kc * 2048 + nb * 512 + w],
                         s=(kc == 0), p=(kc == NCC - 1):
                         e.matmul(o, l, r, start=s, stop=p),
                         waits=KFr.raw() + qb.raw()
                         + (pbx.war() if kc == 0 else ()))
                KFr.read_by(tokm)
                qb.read_by(tokm)
            pbx.wrote(tokm)
            tse = E(P, "scalar", c.s_act,
                    lambda e, o=SCq.ap[:, nb * 512:nb * 512 + w],
                    i=psx.ap()[:, 0:w]: e.activation(o, i, AF.Identity),
                    waits=pbx.raw() + (SCq.war() if nb == 0 else ()))
            (SCq.wrote if nb == 0 else SCq.add_write)(tse)
            pbx.read_by(tse)
        if qc + 2 < 16:
            load_q(qc + 2)
        tmk = E(P, "vector", c.s_dve,
                lambda e, o=SCq.ap[:, qc * 128:kr], a=SCq.ap[:, qc * 128:kr],
                b=c.CAUS.ap()[:, :]:
                e.tensor_tensor(o, a, b, OP.add),
                waits=SCq.raw() + c.tok_init)
        SCq.add_write(tmk)
        t1 = E(P, "vector", c.s_dve,
               lambda e, o=RMq.ap[:, 0:1], i=SCq.ap[:, 0:kr]:
               e.reduce_max(o, i, axis=AX), waits=SCq.raw() + RMq.war())
        t2 = E(P, "vector", c.s_dve,
               lambda e, o=RMq.ap[:, 1:2], i=RMq.ap[:, 0:1]:
               e.tensor_scalar(o, i, -1.0 / 32, None, OP.mult))
        RMq.wrote(t2)
        SCq.read_by(t1)
        t3 = E(P, "scalar", c.s_act,
               lambda e, o=PBUF.ap[:, 0:kr], i=SCq.ap[:, 0:kr],
               b=RMq.ap[:, 1:2], a=RMq.ap[:, 2:3]:
               e.activation(o, i, AF.Exp, bias=b, scale=1.0 / 32, accum_out=a),
               waits=(t2,) + PBUF.war() + SCq.raw())
        PBUF.wrote(t3)
        RMq.add_write(t3)
        SCq.read_by(t3)
        t4 = E(P, "vector", c.s_dve,
               lambda e, o=RMq.ap[:, 3:4], i=RMq.ap[:, 2:3]:
               e.reciprocal(o, i), waits=RMq.raw())
        RMq.add_write(t4)
        t5 = E(P, "vector", c.s_dve,
               lambda e, o=PBUF.ap[:, 0:kr], i=PBUF.ap[:, 0:kr],
               s=RMq.ap[:, 3:4]:
               e.tensor_scalar(o, i, s, None, OP.mult),
               waits=PBUF.raw() + (t4,), hard=True)
        PBUF.wrote(t5)
        RMq.read_by(t5)
        for kb in range(qc + 1):
            psb = c.PS[1] if kb % 2 == 0 else c.PS[2]
            pbb = c.PB[1] if kb % 2 == 0 else c.PB[2]
            ttr = E(P, "tensor", c.s_pe,
                    lambda e, o=psb.ap().bitcast(BF16)[:, 0:128],
                    i=PBUF.ap[:, kb * 128:(kb + 1) * 128],
                    ident=c.IDENT.ap()[:, :]:
                    e.transpose(o, i, ident),
                    waits=PBUF.raw() + pbb.war() + c.tok_init)
            pbb.wrote(ttr)
            PBUF.read_by(ttr)
            if kb % 2 == 0:
                tte = E(P, "vector", c.s_dve,
                        lambda e, o=PT.ap[:, kb * 128:(kb + 1) * 128],
                        i=psb.ap().bitcast(BF16)[:, 0:128]:
                        e.tensor_copy(o, i),
                        waits=pbb.raw() + (PT.war() if kb == 0 else ()))
            else:
                tte = E(P, "scalar", c.s_act,
                        lambda e, o=PT.ap[:, kb * 128:(kb + 1) * 128],
                        i=psb.ap().bitcast(BF16)[:, 0:128]:
                        e.activation(o, i, AF.Identity),
                        waits=pbb.raw())
            (PT.wrote if kb == 0 else PT.add_write)(tte)
            pbb.read_by(tte)
        toks = [None, None]
        for mt in range(8):
            psx, pbx = ((c.PS[6], c.PB[6]) if mt < 4 else (c.PS[7], c.PB[7]))
            col = (mt % 4) * 128
            for kb in range(qc + 1):
                tokm = E(P, "tensor", c.s_pe,
                         lambda e, o=psx.ap()[:, col:col + 128],
                         l=VFr.ap[:, kb * 1024 + mt * 128:
                                  kb * 1024 + mt * 128 + 128],
                         r=PT.ap[:, kb * 128:(kb + 1) * 128],
                         s=(kb == 0), p=(kb == qc):
                         e.matmul(o, l, r, start=s, stop=p),
                         waits=(vftoks[kb // 4],) + PT.raw()
                         + (pbx.war() if (kb == 0 and mt % 4 == 0) else ()))
                VFr.read_by(tokm)
                PT.read_by(tokm)
            toks[mt // 4] = tokm
        for half in range(2):
            psx, pbx = ((c.PS[6], c.PB[6]) if half == 0 else (c.PS[7], c.PB[7]))
            pbx.wrote(toks[half])
            cc0 = half * 4
            base = cc0 * T_SEQ + qc * 128
            oview = (c.XRES.ap()[:, base:base + 3 * T_SEQ + 128]
                     .rearrange("p (g f) -> p g f", g=4)
                     [:, :, 0:128] if False else
                     c.XRES.ap()[:, cc0 * T_SEQ + qc * 128:
                                 cc0 * T_SEQ + qc * 128 + 3 * T_SEQ + 128])
            ovw = (c.XRES.ap()[:, cc0 * T_SEQ:(cc0 + 4) * T_SEQ]
                   .rearrange("p (g f) -> p g f", g=4)
                   [:, :, qc * 128:(qc + 1) * 128])
            waits = (pbx.raw()
                     + tuple(t for b in c.xres[cc0:cc0 + 4] for t in b.war()))
            tev = E(P, "vector", c.s_dve,
                    lambda e, o=ovw,
                    a=ovw,
                    b=psx.ap()[:, :].rearrange("p (g f) -> p g f", g=4):
                    e.tensor_tensor(o, a, b, OP.add), waits=waits)
            for b in c.xres[cc0:cc0 + 4]:
                b.add_write(tev)
            pbx.read_by(tev)
    c.AF.reset(f0)
    c.AB.reset(b0)
    set_fence(c)


def phase_l1ffn(c):
    P = c.P
    f0, b0 = c.AF.mark(), c.AB.mark()
    XN = [c.AB.buf(2048) for _ in range(NCC)]
    SQB = c.AB.buf(512)
    TMPB = c.AB.buf(512)
    SGF = Buf(c.AF.take(2048, rows=32))
    SGT = Buf(c.AF.take(256))
    WEF = Buf(c.AB.take(1024, rows=16))
    OH16 = Buf(c.AB.take(2048, rows=16))
    GA = c.AB.buf(4096)
    GB = c.AB.buf(4096)
    UU = c.AB.buf(4096)
    CBp = [c.AB.buf(2048) for _ in range(2)]
    W16B = c.AB.buf(16)
    GRP = Buf(c.AF.take(64))
    M8A = Buf(c.AF.take(8))
    MS16 = Buf(c.AF.take(16))
    RC = Buf(c.AF.take(4))
    for b in (XN + CBp + [SQB, TMPB, SGF, SGT, WEF, OH16, GA, GB, UU, W16B,
                          GRP, M8A, MS16, RC]):
        b.ready = list(c.AB.fence)
    _t = E(P, "sync", c.s_ld, lambda e, d=OH16.ap, s=c.inp["c_oh16"]:
           e.dma_start(d, s), waits=OH16.war() + ((c.s_ld, c.s_ld.val),), k=16)
    OH16.wrote(_t)
    emit_preln(c, 0, T_SEQ, XN, SQB, TMPB)
    for nb in range(4):
        tok = None
        for kc in range(NCC):
            tok = E(P, "tensor", c.s_pe,
                    lambda e, o=c.PS[2].ap()[0:16, :],
                    l=c.GATE1.ap()[:, kc * 16:(kc + 1) * 16],
                    r=XN[kc].ap[:, nb * 512:(nb + 1) * 512],
                    s=(kc == 0), p=(kc == NCC - 1):
                    e.matmul(o, l, r, start=s, stop=p),
                    waits=XN[kc].raw() + (c.PB[2].war() if kc == 0 else ())
                    + c.tok_init)
            XN[kc].read_by(tok)
        c.PB[2].wrote(tok)
        tg = E(P, "scalar", c.s_act,
               lambda e, o=SGF.ap[0:16, nb * 512:(nb + 1) * 512],
               i=c.PS[2].ap()[0:16, :]:
               e.activation(o, i, AF.Sigmoid),
               waits=c.PB[2].raw() + (SGF.war() if nb == 0 else ()))
        (SGF.wrote if nb == 0 else SGF.add_write)(tg)
        c.PB[2].read_by(tg)

    for half in range(2):
        t0 = half * 1024
        for tb in range(8):
            tbg = half * 8 + tb
            ttr = E(P, "tensor", c.s_pe,
                    lambda e, o=c.PS[3].ap()[:, 0:32],
                    i=SGF.ap[0:32, tbg * 128:(tbg + 1) * 128],
                    ident=c.IDENTF.ap()[0:32, 0:32]:
                    e.transpose(o, i, ident),
                    waits=SGF.raw() + c.PB[3].war() + c.tok_init)
            c.PB[3].wrote(ttr)
            SGF.read_by(ttr)
            te = E(P, "vector", c.s_dve,
                   lambda e, o=SGT.ap[:, tb * 16:(tb + 1) * 16],
                   i=c.PS[3].ap()[:, 0:16]:
                   e.tensor_copy(o, i),
                   waits=c.PB[3].raw() + (SGT.war() if tb == 0 else ()))
            (SGT.wrote if tb == 0 else SGT.add_write)(te)
            c.PB[3].read_by(te)
        for tb in range(8):
            Bap = SGT.ap[:, tb * 16:(tb + 1) * 16]
            g3 = GRP.ap
            x_ = [Bap.rearrange("p (g i) -> p g i", i=4)[:, :, i]
                  for i in range(4)]
            ops = [
                ("s1", OP.max, x_[0], x_[1]), ("t1", OP.min, x_[0], x_[1]),
                ("s2", OP.max, x_[2], x_[3]), ("t2", OP.min, x_[2], x_[3]),
            ]
            cols = {}
            off = 0
            tokl = None
            for nm, op, a, b in ops:
                cols[nm] = g3[:, off:off + 4]
                tokl = E(P, "vector", c.s_dve,
                         lambda e, o=cols[nm], a=a, b=b:
                         e.tensor_tensor(o, a, b, op),
                         waits=SGT.raw() + (GRP.war() if off == 0 else ()))
                SGT.read_by(tokl)
                off += 4
            for nm, op, a, b in (("m1", OP.max, cols["s1"], cols["s2"]),
                                 ("ge", OP.min, cols["s1"], cols["s2"]),
                                 ("gf", OP.max, cols["t1"], cols["t2"])):
                cols[nm] = g3[:, off:off + 4]
                E(P, "vector", c.s_dve,
                  lambda e, o=cols[nm], a=a, b=b: e.tensor_tensor(o, a, b, op))
                off += 4
            cols["m2"] = g3[:, off:off + 4]; off += 4
            E(P, "vector", c.s_dve,
              lambda e, o=cols["m2"], a=cols["ge"], b=cols["gf"]:
              e.tensor_tensor(o, a, b, OP.max))
            cols["gsum"] = g3[:, off:off + 4]; off += 4
            tgs = E(P, "vector", c.s_dve,
                    lambda e, o=cols["gsum"], a=cols["m1"], b=cols["m2"]:
                    e.tensor_tensor(o, a, b, OP.add))
            GRP.wrote(tgs)
            gs = cols["gsum"]
            pp = g3[:, off:off + 2]; off += 2
            qq = g3[:, off:off + 2]; off += 2
            m2g = g3[:, off:off + 1]; off += 1
            E(P, "vector", c.s_dve,
              lambda e, o=pp, a=gs[:, 0::2], b=gs[:, 1::2]:
              e.tensor_tensor(o, a, b, OP.max))
            E(P, "vector", c.s_dve,
              lambda e, o=qq, a=gs[:, 0::2], b=gs[:, 1::2]:
              e.tensor_tensor(o, a, b, OP.min))
            E(P, "vector", c.s_dve,
              lambda e, o=m2g, a=pp[:, 0:1], b=pp[:, 1:2]:
              e.tensor_tensor(o, a, b, OP.min))
            E(P, "vector", c.s_dve,
              lambda e, o=qq[:, 0:1], a=qq[:, 0:1], b=qq[:, 1:2]:
              e.tensor_tensor(o, a, b, OP.max))
            tm2g = E(P, "vector", c.s_dve,
                     lambda e, o=m2g, a=m2g, b=qq[:, 0:1]:
                     e.tensor_tensor(o, a, b, OP.max))
            GRP.add_write(tm2g)
            cols["ag"] = g3[:, off:off + 4]; off += 4
            tag = E(P, "vector", c.s_dve,
                    lambda e, o=cols["ag"], i=cols["gsum"], s=m2g:
                    e.tensor_single_scalar(o, i, s, OP.is_ge),
                    hard=True)
            GRP.add_write(tag)
            tms = None
            for g in range(4):
                tp1 = E(P, "vector", c.s_dve,
                        lambda e, o=MS16.ap[:, g * 4:(g + 1) * 4],
                        i=Bap[:, g * 4:(g + 1) * 4]:
                        e.tensor_scalar(o, i, 1e9, None, OP.add),
                        waits=SGT.raw() + (MS16.war() if g == 0 else ()))
                MS16.add_write(tp1)
                SGT.read_by(tp1)
                tms = E(P, "vector", c.s_dve,
                        lambda e, o=MS16.ap[:, g * 4:(g + 1) * 4],
                        i=MS16.ap[:, g * 4:(g + 1) * 4],
                        s=cols["ag"][:, g:g + 1]:
                        e.tensor_scalar(o, i, s, -1e9, OP.mult, OP.add),
                        waits=GRP.raw(), hard=True)
                MS16.add_write(tms)
                GRP.read_by(tms)
            tmi = E(P, "vector", c.s_dve,
                    lambda e, o=MS16.ap, a=MS16.ap, b=c.IOTAEPS.ap()[:, :]:
                    e.tensor_tensor(o, a, b, OP.add), waits=c.tok_init)
            MS16.wrote(tmi)
            tm84 = E(P, "vector", c.s_dve,
                     lambda e, o=M8A.ap, i=MS16.ap: e.max(o, i),
                     waits=M8A.war())
            M8A.wrote(tm84)
            MS16.read_by(tm84)
            ttk = E(P, "vector", c.s_dve,
                    lambda e, o=MS16.ap, i=MS16.ap, s=M8A.ap[:, 3:4]:
                    e.tensor_single_scalar(o, i, s, OP.is_ge),
                    waits=M8A.raw(), hard=True)
            MS16.wrote(ttk)
            M8A.read_by(ttk)
            tws = E(P, "vector", c.s_dve,
                    lambda e, o=MS16.ap, a=SGT.ap[:, tb * 16:(tb + 1) * 16],
                    b=MS16.ap:
                    e.tensor_tensor(o, a, b, OP.mult))
            MS16.wrote(tws)
            SGT.read_by(tws)
            trs = E(P, "vector", c.s_dve,
                    lambda e, o=RC.ap[:, 0:1], i=MS16.ap:
                    e.tensor_reduce(o, i, AX, OP.add), waits=RC.war())
            trs2 = E(P, "vector", c.s_dve,
                     lambda e, o=RC.ap[:, 1:2], i=RC.ap[:, 0:1]:
                     e.tensor_scalar(o, i, 1e-6, None, OP.add))
            trs3 = E(P, "vector", c.s_dve,
                     lambda e, o=RC.ap[:, 2:3], i=RC.ap[:, 1:2]:
                     e.reciprocal(o, i))
            RC.wrote(trs3)
            tw16 = E(P, "vector", c.s_dve,
                     lambda e, o=W16B.ap, i=MS16.ap, s=RC.ap[:, 2:3]:
                     e.tensor_scalar(o, i, s, None, OP.mult),
                     waits=W16B.war(), hard=True)
            W16B.wrote(tw16)
            MS16.read_by(tw16)
            RC.read_by(tw16)
            twt = E(P, "tensor", c.s_pe,
                    lambda e, o=c.PS[3].ap().bitcast(BF16)[0:16, 0:128],
                    i=W16B.ap, ident=c.IDENT.ap()[:, :]:
                    e.transpose(o, i, ident),
                    waits=W16B.raw() + c.PB[3].war())
            c.PB[3].wrote(twt)
            W16B.read_by(twt)
            twe = E(P, "vector", c.s_dve,
                    lambda e, o=WEF.ap[0:16, tb * 128:(tb + 1) * 128],
                    i=c.PS[3].ap().bitcast(BF16)[0:16, 0:128]:
                    e.tensor_copy(o, i),
                    waits=c.PB[3].raw() + (WEF.war() if tb == 0 else ()))
            (WEF.wrote if tb == 0 else WEF.add_write)(twe)
            c.PB[3].read_by(twe)

        def gu_compute(gbuf, w1s, w3s, p1, p3, cb, kwidth):
            for wname, prol, act in ((w1s, p1, True), (w3s, p3, False)):
                for sup in range(2):
                    wb = c.WRA.load(wname[:, sup * 256:(sup + 1) * 256]
                                    .rearrange("(kc p) m -> p kc m", p=128),
                                    cols=2048, extra=(prol,))
                    for mt in range(2):
                        fc = sup * 2 + mt
                        for nb in range(2):
                            psx, pbx = ((c.PS[4], c.PB[4]) if nb == 0
                                        else (c.PS[5], c.PB[5]))
                            tokm = None
                            for kc in range(NCC):
                                tokm = E(P, "tensor", c.s_pe,
                                         lambda e, o=psx.ap()[:, :],
                                         l=wb.ap[:, kc * 256 + mt * 128:
                                                 kc * 256 + mt * 128 + 128],
                                         r=XN[kc].ap[:, t0 + nb * 512:
                                                     t0 + (nb + 1) * 512],
                                         s=(kc == 0), p=(kc == NCC - 1):
                                         e.matmul(o, l, r, start=s, stop=p),
                                         waits=wb.raw() + XN[kc].raw()
                                         + (pbx.war() if kc == 0 else ()))
                                wb.read_by(tokm)
                                XN[kc].read_by(tokm)
                            pbx.wrote(tokm)
                            if act:
                                tev = E(P, "scalar", c.s_act,
                                        lambda e,
                                        o=gbuf.ap[:, fc * 1024 + nb * 512:
                                                  fc * 1024 + (nb + 1) * 512],
                                        i=psx.ap()[:, :]:
                                        e.activation(o, i, AF.Silu),
                                        waits=pbx.raw()
                                        + (gbuf.war() if fc == 0 and nb == 0
                                           else ()))
                                (gbuf.wrote if (fc == 0 and nb == 0)
                                 else gbuf.add_write)(tev)
                                pbx.read_by(tev)
                            else:
                                tev = E(P, "scalar", c.s_act,
                                        lambda e,
                                        o=UU.ap[:, fc * 1024 + nb * 512:
                                                fc * 1024 + (nb + 1) * 512],
                                        i=psx.ap()[:, :]:
                                        e.activation(o, i, AF.Identity),
                                        waits=pbx.raw()
                                        + (UU.war() if fc == 0 and nb == 0
                                           else ()))
                                (UU.wrote if (fc == 0 and nb == 0)
                                 else UU.add_write)(tev)
                                pbx.read_by(tev)
            for fc in range(4):
                tgu = E(P, "vector", c.s_dve,
                        lambda e, o=gbuf.ap[:, fc * 1024:(fc + 1) * 1024],
                        a=gbuf.ap[:, fc * 1024:(fc + 1) * 1024],
                        b=UU.ap[:, fc * 1024:(fc + 1) * 1024]:
                        e.tensor_tensor(o, a, b, OP.mult),
                        waits=gbuf.raw() + UU.raw())
                gbuf.add_write(tgu)
                UU.read_by(tgu)
                if cb is not None:
                    tcb = E(P, "vector", c.s_dve,
                            lambda e, o=gbuf.ap[:, fc * 1024:(fc + 1) * 1024],
                            a=gbuf.ap[:, fc * 1024:(fc + 1) * 1024],
                            b=cb.ap[:, 0:1024]:
                            e.tensor_tensor(o, a, b, OP.mult),
                            waits=cb.raw())
                    gbuf.add_write(tcb)
                    cb.read_by(tcb)

        def build_cb(e_idx, cb):
            for nb in range(2):
                tcm = E(P, "tensor", c.s_pe,
                        lambda e, o=c.PS[0].ap()[:, :],
                        l=OH16.ap[0:16, e_idx * 128:(e_idx + 1) * 128],
                        r=WEF.ap[0:16, nb * 512:(nb + 1) * 512]:
                        e.matmul(o, l, r, start=True, stop=True),
                        waits=OH16.raw() + WEF.raw() + c.PB[0].war())
                c.PB[0].wrote(tcm)
                OH16.read_by(tcm)
                WEF.read_by(tcm)
                tce = E(P, "scalar", c.s_act,
                        lambda e, o=cb.ap[:, nb * 512:(nb + 1) * 512],
                        i=c.PS[0].ap()[:, :]:
                        e.activation(o, i, AF.Identity),
                        waits=c.PB[0].raw() + (cb.war() if nb == 0 else ()))
                (cb.wrote if nb == 0 else cb.add_write)(tce)
                c.PB[0].read_by(tce)

        for pr in range(9):
            if pr < 8:
                members = []
                for m in range(2):
                    e_idx = pr * 2 + m
                    cb = CBp[m]
                    build_cb(e_idx, cb)
                    gbuf = GA if m == 0 else GB
                    gu_compute(gbuf, c.scr["w_1w1e"][e_idx],
                               c.scr["w_1w3e"][e_idx],
                               c.prol["w_1w1e"], c.prol["w_1w3e"], cb, 512)
                    members.append((gbuf, c.scr["w_1w2e"][e_idx],
                                    c.prol["w_1w2e"]))
            else:
                gu_compute(GA, c.scr["w_1w1s"][:, 0:512],
                           c.scr["w_1w3s"][:, 0:512],
                           c.prol["w_1w1s"], c.prol["w_1w3s"], None, 512)
                gu_compute(GB, c.scr["w_1w1s"][:, 512:1024],
                           c.scr["w_1w3s"][:, 512:1024],
                           c.prol["w_1w1s"], c.prol["w_1w3s"], None, 512)
                members = [(GA, c.scr["w_1w2s"][0:512, :], c.prol["w_1w2s"]),
                           (GB, c.scr["w_1w2s"][512:1024, :],
                            c.prol["w_1w2s"])]
            for sup in range(4):
                wbs = []
                for m, (gbuf, w2s, p2) in enumerate(members):
                    wbs.append(c.WRA.load(
                        w2s[:, sup * 256:(sup + 1) * 256]
                        .rearrange("(kc p) m -> p kc m", p=128),
                        cols=1024, extra=(p2,)))
                for mt in range(2):
                    mc = sup * 2 + mt
                    for nb in range(2):
                        psx, pbx = ((c.PS[6], c.PB[6]) if nb == 0
                                    else (c.PS[7], c.PB[7]))
                        tokm = None
                        nmm = 0
                        for m, (gbuf, w2s, p2) in enumerate(members):
                            wb2 = wbs[m]
                            for kc in range(4):
                                tokm = E(P, "tensor", c.s_pe,
                                         lambda e, o=psx.ap()[:, :],
                                         l=wb2.ap[:, kc * 256 + mt * 128:
                                                  kc * 256 + mt * 128 + 128],
                                         r=gbuf.ap[:, kc * 1024 + nb * 512:
                                                   kc * 1024 + (nb + 1) * 512],
                                         s=(nmm == 0), p=(nmm == 7):
                                         e.matmul(o, l, r, start=s, stop=p),
                                         waits=wb2.raw() + gbuf.raw()
                                         + (pbx.war() if nmm == 0 else ()))
                                wb2.read_by(tokm)
                                gbuf.read_by(tokm)
                                nmm += 1
                        pbx.wrote(tokm)
                        xap = c.XRES.ap()[:, mc * T_SEQ + t0 + nb * 512:
                                          mc * T_SEQ + t0 + (nb + 1) * 512]
                        tev = E(P, "vector", c.s_dve,
                                lambda e, o=xap, a=psx.ap()[:, :], b=xap:
                                e.scalar_tensor_tensor(o, a, 0.0, b,
                                                       OP.add, OP.add),
                                waits=pbx.raw() + c.xres[mc].war())
                        c.xres[mc].add_write(tev)
                        pbx.read_by(tev)
    c.AF.reset(f0)
    c.AB.reset(b0)
    set_fence(c)
